# version 02
# baseline (speedup 1.0000x reference)
import contextlib
from collections import defaultdict

import numpy as np
import concourse.bass as bass
import concourse.mybir as mybir
from concourse.bass_utils import run_bass_kernel_spmd

F32 = mybir.dt.float32
BF16 = mybir.dt.bfloat16
AF = mybir.ActivationFunctionType
ALU = mybir.AluOpType

PH = 64
NST = 128
KCV = 4
Q = 128
EPS = 1e-5
LNDT_CLAMP = -30.0
ENGS = ("t", "v", "s", "p", "q")
_LAST_SCH = None
DBG_Q = 0
HB = ""
ZINC = False
DPSPLIT = True


def K_HBX():
    return HB
DBG_PH = 'ABCDE0'


def full_cfg():
    return dict(D=2048, T=2048, H=64, FI=5632)


def derive(cfg):
    c = dict(cfg)
    D, T, H, FI = c["D"], c["T"], c["H"], c["FI"]
    c["G2"] = H * PH
    assert c["G2"] == 2 * D
    c["CONVD"] = c["G2"] + 2 * NST
    c["PROJ"] = c["G2"] + c["CONVD"] + H
    c["PROJP"] = c["G2"] + c["CONVD"] + 128
    c["KT"] = D // 128
    c["CT"] = c["G2"] // 128
    c["OCH"] = c["CONVD"] // 128
    c["NBLK"] = c["PROJP"] // 128
    c["ICH"] = FI // 128
    c["TQ"] = T // 4
    c["NTT"] = c["TQ"] // 128
    c["NCHQ"] = c["TQ"] // Q
    c["AGH"] = min(16, H)
    c["YGH"] = min(8, H)
    c["OKT"] = c["G2"] // 128
    fg = min(22, c["ICH"])
    while c["ICH"] % fg:
        fg -= 1
    c["FGRP"] = fg
    assert c["TQ"] <= 512 and c["CT"] % 4 == 0 and c["D"] % 128 == 0
    return c


def host_arrays(c, w):
    D, H, FI = c["D"], c["H"], c["FI"]
    CONVD, PROJ, PROJP = c["CONVD"], c["PROJ"], c["PROJP"]
    KT, CT, OCH, NBLK, ICH, OKT = (c["KT"], c["CT"], c["OCH"], c["NBLK"],
                                   c["ICH"], c["OKT"])
    o = {}
    o["c_tri"] = (np.arange(128)[:, None] <= np.arange(128)[None, :]).astype(np.float32)
    o["c_I"] = np.eye(128, dtype=np.float32)
    o["c_up"] = np.where(np.arange(128)[:, None] > np.arange(128)[None, :],
                         np.float32(-1.0e6), np.float32(0.0))
    o["c_ones2"] = np.ones((2, 128), np.float32)
    wp = np.zeros((PROJP, D), np.float32)
    wp[:PROJ] = w["in_proj_w"]
    G2 = 2 * D
    gate_part = np.ascontiguousarray(
        wp[:G2].reshape(G2 // 512, 512, KT, 128).transpose(0, 3, 2, 1))
    rest_part = np.ascontiguousarray(
        wp[G2:].reshape(NBLK - G2 // 128, 128, KT, 128).transpose(0, 3, 2, 1))
    o["w_in"] = np.concatenate([gate_part.reshape(-1), rest_part.reshape(-1)])
    for nm, src in (("w_g", w["gate_w"]), ("w_u", w["up_w"])):
        o[nm] = np.ascontiguousarray(
            src.reshape(ICH, 128, KT, 128).transpose(0, 3, 2, 1)).reshape(-1)
    o["w_d"] = np.ascontiguousarray(w["down_w"])
    o["w_out"] = np.ascontiguousarray(w["out_proj_w"].T)
    cw = w["conv_w"].reshape(CONVD, KCV)
    cd = np.zeros((128, OCH * KCV * 128), np.float32)
    for m in range(OCH):
        for k in range(KCV):
            blk = (m * KCV + k) * 128
            cd[np.arange(128), blk + np.arange(128)] = cw[m * 128:(m + 1) * 128, k]
    o["c_convd"] = cd
    dd = np.zeros((128, CT * 128), np.float32)
    for m in range(CT):
        ch = (m * 128 + np.arange(128)) // PH
        dd[np.arange(128), m * 128 + np.arange(128)] = w["D"][ch]
    o["c_Dd"] = dd
    o["c_convb"] = np.ascontiguousarray(w["conv_b"].reshape(OCH, 128).T)
    o["c_nw"] = np.ascontiguousarray(w["norm_weight"].reshape(KT, 128).T)
    o["c_dtb"] = w["dt_bias"].reshape(H, 1).astype(np.float32)
    o["c_alog"] = w["A_log"].reshape(H, 1).astype(np.float32)
    o["c_gs"] = np.ascontiguousarray(w["gate_s"].reshape(ICH, 128).T)
    o["c_us"] = np.ascontiguousarray(w["up_s"].reshape(ICH, 128).T)
    o["c_ds"] = np.ascontiguousarray(w["down_s"].reshape(KT, 128).T)
    o["c_gnw"] = np.ascontiguousarray(w["gate_norm_weight"].reshape(OKT, 128).T)
    return o


class Sch:
    def __init__(self):
        self.ops = {e: [] for e in ENGS}
        self.cnt = defaultdict(int)
        self.bcnt = {e: 0 for e in ENGS}

    def op(self, e, fn, inc=None, n=1):
        if inc is None and ZINC and e in ("t", "v", "s", "p"):
            inc, n = self.OWN[e], 1
        incs = []
        if inc is not None:
            self.cnt[inc] += n
            incs.append((inc, n))
        self.ops[e].append(("op", fn, incs))
        return (inc, self.cnt[inc]) if inc is not None else None

    def dma(self, e, fn, sem=None):
        if sem is None:
            sem = "d" if (e == "q" or not DPSPLIT) else "dp"
        return self.op(e, fn, inc=sem, n=16)

    def wait(self, e, sem, val=None):
        v = self.cnt[sem] if val is None else val
        self.ops[e].append(("wait", sem, v))

    def selfedge(self, e):
        for item in reversed(self.ops[e]):
            if item[0] == "op":
                if not item[2]:
                    own = self.OWN[e]
                    self.cnt[own] += 1
                    item[2].append((own, 1))
                nm, n = item[2][0]
                self.ops[e].append(("wait", nm, self.cnt[nm]))
                return
            if item[0] == "wait":
                continue
            return

    def need(self, e, tok):
        if tok is not None:
            self.ops[e].append(("wait", tok[0], tok[1]))

    OWN = {"t": "bt", "v": "bv", "s": "bs", "p": "bp", "q": "d"}
    P_DMA = ("dp", "dp0", "dp1")

    def q_dma_sems(self):
        return [k for k in self.cnt
                if k.startswith("d") and k not in self.P_DMA and self.cnt[k]]

    def barrier(self, engines=ENGS):
        for e in engines:
            self.bcnt[e] += 1
            if e == "q":
                owns = self.q_dma_sems()
            elif e == "p":
                owns = ["bp", "dp", "dp0", "dp1"] if DPSPLIT else ["bp", "d"]
            else:
                owns = [self.OWN[e]]
            for own in owns:
                if self.cnt[own]:
                    self.ops[e].append(("wait", own, self.cnt[own]))
            self.ops[e].append(("barinc", e))
        for e in engines:
            for oth in engines:
                if oth != e:
                    self.ops[e].append(("wait", "B" + oth, self.bcnt[oth]))

    def simulate(self):
        pos = {e: 0 for e in ENGS}
        sem = defaultdict(int)
        progress = True
        while progress:
            progress = False
            for e in ENGS:
                while pos[e] < len(self.ops[e]):
                    item = self.ops[e][pos[e]]
                    if item[0] == "op":
                        for nm_, n_ in item[2]:
                            sem[nm_] += n_
                        pos[e] += 1
                        progress = True
                    elif item[0] == "wait":
                        if sem[item[1]] >= item[2]:
                            pos[e] += 1
                            progress = True
                        else:
                            break
                    else:
                        sem["B" + item[1]] += 1
                        pos[e] += 1
                        progress = True
        stuck = {e: pos[e] for e in ENGS if pos[e] < len(self.ops[e])}
        if stuck:
            msg = []
            for e, p in stuck.items():
                it = self.ops[e][p]
                msg.append(f"{e}@{p}/{len(self.ops[e])}: wait {it[1]} >= {it[2]} "
                           f"(sim has {dict(sem).get(it[1], 0)})")
            raise RuntimeError("DEADLOCK in schedule:\n" + "\n".join(msg))

    def emit(self, nc, sems):
        finals = dict(self.cnt)
        for e in ENGS:
            finals["B" + e] = self.bcnt[e]
        for e in ENGS:
            for item in self.ops[e]:
                if item[0] == "wait":
                    assert item[2] <= finals.get(item[1], 0), (
                        f"unreachable wait {item[1]} >= {item[2]} (final "
                        f"{finals.get(item[1], 0)}) on engine {e}")

        def run(eng, stream):
            for item in self.ops[stream]:
                if item[0] == "op":
                    _, fn, incs = item
                    inst = fn(eng)
                    for nm, n in incs:
                        inst = inst.then_inc(sems[nm], n)
                elif item[0] == "wait":
                    eng.wait_ge(sems[item[1]], item[2])
                else:
                    eng.nop().then_inc(sems["B" + item[1]], 1)

        with nc.Block() as block:
            @block.tensor
            def _(e):
                run(e, "t")

            @block.vector
            def _(e):
                run(e, "v")

            @block.scalar
            def _(e):
                run(e, "s")

            @block.gpsimd
            def _(e):
                run(e, "p")

            @block.sync
            def _(e):
                run(e, "q")


def build(c, limit=99, dbg=False, dump=False):
    nc = bass.Bass("TRN2", target_bir_lowering=False, debug=False)
    D, T, H, FI = c["D"], c["T"], c["H"], c["FI"]
    G2, CONVD = c["G2"], c["CONVD"]
    KT, CT, OCH, NBLK, ICH = c["KT"], c["CT"], c["OCH"], c["NBLK"], c["ICH"]
    TQ, NTT, NCHQ = c["TQ"], c["NTT"], c["NCHQ"]
    AGH, YGH, OKT, FGRP = c["AGH"], c["YGH"], c["OKT"], c["FGRP"]
    NYG, NAG, NFG = H // YGH, H // AGH, ICH // FGRP

    ap = {}
    for nm, shp in [
        ("x", (T, D)), ("w_in", (NBLK * KT * 128 * 128,)),
        ("w_g", (D * FI,)), ("w_u", (D * FI,)),
        ("w_d", (D, FI)), ("w_out", (G2, D)),
        ("c_convd", (128, OCH * KCV * 128)), ("c_Dd", (128, CT * 128)),
        ("c_convb", (128, OCH)), ("c_nw", (128, KT)), ("c_dtb", (H, 1)),
        ("c_alog", (H, 1)), ("c_gs", (128, ICH)), ("c_us", (128, ICH)),
        ("c_ds", (128, KT)), ("c_gnw", (128, OKT)), ("c_tri", (128, 128)),
        ("c_I", (128, 128)), ("c_up", (128, 128)), ("c_ones2", (2, 128)),
    ]:
        ap[nm] = nc.dram_tensor(nm, list(shp), F32, kind="ExternalInput").ap()
    out_t = nc.dram_tensor("out", [T, D], F32, kind="ExternalOutput").ap()
    dbg_t = {}
    if dbg:
        for nm, shp, dt_ in [
            ("dbg_xn", (128, D), BF16), ("dbg_xnm", (128, D), BF16),
            ("dbg_xpxn", (128, D), F32), ("dbg_hdn", (128, FGRP * TQ), BF16),
            ("dbg_gate", (128, G2), BF16), ("dbg_dtraw", (H, TQ), F32),
            ("dbg_dt", (H, TQ), F32), ("dbg_cum", (H, TQ), F32),
            ("dbg_xbc0", (128, TQ), BF16), ("dbg_Bc", (128, TQ), BF16),
            ("dbg_tok", (128, 3 * H), BF16), ("dbg_Gm", (128, 128), BF16),
            ("dbg_gml", (128, H * 128), BF16), ("dbg_Etok", (128, H), F32),
            ("dbg_Lam", (128, H), F32), ("dbg_y", (128, 2 * D), F32),
            ("dbg_ygn", (128, G2), BF16), ("dbg_S", (128, G2), F32),
            ("dbg_xsT", (128, G2), BF16), ("dbg_WX", (128, G2), BF16),
            ("dbg_stat", (128, 16), F32), ("dbg_x", (128, D), F32),
            ("dbg_xnT", (128, KT * TQ), BF16), ("dbg_xnmT", (128, KT * TQ), BF16),
            ("dbg_wd", (128, D), BF16), ("dbg_win", (128, KT * 128), BF16),
            ("dbg_fx", (128, D), F32),
        ]:
            dbg_t[nm] = nc.dram_tensor(nm, list(shp), dt_, kind="ExternalOutput").ap()

    def dsc(nm, shp, dt=BF16):
        return nc.dram_tensor(nm, list(shp), dt, kind="Internal").ap()

    sc_in = dsc("sc_in", (NBLK * KT * 128 * 128,))
    sc_g = dsc("sc_g", (D * FI,))
    sc_u = dsc("sc_u", (D * FI,))
    sc_d = dsc("sc_d", (FI, D))
    sc_o = dsc("sc_o", (G2, D))
    sc_cv = dsc("sc_cv", (128, OCH * KCV * 128))
    sc_gate = dsc("sc_gate", (TQ, G2))
    sc_xpxn = dsc("sc_xpxn", (TQ, D), F32)
    sc_ygnT = dsc("sc_ygnT", (OKT, 128, TQ))
    sc_cumhl = dsc("sc_cumhl", (2, NCHQ * H * Q))
    sc_cend = dsc("sc_cend", (2, NCHQ * H))

    o_xnT, o_xnmT = 0, KT * TQ
    o_xsT, o_wx, o_gml = 0, G2, 2 * G2
    o_expl = o_gml + H * 128
    o_yg = o_expl + AGH * 128
    AR_COLS = max(2 * KT * TQ, o_yg + G2)

    es = contextlib.ExitStack()
    sb = {}

    def sbuf(name, shape, dt=BF16):
        sb[name] = es.enter_context(nc.sbuf_tensor("sb_" + name, list(shape), dt))

    with es:
        for nm in ["c_tri", "c_I", "c_up"]:
            sbuf(nm, (128, 128))
        sbuf("c_If", (128, 128), F32)
        sbuf("c_ones2", (2, 128))
        sbuf("c_Dd", (128, CT * 128))
        sbuf("c_convb", (128, OCH), F32)
        sbuf("c_nw", (128, KT), F32)
        sbuf("c_dtb", (H, 1), F32)
        sbuf("c_negA", (H, 1), F32)
        sbuf("c_gs", (128, ICH), F32)
        sbuf("c_us", (128, ICH), F32)
        sbuf("c_ds", (128, KT), F32)
        sbuf("c_gnw", (128, OKT), F32)
        sbuf("S", (128, G2), F32)
        sbuf("Sb", (128, G2))
        sbuf("AR", (128, AR_COLS))
        sbuf("XP", (128, NTT * D), F32)
        sbuf("XBCB", (128, 2 * TQ + OCH * (KCV - 1)))
        sbuf("XY", (128, 2 * D), F32)
        sbuf("HDN", (128, FGRP * TQ))
        sbuf("GSL", (128, max(G2, NTT * 512)))
        sbuf("WA", (128, 2 * 2048))
        sbuf("WB", (128, 2 * 2048))
        sbuf("WY", (128, 2 * TQ))
        for nm in ["b_dtraw", "b_dt", "b_cum", "b_t1"]:
            sbuf(nm, (H, TQ), F32)
        for nm in ["b_dtA", "b_dtAn", "b_lndt", "b_wdt", "b_chi", "b_clo"]:
            sbuf(nm, (H, TQ))
        sbuf("b_cendh", (H, NCHQ))
        sbuf("b_cendl", (H, NCHQ))
        sbuf("b_z", (H, Q), F32)
        sbuf("b_cumhl", (2, 2 * AGH * Q))
        sbuf("b_cendhl", (2, NCHQ * H))
        sbuf("b_tok", (128, 3 * H))
        sbuf("b_Etok", (128, H), F32)
        sbuf("b_Lam", (128, H), F32)
        sbuf("b_Gm", (128, 128))
        sbuf("b_BT", (128, NST))
        sbuf("b_Sd", (128, YGH * PH), F32)
        sbuf("b_stat", (128, 16), F32)
        sbuf("b_cscr", (H, 2 * NCHQ), F32)
        sbuf("c_bias", (128, 2), F32)
        nc.const_aps.register(F32, float(EPS), sb["c_bias"][:, 0:1])
        nc.const_aps.register(F32, 1.0, sb["c_bias"][:, 1:2])

        PA = es.enter_context(nc.psum_tensor("PA", [128, 2048], F32))
        PB = es.enter_context(nc.psum_tensor("PB", [128, 1024], F32))
        PC = es.enter_context(nc.psum_tensor("PC", [128, 512], F32))
        PD = es.enter_context(nc.psum_tensor("PD", [128, 512], F32))

        sems = {}
        for nm in (["d", "dc", "dp", "dp0", "dp1", "dw0", "dw1", "da0", "da1",
                    "db0", "db1", "dy0", "dy1", "dx0", "dx1", "dgt", "do"]
                   + ["dch0", "dch1"]
                   + ["dg%d" % k for k in range(8)]
                   + ["b" + e for e in ENGS] + ["B" + e for e in ENGS]):
            sems[nm] = es.enter_context(nc.semaphore(nm))

        s = Sch()

        PHOF = {"dbg_stat": "A", "dbg_x": "A", "dbg_xn": "A", "dbg_xnm": "A",
                "dbg_xnT": "B", "dbg_xnmT": "B", "dbg_hdn": "B", "dbg_xpxn": "B",
                "dbg_wd": "B", "dbg_win": "C", "dbg_dtraw": "C", "dbg_gate": "E",
                "dbg_xbc0": "D", "dbg_Bc": "D", "dbg_dt": "0", "dbg_cum": "0",
                "dbg_tok": "E", "dbg_Etok": "E", "dbg_Lam": "E", "dbg_xsT": "E",
                "dbg_WX": "E", "dbg_Gm": "E", "dbg_gml": "E", "dbg_y": "E",
                "dbg_S": "E", "dbg_ygn": "E", "dbg_fx": "F"}

        def tap(nm, src_ap):
            if dbg and nm in dbg_t and PHOF.get(nm, "?") in DBG_PH:
                s.wait("q", "bv")
                s.wait("q", "bs")
                s.wait("q", "bt")
                s.dma("q", lambda e, nm=nm, src_ap=src_ap: e.dma_start(
                    out=dbg_t[nm], in_=src_ap))

        AR, XP, XY, XBCB = sb["AR"], sb["XP"], sb["XY"], sb["XBCB"]
        HDN, GSL, WA, WB, WY = sb["HDN"], sb["GSL"], sb["WA"], sb["WB"], sb["WY"]
        ST, SBb = sb["S"], sb["Sb"]
        HIST0 = 2 * TQ
        IDb = sb["c_I"][:, :]

        def xbca(lo, hi):
            return XP[:, :].bitcast(BF16)[:, lo:hi]

        def pbh(i):
            return PB[:, i * 512:(i + 1) * 512].bitcast(BF16)

        for nm in ["c_tri", "c_I", "c_up", "c_ones2", "c_Dd"]:
            s.dma("p", lambda e, d=sb[nm], sr=ap[nm]: e.dma_start(out=d[:, :], in_=sr))
        s.dma("q", lambda e: e.dma_start(out=sb["c_If"][:, :], in_=ap["c_I"]),
              sem="dc")
        for nm in ["c_convb", "c_nw", "c_dtb", "c_gs", "c_us", "c_ds", "c_gnw"]:
            s.dma("q", lambda e, d=sb[nm], sr=ap[nm]: e.dma_start(out=d[:, :], in_=sr),
                  sem="dc")
        s.dma("q", lambda e: e.dma_start(out=sb["c_negA"][:, :], in_=ap["c_alog"]),
              sem="dc")
        s.wait("s", "dc")
        s.wait("s", "dp")
        s.op("s", lambda e: e.activation(out=sb["c_negA"][:, :], in_=sb["c_negA"][:, :],
                                         func=AF.Exp), inc="bs")
        s.wait("v", "bs")
        s.op("v", lambda e: e.tensor_scalar_mul(sb["c_negA"][:, :], sb["c_negA"][:, :], -1.0))
        s.op("v", lambda e: e.memset(sb["b_z"][:, :], 0.0))
        s.op("v", lambda e: e.memset(sb["c_bias"][:, 0:1], float(EPS)))
        s.op("v", lambda e: e.memset(sb["c_bias"][:, 1:2], 1.0))
        s.op("v", lambda e: e.memset(ST[:, :], 0.0))
        s.op("v", lambda e: e.memset(SBb[:, :], 0.0))
        s.op("v", lambda e: e.memset(XBCB[:, HIST0:HIST0 + OCH * (KCV - 1)], 0.0),
             inc="bv")

        CH = 2048
        wb_done = {}

        def plain_cast(src_ap, dst_ap, total_elems, src2d=False, dst2d=False):
            cols = total_elems // 128
            assert cols * 128 == total_elems
            c0_ = 0
            i = 0
            while c0_ < cols:
                w_ = min(CH, cols - c0_)
                slot = (i % 2) * 2048
                s.need("p", wb_done.get(slot))
                if src2d:
                    sview = src_ap[:, c0_:c0_ + w_]
                else:
                    sview = src_ap.rearrange("(p m) -> p m", p=128)[:, c0_:c0_ + w_]
                tok = s.dma("p", lambda e, sview=sview, slot=slot, w_=w_: e.dma_start(
                    out=WA[:, slot:slot + w_], in_=sview),
                    sem="dp0" if slot == 0 else "dp1")
                s.need("q", tok)
                if dst2d:
                    dview = dst_ap[:, c0_:c0_ + w_]
                else:
                    dview = dst_ap.rearrange("(p m) -> p m", p=128)[:, c0_:c0_ + w_]
                wb_done[slot] = s.dma("q", lambda e, dview=dview, slot=slot, w_=w_:
                                      e.dma_start(out=dview, in_=WA[:, slot:slot + w_]),
                                      sem="dw0" if slot == 0 else "dw1")
                c0_ += w_
                i += 1

        if limit >= 1:
            plain_cast(ap["w_in"], sc_in, NBLK * KT * 128 * 128)
        if limit >= 1:
            plain_cast(ap["w_g"], sc_g, D * FI)
            plain_cast(ap["w_u"], sc_u, D * FI)
            plain_cast(ap["c_convd"], sc_cv, 128 * OCH * KCV * 128, src2d=True, dst2d=True)
        for it in range(OKT):
            slot = (it % 2) * 2048
            s.need("p", wb_done.get(slot))
            tok = s.dma("p", lambda e, it=it, slot=slot: e.dma_start(
                out=WA[:, slot:slot + D], in_=ap["w_out"][it * 128:(it + 1) * 128, :]),
                sem="dp0" if slot == 0 else "dp1")
            s.need("v", tok)
            vt = s.op("v", lambda e, it=it, slot=slot: e.tensor_scalar_mul(
                WA[:, slot:slot + D], WA[:, slot:slot + D],
                sb["c_gnw"][:, it:it + 1]), inc="bv")
            s.need("q", vt)
            wb_done[slot] = s.dma("q", lambda e, it=it, slot=slot: e.dma_start(
                out=sc_o[it * 128:(it + 1) * 128, :], in_=WA[:, slot:slot + D]),
                sem="dw0" if slot == 0 else "dw1")
        wb_done[("d", 0)] = wb_done.get(0)
        wb_done[("d", 2048)] = wb_done.get(2048)
        gsl_done = {}
        half_free = {0: None, 1: None}
        for it in range(KT if limit >= 3 else 0):
            nfc = (FI + 2047) // 2048
            for fc in range(nfc):
                w0 = fc * 2048
                wlen = min(2048, FI - w0)
                slot = (fc % 2) * 2048
                s.need("p", wb_done.get(("d", slot)))
                tok = s.dma("p", lambda e, it=it, w0=w0, wlen=wlen, slot=slot: e.dma_start(
                    out=WA[:, slot:slot + wlen],
                    in_=ap["w_d"][it * 128:(it + 1) * 128, w0:w0 + wlen]),
                    sem="dp0" if slot == 0 else "dp1")
                s.need("v", tok)
                vt = s.op("v", lambda e, wlen=wlen, slot=slot, it=it: e.tensor_scalar_mul(
                    WA[:, slot:slot + wlen], WA[:, slot:slot + wlen],
                    sb["c_ds"][:, it:it + 1]), inc="bv")
                wb_done[("d", slot)] = vt
                s.need("t", vt)
                nblk = wlen // 128
                for bi in range(nblk):
                    half = bi % 2
                    gslot = (bi % 8) * 128
                    s.need("t", half_free[half])
                    tt_ = s.op("t", lambda e, slot=slot, bi=bi, half=half: e.transpose(
                        pbh(half)[:, (bi % 2) * 0:128] if False else pbh(half)[:, 0:128],
                        WA[:, slot + bi * 128:slot + (bi + 1) * 128],
                        identity=IDb), inc="bt")
                    s.need("v", tt_)
                    s.need("v", gsl_done.get(gslot))
                    vt2 = s.op("v", lambda e, half=half, gslot=gslot: e.tensor_copy(
                        GSL[:, gslot:gslot + 128], pbh(half)[:, 0:128]), inc="bv")
                    half_free[half] = vt2
                    s.need("q", vt2)
                    gsl_done[gslot] = s.dma("q", lambda e, it=it, w0=w0, bi=bi,
                                            gslot=gslot: e.dma_start(
                        out=sc_d[w0 + bi * 128:w0 + (bi + 1) * 128,
                                 it * 128:(it + 1) * 128],
                        in_=GSL[:, gslot:gslot + 128]), sem="dg%d" % (gslot // 128))
                wb_done[("d", slot)] = ("bt", s.cnt["bt"])
        s.barrier()

        for q_i in range(4 if limit >= 4 else 0):
            T0 = q_i * TQ

            st = sb["b_stat"]
            do_b = limit >= 5
            do_c = limit >= 6 and limit != 55
            do_d = limit >= 7 and limit != 55
            do_d0 = limit >= 8 and limit != 55
            do_e = limit >= 9 and limit != 55
            do_f = limit >= 10 or limit == 55
            tp_half_free = {0: None, 1: None}
            prev_tt_done = {}
            for tt in range(NTT):
                xoff = (tt % 2) * D
                s.need("q", prev_tt_done.get(tt % 2))
                xtok = s.dma("q", lambda e, tt=tt, xoff=xoff, T0=T0: e.dma_start(
                    out=XY[:, xoff:xoff + D],
                    in_=ap["x"][T0 + tt * 128:T0 + (tt + 1) * 128, :]),
                    sem="dx%d" % (tt % 2))
                s.need("v", xtok)
                xsl = XY[:, xoff:xoff + D]
                s.op("v", lambda e, xsl=xsl: e.reduce_sum(
                    st[:, 0:1], xsl, axis=mybir.AxisListType.X), inc="bv")
                s.selfedge("v")
                s.op("v", lambda e: e.tensor_scalar_mul(st[:, 2:3], st[:, 0:1], 1.0 / D))
                s.op("v", lambda e: e.tensor_scalar_mul(st[:, 3:4], st[:, 0:1],
                                                        -1.0 / D), inc="bv")
                s.wait("s", "bv")
                s.wait("s", "bt")
                s.op("s", lambda e, xsl=xsl: e.activation(
                    out=GSL[:, 0:D], in_=xsl, func=AF.Square,
                    bias=st[:, 3:4], accum_out=st[:, 4:5]), inc="bs")
                s.wait("v", "bs")
                s.op("v", lambda e: e.tensor_scalar_mul(st[:, 5:6], st[:, 4:5], 1.0 / D),
                     inc="bv")
                s.wait("s", "bv")
                s.op("s", lambda e: e.activation(out=st[:, 6:7], in_=st[:, 5:6],
                                                 func=AF.Ln, bias=float(EPS)))
                s.op("s", lambda e: e.activation(out=st[:, 7:8], in_=st[:, 6:7],
                                                 func=AF.Exp, scale=-0.5), inc="bs")
                s.wait("v", "bs")
                s.op("v", lambda e: e.tensor_scalar(
                    out=st[:, 8:9], in0=st[:, 5:6], scalar1=st[:, 7:8],
                    scalar2=st[:, 7:8], op0=ALU.mult, op1=ALU.mult), inc="bv")
                s.wait("s", "bv")
                s.op("s", lambda e: e.activation(out=st[:, 9:10], in_=st[:, 8:9],
                                                 func=AF.Ln, bias=float(EPS)))
                s.op("s", lambda e: e.activation(out=st[:, 10:11], in_=st[:, 9:10],
                                                 func=AF.Exp, scale=-0.5), inc="bs")
                s.wait("v", "bs")
                s.op("v", lambda e: e.tensor_scalar(
                    out=st[:, 11:12], in0=st[:, 7:8], scalar1=st[:, 10:11],
                    scalar2=None, op0=ALU.mult))
                s.op("v", lambda e: e.tensor_scalar_add(st[:, 12:13], st[:, 7:8], 1.0))
                s.op("v", lambda e: e.tensor_scalar(
                    out=st[:, 13:14], in0=st[:, 2:3], scalar1=st[:, 7:8],
                    scalar2=None, op0=ALU.mult))
                s.selfedge("v")
                s.wait("v", "bt")
                s.op("v", lambda e, xsl=xsl: e.tensor_scalar(
                    out=GSL[:, 0:D], in0=xsl, scalar1=st[:, 2:3], scalar2=st[:, 7:8],
                    op0=ALU.subtract, op1=ALU.mult))
                s.op("v", lambda e, xsl=xsl: e.tensor_scalar(
                    out=GSL[:, D:2 * D], in0=xsl, scalar1=st[:, 2:3],
                    scalar2=st[:, 11:12], op0=ALU.subtract, op1=ALU.mult))
                vlast = s.op("v", lambda e, xsl=xsl, tt=tt: e.tensor_scalar(
                    out=XP[:, tt * D:(tt + 1) * D], in0=xsl, scalar1=st[:, 12:13],
                    scalar2=st[:, 13:14], op0=ALU.mult, op1=ALU.subtract), inc="bv")
                prev_tt_done[tt % 2] = vlast
                if q_i == DBG_Q and tt == 0:
                    tap("dbg_stat", sb["b_stat"][:, :])
                    tap("dbg_x", XY[:, 0:D])
                    tap("dbg_xn", GSL[:, 0:D])
                    tap("dbg_xnm", GSL[:, D:2 * D])
                    tap("dbg_xpxn", XP[:, 0:D]) if False else None
                if limit == 43:
                    continue
                s.need("t", vlast)
                for blk in range(2 * KT):
                    ht, which = blk // 2, blk % 2
                    half = blk % 2
                    s.need("t", tp_half_free[half])
                    src = GSL[:, which * D + ht * 128: which * D + (ht + 1) * 128]
                    ttok = s.op("t", lambda e, src=src, half=half: e.transpose(
                        pbh(half)[:, 0:128], src, identity=IDb), inc="bt")
                    dst_off = (o_xnT if which == 0 else o_xnmT) + ht * TQ + tt * 128
                    if which == 0:
                        s.need("s", ttok)
                        ev = s.op("s", lambda e, half=half, dst_off=dst_off: e.copy(
                            AR[:, dst_off:dst_off + 128], pbh(half)[:, 0:128]),
                            inc="bs")
                    else:
                        s.need("v", ttok)
                        ev = s.op("v", lambda e, half=half, dst_off=dst_off, ht=ht:
                                  e.tensor_scalar_mul(
                                      AR[:, dst_off:dst_off + 128],
                                      pbh(half)[:, 0:128],
                                      sb["c_nw"][:, ht:ht + 1]), inc="bv")
                    tp_half_free[half] = ev
                if "a" in HB:
                    s.barrier()
            s.barrier()

            wa_free = {}
            if not do_b:
                continue
            b_sub = limit if limit in (51, 52, 53) else (limit - 470 if limit >= 520 else 99)
            wbslot_free = {}
            pbg_free = None
            pbu_free = None
            for fg in range(NFG):
                for icl in range(FGRP):
                    ic = fg * FGRP + icl
                    sA = (ic % 2) * 2048
                    s.need("q", wa_free.get(sA))
                    gt = s.dma("q", lambda e, ic=ic, sA=sA: e.dma_start(
                        out=WA[:, sA:sA + KT * 128],
                        in_=sc_g.rearrange("(n p m) -> n p m", p=128, m=KT * 128)[ic]),
                        sem="da0" if sA == 0 else "da1")
                    s.need("q", wbslot_free.get(sA))
                    ut = s.dma("q", lambda e, ic=ic, sA=sA: e.dma_start(
                        out=WB[:, sA:sA + KT * 128],
                        in_=sc_u.rearrange("(n p m) -> n p m", p=128, m=KT * 128)[ic]),
                        sem="db0" if sA == 0 else "db1")
                    s.need("t", gt)
                    s.need("t", ut)
                    if "y" in K_HBX():
                        s.wait("t", "d")
                    s.need("t", pbg_free)
                    s.need("t", pbu_free)
                    for kt in range(KT):
                        lastg = s.op("t", lambda e, kt=kt, sA=sA: e.matmul(
                            PB[:, 0:TQ],
                            WA[:, sA + kt * 128:sA + (kt + 1) * 128],
                            AR[:, o_xnT + kt * TQ:o_xnT + (kt + 1) * TQ],
                            start=(kt == 0), stop=(kt == KT - 1),
                            skip_group_check=True),
                            inc=("bt" if kt == KT - 1 else None))
                    for kt in range(KT):
                        tok = s.op("t", lambda e, kt=kt, sA=sA: e.matmul(
                            PB[:, 512:512 + TQ],
                            WB[:, sA + kt * 128:sA + (kt + 1) * 128],
                            AR[:, o_xnT + kt * TQ:o_xnT + (kt + 1) * TQ],
                            start=(kt == 0), stop=(kt == KT - 1),
                            skip_group_check=True),
                            inc=("bt" if kt == KT - 1 else None))
                    wa_free[sA] = tok
                    wbslot_free[sA] = tok
                    if b_sub == 51:
                        pbg_free = tok
                        pbu_free = tok
                        continue
                    s.need("s", lastg)
                    if "x" in K_HBX():
                        s.need("s", tok)
                    s.need("s", pbu_free)
                    if b_sub == 61:
                        sg = s.op("s", lambda e: e.copy(
                            GSL[:, 0:TQ], PB[:, 0:TQ]), inc="bs")
                    elif b_sub == 62:
                        sg = s.op("s", lambda e: e.activation(
                            out=GSL[:, 0:TQ], in_=PB[:, 0:TQ], func=AF.Silu), inc="bs")
                    else:
                        sg = s.op("s", lambda e, ic=ic: e.activation(
                            out=GSL[:, 0:TQ], in_=PB[:, 0:TQ], func=AF.Silu,
                            scale=sb["c_gs"][:, ic:ic + 1]), inc="bs")
                    pbg_free = sg
                    s.need("v", sg)
                    s.need("v", tok)
                    if b_sub == 63:
                        hv = s.op("v", lambda e, icl=icl: e.tensor_mul(
                            HDN[:, icl * TQ:(icl + 1) * TQ], PB[:, 512:512 + TQ],
                            GSL[:, 0:TQ]), inc="bv")
                    elif b_sub == 64:
                        hv = s.op("v", lambda e, icl=icl: e.tensor_copy(
                            HDN[:, icl * TQ:(icl + 1) * TQ], PB[:, 512:512 + TQ]),
                            inc="bv")
                    else:
                        hv = s.op("v", lambda e, ic=ic, icl=icl: e.scalar_tensor_tensor(
                            out=HDN[:, icl * TQ:(icl + 1) * TQ], in0=PB[:, 512:512 + TQ],
                            scalar=sb["c_us"][:, ic:ic + 1], in1=GSL[:, 0:TQ],
                            op0=ALU.mult, op1=ALU.mult), inc="bv")
                    pbu_free = hv
                    if "b" in HB:
                        s.barrier()
                if b_sub in (51, 52, 61, 62, 63, 64):
                    continue
                hdn_ready = hv
                wbslot_free[("dn", 0)] = wbslot_free.get(0)
                wbslot_free[("dn", 2048)] = wbslot_free.get(2048)
                pa_free = None
                for tt in range(NTT):
                    for icl in range(FGRP):
                        ic = fg * FGRP + icl
                        sB = (icl % 2) * 2048
                        s.need("q", wbslot_free.get(("dn", sB)))
                        dt_ = s.dma("q", lambda e, ic=ic, sB=sB: e.dma_start(
                            out=WB[:, sB:sB + D],
                            in_=sc_d[ic * 128:(ic + 1) * 128, :]),
                            sem="db0" if sB == 0 else "db1")
                        s.need("t", dt_)
                        if q_i == DBG_Q and fg == 0 and tt == 0 and icl == 0:
                            s.need("q", dt_)
                            tap("dbg_wd", WB[:, sB:sB + D])
                        if icl == 0:
                            s.need("t", hdn_ready)
                            s.need("t", pa_free)
                        ocl = [(i * 512, min(512, D - i * 512))
                               for i in range((D + 511) // 512)]
                        for oi, (o0, olen) in enumerate(ocl):
                            tok = s.op("t", lambda e, icl=icl, tt=tt, sB=sB, o0=o0,
                                       olen=olen: e.matmul(
                                           PA[:, o0:o0 + olen],
                                           HDN[:, icl * TQ + tt * 128:
                                               icl * TQ + (tt + 1) * 128],
                                           WB[:, sB + o0:sB + o0 + olen],
                                           start=(icl == 0), stop=(icl == FGRP - 1),
                                           skip_group_check=True),
                                       inc=("bt" if oi == len(ocl) - 1 else None))
                        wbslot_free[("dn", sB)] = tok
                    s.need("v", tok)
                    pa_free = s.op("v", lambda e, tt=tt: e.scalar_tensor_tensor(
                        out=XP[:, tt * D:(tt + 1) * D], in0=PA[:, 0:D], scalar=1.0,
                        in1=XP[:, tt * D:(tt + 1) * D], op0=ALU.mult, op1=ALU.add),
                        inc="bv")
                    if "d" in HB:
                        s.barrier()
                wbslot_free[0] = wbslot_free.get(("dn", 0), wbslot_free.get(0))
                wbslot_free[2048] = wbslot_free.get(("dn", 2048),
                                                    wbslot_free.get(2048))
            if q_i == DBG_Q:
                tap("dbg_xnT", AR[:, o_xnT:o_xnT + KT * TQ])
                tap("dbg_xnmT", AR[:, o_xnmT:o_xnmT + KT * TQ])
                tap("dbg_hdn", HDN[:, :])
                tap("dbg_xpxn", XP[:, 0:D])
            if b_sub == 53:
                s.barrier()
                continue
            s.wait("q", "bv")
            for tt in range(NTT):
                s.dma("q", lambda e, tt=tt: e.dma_start(
                    out=sc_xpxn[tt * 128:(tt + 1) * 128, :],
                    in_=XP[:, tt * D:(tt + 1) * D]))
            s.barrier()

            if not do_c:
                continue
            blocks = ([("xbc", CT + i) for i in range(OCH)] +
                      [("dt", CT + OCH)])
            wa_free = {}
            pb_free = {0: None, 1: None}
            pc_free = None
            pa_gate_free = None
            gate_store_tok = {}
            pbreg = 0
            for bi_, (sect, oc) in enumerate(blocks):
                sA = (bi_ % 2) * 2048
                s.need("q", wa_free.get(sA))
                wt = s.dma("q", lambda e, oc=oc, sA=sA: e.dma_start(
                    out=WA[:, sA:sA + KT * 128],
                    in_=sc_in.rearrange("(n p m) -> n p m", p=128, m=KT * 128)[oc]),
                    sem="da0" if sA == 0 else "da1")
                s.need("t", wt)
                if q_i == DBG_Q and bi_ == 0:
                    s.need("q", wt)
                    tap("dbg_win", WA[:, sA:sA + KT * 128])
                if sect == "xbc":
                    m = oc - CT
                    reg = pbreg
                    pbreg ^= 1
                    s.need("t", pb_free[reg])
                    for kt in range(KT):
                        tok = s.op("t", lambda e, kt=kt, sA=sA, reg=reg: e.matmul(
                            PB[:, reg * 512:reg * 512 + TQ],
                            WA[:, sA + kt * 128:sA + (kt + 1) * 128],
                            AR[:, o_xnmT + kt * TQ:o_xnmT + (kt + 1) * TQ],
                            start=(kt == 0), stop=(kt == KT - 1),
                            skip_group_check=True),
                            inc=("bt" if kt == KT - 1 else None))
                    wa_free[sA] = tok
                    s.need("v", tok)
                    if m < CT:
                        ddst = xbca(m * TQ, (m + 1) * TQ)
                    else:
                        ddst = XBCB[:, (m - CT) * TQ:(m - CT + 1) * TQ]
                    pb_free[reg] = s.op("v", lambda e, reg=reg, ddst=ddst: e.tensor_copy(
                        ddst, PB[:, reg * 512:reg * 512 + TQ]), inc="bv")
                elif sect == "dt":
                    s.need("t", pc_free)
                    for kt in range(KT):
                        tok = s.op("t", lambda e, kt=kt, sA=sA: e.matmul(
                            PC[0:H, 0:TQ],
                            WA[:, sA + kt * 128:sA + kt * 128 + H],
                            AR[:, o_xnmT + kt * TQ:o_xnmT + (kt + 1) * TQ],
                            start=(kt == 0), stop=(kt == KT - 1),
                            skip_group_check=True),
                            inc=("bt" if kt == KT - 1 else None))
                    wa_free[sA] = tok
                    s.need("v", tok)
                    pc_free = s.op("v", lambda e: e.tensor_copy(
                        sb["b_dtraw"][:, :], PC[0:H, 0:TQ]), inc="bv")
                else:
                    raise AssertionError("unused branch")
            KTL = min(4, KT)
            ld_i = len(blocks)
            gblk = 128 * KT * 512
            for gg in range(G2 // 512):
                gtoks = []
                for ktq in range(KT // KTL):
                    sA = (ld_i % 2) * 2048
                    ld_i += 1
                    s.need("q", wa_free.get(sA))
                    wt = s.dma("q", lambda e, gg=gg, ktq=ktq, sA=sA: e.dma_start(
                        out=WA[:, sA:sA + KTL * 512],
                        in_=sc_in[gg * gblk:(gg + 1) * gblk]
                        .rearrange("(p m) -> p m", p=128)
                        [:, ktq * KTL * 512:(ktq + 1) * KTL * 512]),
                        sem="da0" if sA == 0 else "da1")
                    s.need("t", wt)
                    if ktq == 0:
                        s.need("t", pa_gate_free)
                    for kt4 in range(KTL):
                        kt = ktq * KTL + kt4
                        for tt in range(NTT):
                            tok = s.op("t", lambda e, kt=kt, kt4=kt4, tt=tt, sA=sA:
                                       e.matmul(
                                           PA[:, tt * 512:(tt + 1) * 512],
                                           AR[:, o_xnmT + kt * TQ + tt * 128:
                                              o_xnmT + kt * TQ + (tt + 1) * 128],
                                           WA[:, sA + kt4 * 512:sA + (kt4 + 1) * 512],
                                           start=(kt == 0), stop=(kt == KT - 1),
                                           skip_group_check=True),
                                       inc=("bt" if (kt4 == KTL - 1 and tt == NTT - 1)
                                            else None))
                    wa_free[sA] = tok
                s.need("s", tok)
                for tt in range(NTT):
                    s.need("s", gate_store_tok.get(tt))
                evs = []
                for tt in range(NTT):
                    evs.append(s.op("s", lambda e, tt=tt: e.activation(
                        out=GSL[:, tt * 512:(tt + 1) * 512],
                        in_=PA[:, tt * 512:(tt + 1) * 512],
                        func=AF.Silu), inc="bs"))
                pa_gate_free = evs[-1]
                s.need("q", evs[-1])
                g0 = gg * 512
                for tt in range(NTT):
                    gate_store_tok[tt] = s.dma(
                        "q", lambda e, tt=tt, g0=g0: e.dma_start(
                            out=sc_gate[tt * 128:(tt + 1) * 128, g0:g0 + 512],
                            in_=GSL[:, tt * 512:(tt + 1) * 512]),
                        sem="dg%d" % tt)
            if q_i == DBG_Q:
                tap("dbg_dtraw", sb["b_dtraw"][:, :])
            s.barrier()

            if not do_d:
                continue
            wb_free = {}
            pb_free = {0: None, 1: None}
            pbreg = 0
            for m in range(OCH):
                sA = (m % 2) * 1024
                s.need("q", wb_free.get(sA))
                wt = s.dma("q", lambda e, m=m, sA=sA: e.dma_start(
                    out=WB[:, sA:sA + KCV * 128],
                    in_=sc_cv[:, m * KCV * 128:(m + 1) * KCV * 128]),
                    sem="db0" if sA == 0 else "db1")
                if m < CT:
                    srcmain = lambda lo, hi, m=m: xbca(m * TQ + lo, m * TQ + hi)
                else:
                    srcmain = lambda lo, hi, mb=m - CT: XBCB[:, mb * TQ + lo:mb * TQ + hi]
                hist = XBCB[:, HIST0 + m * (KCV - 1):HIST0 + (m + 1) * (KCV - 1)]
                reg = pbreg
                pbreg ^= 1
                s.need("t", wt)
                s.need("t", pb_free[reg])
                first = True
                for k in range(KCV):
                    nh = (KCV - 1) - k
                    if nh > 0:
                        s.op("t", lambda e, k=k, nh=nh, sA=sA, reg=reg, hist=hist,
                             first=first: e.matmul(
                                 PB[:, reg * 512:reg * 512 + nh],
                                 WB[:, sA + k * 128:sA + (k + 1) * 128],
                                 hist[:, k:k + nh],
                                 start=first, stop=False, skip_group_check=True))
                        first = False
                    tok = s.op("t", lambda e, k=k, nh=nh, sA=sA, reg=reg,
                               srcmain=srcmain, first=first: e.matmul(
                                   PB[:, reg * 512 + nh:reg * 512 + TQ],
                                   WB[:, sA + k * 128:sA + (k + 1) * 128],
                                   srcmain(0, TQ - nh),
                                   start=first, stop=(k == KCV - 1),
                                   skip_group_check=True),
                               inc=("bt" if k == KCV - 1 else None))
                    first = False
                wb_free[sA] = tok
                s.need("v", tok)
                hv = s.op("v", lambda e, hist=hist, srcmain=srcmain: e.tensor_copy(
                    hist[:, :], srcmain(TQ - (KCV - 1), TQ)), inc="bv")
                s.need("s", hv)
                pb_free[reg] = s.op("s", lambda e, m=m, reg=reg, srcmain=srcmain:
                                    e.activation(
                                        out=srcmain(0, TQ),
                                        in_=PB[:, reg * 512:reg * 512 + TQ],
                                        func=AF.Silu,
                                        bias=sb["c_convb"][:, m:m + 1]), inc="bs")
            if q_i == DBG_Q:
                tap("dbg_xbc0", xbca(0, TQ))
                tap("dbg_Bc", XBCB[:, 0:TQ])
            s.barrier()

            if not do_d0:
                continue
            dtb = sb["c_dtb"][:, 0:1]
            B_ = {k: sb[k] for k in ["b_dtraw", "b_dt", "b_cum", "b_t1", "b_dtA",
                                     "b_dtAn", "b_lndt", "b_wdt", "b_chi", "b_clo"]}
            s.op("s", lambda e: e.activation(out=B_["b_t1"][:, :], in_=B_["b_dtraw"][:, :],
                                             func=AF.Abs, bias=dtb))
            s.op("s", lambda e: e.activation(out=B_["b_t1"][:, :], in_=B_["b_t1"][:, :],
                                             func=AF.Exp, scale=-1.0))
            s.op("s", lambda e: e.activation(out=B_["b_t1"][:, :], in_=B_["b_t1"][:, :],
                                             func=AF.Ln, bias=1.0))
            s.op("s", lambda e: e.activation(out=B_["b_dt"][:, :], in_=B_["b_dtraw"][:, :],
                                             func=AF.Relu, bias=dtb), inc="bs")
            s.wait("v", "bs")
            s.op("v", lambda e: e.tensor_add(B_["b_dt"][:, :], B_["b_dt"][:, :],
                                             B_["b_t1"][:, :]))
            s.selfedge("v")
            s.op("v", lambda e: e.tensor_scalar_mul(B_["b_dtA"][:, :], B_["b_dt"][:, :],
                                                    sb["c_negA"][:, 0:1]))
            s.selfedge("v")
            s.op("v", lambda e: e.tensor_scalar_mul(B_["b_dtAn"][:, :], B_["b_dtA"][:, :],
                                                    -1.0))
            for ch in range(NCHQ):
                s.op("v", lambda e, ch=ch: e.tensor_tensor_scan(
                    out=B_["b_cum"][:, ch * Q:(ch + 1) * Q],
                    data0=B_["b_dtA"][:, ch * Q:(ch + 1) * Q],
                    data1=sb["b_z"][:, :], initial=0.0,
                    op0=ALU.add, op1=ALU.add))
            s.selfedge("v")
            s.op("v", lambda e: e.tensor_copy(B_["b_chi"][:, :], B_["b_cum"][:, :]))
            for ch in range(NCHQ):
                s.op("v", lambda e, ch=ch: e.tensor_copy(
                    sb["b_cendh"][:, ch:ch + 1],
                    B_["b_cum"][:, (ch + 1) * Q - 1:(ch + 1) * Q]))
            s.selfedge("v")
            s.op("v", lambda e: e.tensor_sub(B_["b_clo"][:, :], B_["b_cum"][:, :],
                                             B_["b_chi"][:, :]))
            for ch in range(NCHQ):
                s.op("v", lambda e, ch=ch: e.tensor_sub(
                    sb["b_cscr"][:, ch:ch + 1],
                    B_["b_cum"][:, (ch + 1) * Q - 1:(ch + 1) * Q],
                    sb["b_cendh"][:, ch:ch + 1]))
            s.selfedge("v")
            s.op("v", lambda e: e.tensor_copy(sb["b_cendl"][:, :],
                                              sb["b_cscr"][:, 0:NCHQ]))
            for ch in range(NCHQ):
                s.op("v", lambda e, ch=ch: e.tensor_scalar_mul(
                    sb["b_cscr"][:, NCHQ + ch:NCHQ + ch + 1],
                    B_["b_cum"][:, (ch + 1) * Q - 1:(ch + 1) * Q], 1.0))
            vb = s.op("v", lambda e: e.tensor_copy(
                sb["b_cscr"][:, 0:NCHQ], sb["b_cscr"][:, 0:NCHQ]), inc="bv")
            s.wait("s", "bv")
            for ch in range(NCHQ):
                s.op("s", lambda e, ch=ch: e.activation(
                    out=B_["b_wdt"][:, ch * Q:(ch + 1) * Q],
                    in_=B_["b_cum"][:, ch * Q:(ch + 1) * Q], func=AF.Exp, scale=-1.0,
                    bias=sb["b_cscr"][:, NCHQ + ch:NCHQ + ch + 1]))
            s.op("s", lambda e: e.activation(out=B_["b_t1"][:, :], in_=B_["b_dt"][:, :],
                                             func=AF.Ln), inc="bs")
            s.wait("v", "bs")
            s.op("v", lambda e: e.tensor_scalar_max(B_["b_lndt"][:, :], B_["b_t1"][:, :],
                                                    LNDT_CLAMP))
            s.op("v", lambda e: e.tensor_mul(B_["b_wdt"][:, :], B_["b_wdt"][:, :],
                                             B_["b_dt"][:, :]), inc="bv")
            s.selfedge("v")
            s.wait("q", "bv")
            for hl, buf in ((0, B_["b_chi"]), (1, B_["b_clo"])):
                for ch in range(NCHQ):
                    s.dma("q", lambda e, hl=hl, buf=buf, ch=ch: e.dma_start(
                        out=sc_cumhl[hl:hl + 1, ch * H * Q:(ch + 1) * H * Q]
                        .rearrange("a (h i) -> (a h) i", h=H),
                        in_=buf[:, ch * Q:(ch + 1) * Q]))
            for hl, buf in ((0, sb["b_cendh"]), (1, sb["b_cendl"])):
                s.dma("q", lambda e, hl=hl, buf=buf: e.dma_start(
                    out=bass.AP(sc_cend.tensor, hl * NCHQ * H, [[1, H], [H, NCHQ]]),
                    in_=buf[:, :], allow_slow_non_contiguous=True))
            s.wait("q", "d")
            s.dma("q", lambda e: e.dma_start(out=sb["b_cendhl"][:, :], in_=sc_cend[:, :]))
            if q_i == DBG_Q:
                tap("dbg_dt", sb["b_dt"][:, :])
                tap("dbg_cum", sb["b_cum"][:, :])
            s.barrier()

            if not do_e:
                continue
            gsl_dma_done = None
            y_done = None
            prev_lam = None
            prev_pbh = {0: None, 1: None}
            gsl_slot_free = {}
            prev_epi_v = None
            cumhl_slot_free = {0: None, 1: None}
            agi_ctr = 0
            for ch in range(NCHQ):
                prev_ydone = y_done
                c0 = ch * Q
                s.need("q", prev_epi_v)
                if gsl_dma_done is not None:
                    for k in range(8):
                        if s.cnt["dg%d" % k]:
                            s.wait("q", "dg%d" % k)
                gtok = s.dma("q", lambda e, c0=c0: e.dma_start(
                    out=GSL[:, 0:G2], in_=sc_gate[c0:c0 + 128, :]), sem="dgt")
                s.need("t", prev_pbh[0])
                btokv = None
                for i_, nm in enumerate(("b_dtAn", "b_lndt", "b_wdt")):
                    s.need("t", btokv)
                    tk = s.op("t", lambda e, nm=nm, c0=c0: e.transpose(
                        pbh(0)[:, 0:H], sb[nm][:, c0:c0 + Q],
                        identity=sb["c_I"][0:H, 0:H]), inc="bt")
                    s.need("v", tk)
                    btokv = s.op("v", lambda e, i_=i_: e.tensor_copy(
                        sb["b_tok"][:, i_ * H:(i_ + 1) * H], pbh(0)[:, 0:H]), inc="bv")
                s.need("t", prev_lam)
                ctok = s.op("t", lambda e, c0=c0: e.transpose(
                    PD[:, 0:H], sb["b_cum"][:, c0:c0 + Q],
                    identity=sb["c_If"][0:H, 0:H]), inc="bt")
                s.need("s", ctok)
                s.need("s", prev_ydone)
                etok = s.op("s", lambda e: e.activation(
                    out=sb["b_Etok"][:, :], in_=PD[:, 0:H], func=AF.Exp), inc="bs")
                s.need("t", etok)
                ltok = s.op("t", lambda e, ch=ch: e.matmul(
                    PD[:, 128:128 + H], sb["c_ones2"][:, :],
                    sb["b_cendhl"][:, ch * H:(ch + 1) * H],
                    start=True, stop=True, skip_group_check=True), inc="bt")
                s.need("s", ltok)
                lamtok = s.op("s", lambda e: e.activation(
                    out=sb["b_Lam"][:, :], in_=PD[:, 128:128 + H], func=AF.Exp),
                    inc="bs")
                prev_lam = lamtok
                tp_half_free = {0: btokv, 1: prev_pbh[1]}
                for m in range(CT):
                    half = m % 2
                    s.need("t", tp_half_free[half])
                    ttok = s.op("t", lambda e, m=m, half=half, c0=c0: e.transpose(
                        pbh(half)[:, 0:128], xbca(m * TQ + c0, m * TQ + c0 + 128),
                        identity=IDb), inc="bt")
                    s.need("v", ttok)
                    tp_half_free[half] = s.op(
                        "v", lambda e, m=m, half=half: e.tensor_copy(
                            AR[:, o_xsT + m * 128:o_xsT + (m + 1) * 128],
                            pbh(half)[:, 0:128]), inc="bv")
                half = CT % 2
                s.need("t", tp_half_free[half])
                ttok = s.op("t", lambda e, c0=c0, half=half: e.transpose(
                    pbh(half)[:, 0:128], XBCB[:, c0:c0 + 128], identity=IDb), inc="bt")
                s.need("v", ttok)
                btv = s.op("v", lambda e, half=half: e.tensor_copy(
                    sb["b_BT"][:, :], pbh(half)[:, 0:128]), inc="bv")
                tp_half_free[half] = btv
                xsT_ready = ("bv", s.cnt["bv"])
                s.need("p", xsT_ready)
                wxtok = s.op("p", lambda e: e.tensor_tensor(
                    AR[:, o_wx:o_wx + G2].rearrange("p (h f) -> p h f", h=H),
                    AR[:, o_xsT:o_xsT + G2].rearrange("p (h f) -> p h f", h=H),
                    sb["b_tok"][:, 2 * H:3 * H, None].broadcast_to([128, H, PH]),
                    op=ALU.mult), inc="bp")
                if q_i == DBG_Q and ch == 0:
                    tap("dbg_tok", sb["b_tok"][:, :])
                    tap("dbg_Etok", sb["b_Etok"][:, :])
                    tap("dbg_Lam", sb["b_Lam"][:, :])
                    tap("dbg_xsT", AR[:, o_xsT:o_xsT + G2])
                    s.wait("q", "bp")
                    tap("dbg_WX", AR[:, o_wx:o_wx + G2])
                gmt = s.op("t", lambda e, c0=c0: e.matmul(
                    PC[:, 0:128], XBCB[:, c0:c0 + 128],
                    XBCB[:, TQ + c0:TQ + c0 + 128],
                    start=True, stop=True, skip_group_check=True), inc="bt")
                s.need("v", gmt)
                gmv = s.op("v", lambda e: e.tensor_mul(
                    sb["b_Gm"][:, :], PC[:, 0:128], sb["c_tri"][:, :]), inc="bv")

                if "h" in HB:
                    s.barrier()
                exp_done = None
                gml_done = None
                for ag in range(NAG):
                    h0 = ag * AGH
                    cslot = agi_ctr % 2
                    agi_ctr += 1
                    s.need("q", cumhl_slot_free[cslot])
                    chl = s.dma("q", lambda e, ch=ch, h0=h0, cslot=cslot: e.dma_start(
                        out=sb["b_cumhl"][:, cslot * AGH * Q:(cslot + 1) * AGH * Q],
                        in_=sc_cumhl[:, (ch * H + h0) * Q:(ch * H + h0 + AGH) * Q]),
                        sem="dch%d" % cslot)
                    s.need("t", exp_done)
                    s.need("t", chl)
                    for hh in range(AGH):
                        s.op("t", lambda e, hh=hh, cslot=cslot: e.matmul(
                            PA[:, hh * 128:(hh + 1) * 128], sb["c_ones2"][:, :],
                            sb["b_cumhl"][:, (cslot * AGH + hh) * Q:
                                          (cslot * AGH + hh + 1) * Q],
                            start=(hh % 4 == 0), stop=False, skip_group_check=True))
                    for hh in range(AGH):
                        h = h0 + hh
                        s.op("t", lambda e, h=h, hh=hh: e.matmul(
                            PA[:, hh * 128:(hh + 1) * 128], sb["c_tri"][:, :],
                            sb["b_tok"][:, None, h:h + 1].broadcast_to([128, 128, 1]),
                            start=False, stop=False, skip_group_check=True))
                    for hh in range(AGH):
                        h = h0 + hh
                        s.op("t", lambda e, h=h, hh=hh: e.matmul(
                            PA[:, hh * 128:(hh + 1) * 128], sb["c_I"][:, :],
                            sb["b_tok"][:, None, H + h:H + h + 1]
                            .broadcast_to([128, 128, 1]),
                            start=False, stop=False, skip_group_check=True))
                    nmi = AGH * 128 // 512
                    for mi in range(nmi):
                        tok = s.op("t", lambda e, mi=mi: e.matmul(
                            PA[:, mi * 512:(mi + 1) * 512], sb["c_I"][:, :],
                            sb["c_up"][:, None, :].broadcast_to([128, AGH, 128])
                            [:, mi * 4:(mi + 1) * 4, :],
                            start=False, stop=True, skip_group_check=True),
                            inc=("bt" if mi == nmi - 1 else None))
                    cumhl_slot_free[cslot] = tok
                    s.need("s", tok)
                    exp_done = s.op("s", lambda e: e.activation(
                        out=AR[:, o_expl:o_expl + AGH * 128], in_=PA[:, 0:AGH * 128],
                        func=AF.Exp), inc="bs")
                    s.need("v", exp_done)
                    s.need("v", gmv)
                    gml_done = s.op("v", lambda e, h0=h0: e.tensor_tensor(
                        out=AR[:, o_gml + h0 * 128:o_gml + (h0 + AGH) * 128]
                        .rearrange("p (h f) -> p h f", h=AGH),
                        in0=AR[:, o_expl:o_expl + AGH * 128]
                        .rearrange("p (h f) -> p h f", h=AGH),
                        in1=sb["b_Gm"][:, None, :].broadcast_to([128, AGH, 128]),
                        op=ALU.mult), inc="bv")
                    exp_done = gml_done
                if "g" in HB:
                    s.barrier()
                for yg in range(NYG):
                    h0 = yg * YGH
                    s.need("t", y_done)
                    s.need("t", gml_done)
                    itok = s.op("t", lambda e, yg=yg, c0=c0: e.matmul(
                        PB[:, 0:512], XBCB[:, TQ + c0:TQ + c0 + 128],
                        SBb[:, yg * YGH * PH:(yg + 1) * YGH * PH],
                        start=True, stop=True, skip_group_check=True), inc="bt")
                    for hh in range(YGH):
                        h = h0 + hh
                        s.op("t", lambda e, h=h, hh=hh: e.matmul(
                            PB[:, 512 + hh * PH:512 + (hh + 1) * PH],
                            AR[:, o_gml + h * 128:o_gml + (h + 1) * 128],
                            AR[:, o_xsT + h * PH:o_xsT + (h + 1) * PH],
                            start=(hh == 0), stop=False, skip_group_check=True))
                    nmt = YGH * PH // 128
                    for mm in range(nmt):
                        mt = yg * nmt + mm
                        tok = s.op("t", lambda e, mt=mt, mm=mm, c0=c0: e.matmul(
                            PB[:, 512 + mm * 128:512 + (mm + 1) * 128],
                            xbca(mt * TQ + c0, mt * TQ + c0 + 128),
                            sb["c_Dd"][:, mt * 128:(mt + 1) * 128],
                            start=False, stop=(mm == nmt - 1),
                            skip_group_check=True),
                            inc=("bt" if mm == nmt - 1 else None))
                    s.need("v", tok)
                    s.op("v", lambda e, yg=yg, h0=h0: e.tensor_tensor(
                        out=XY[:, yg * YGH * PH:(yg + 1) * YGH * PH]
                        .rearrange("p (h f) -> p h f", h=YGH),
                        in0=PB[:, 0:YGH * PH].rearrange("p (h f) -> p h f", h=YGH),
                        in1=sb["b_Etok"][:, h0:h0 + YGH, None]
                        .broadcast_to([128, YGH, PH]), op=ALU.mult))
                    s.selfedge("v")
                    yv = s.op("v", lambda e, yg=yg: e.scalar_tensor_tensor(
                        out=XY[:, yg * YGH * PH:(yg + 1) * YGH * PH],
                        in0=PB[:, 512:512 + YGH * PH], scalar=1.0,
                        in1=XY[:, yg * YGH * PH:(yg + 1) * YGH * PH],
                        op0=ALU.mult, op1=ALU.add), inc="bv")
                    y_done = yv
                    s.need("t", wxtok)
                    s.need("t", yv)
                    dstok = s.op("t", lambda e, yg=yg: e.matmul(
                        PC[:, 0:512], sb["b_BT"][:, :],
                        AR[:, o_wx + yg * YGH * PH:o_wx + (yg + 1) * YGH * PH],
                        start=True, stop=True, skip_group_check=True), inc="bt")
                    s.need("v", dstok)
                    s.op("v", lambda e, yg=yg, h0=h0: e.tensor_tensor(
                        out=sb["b_Sd"][:, :].rearrange("p (h f) -> p h f", h=YGH),
                        in0=ST[:, yg * YGH * PH:(yg + 1) * YGH * PH]
                        .rearrange("p (h f) -> p h f", h=YGH),
                        in1=sb["b_Lam"][:, h0:h0 + YGH, None]
                        .broadcast_to([128, YGH, PH]), op=ALU.mult))
                    s.selfedge("v")
                    s.op("v", lambda e, yg=yg: e.scalar_tensor_tensor(
                        out=ST[:, yg * YGH * PH:(yg + 1) * YGH * PH],
                        in0=PC[:, 0:512], scalar=1.0, in1=sb["b_Sd"][:, :],
                        op0=ALU.mult, op1=ALU.add))
                    s.selfedge("v")
                    y_done = s.op("v", lambda e, yg=yg: e.tensor_copy(
                        SBb[:, yg * YGH * PH:(yg + 1) * YGH * PH],
                        ST[:, yg * YGH * PH:(yg + 1) * YGH * PH]), inc="bv")
                if q_i == DBG_Q and ch == 0:
                    tap("dbg_Gm", sb["b_Gm"][:, :])
                    tap("dbg_gml", AR[:, o_gml:o_gml + H * 128])
                    tap("dbg_y", XY[:, :])
                    tap("dbg_S", ST[:, :])
                    tap("dbg_gate", GSL[:, 0:G2])
                s.need("v", gtok)
                ygv = s.op("v", lambda e: e.tensor_mul(
                    AR[:, o_yg:o_yg + G2], XY[:, 0:G2], GSL[:, 0:G2]), inc="bv")
                s.need("s", ygv)
                sqv = s.op("s", lambda e: e.activation(
                    out=AR[:, o_wx:o_wx + G2], in_=AR[:, o_yg:o_yg + G2],
                    func=AF.Square, accum_out=st[:, 14:15]), inc="bs")
                s.selfedge("s")
                s.op("s", lambda e: e.activation(out=st[:, 15:16], in_=st[:, 14:15],
                                                 func=AF.Ln, scale=1.0 / G2,
                                                 bias=float(EPS)))
                rstok = s.op("s", lambda e: e.activation(
                    out=st[:, 15:16], in_=st[:, 15:16], func=AF.Exp, scale=-0.5),
                    inc="bs")
                s.need("v", rstok)
                s.selfedge("v")
                ygn = s.op("v", lambda e: e.tensor_scalar_mul(
                    AR[:, o_yg:o_yg + G2], AR[:, o_yg:o_yg + G2],
                    st[:, 15:16]), inc="bv")
                s.selfedge("v")
                if q_i == DBG_Q and ch == 0:
                    tap("dbg_ygn", AR[:, o_yg:o_yg + G2])
                s.need("t", ygn)
                tp_half_free = {0: None, 1: None}
                for m in range(OKT):
                    half = m % 2
                    gslot = (m % 8) * 128
                    s.need("t", tp_half_free[half])
                    ttok = s.op("t", lambda e, m=m, half=half: e.transpose(
                        pbh(half)[:, 0:128], AR[:, o_yg + m * 128:o_yg + (m + 1) * 128],
                        identity=IDb), inc="bt")
                    s.need("v", ttok)
                    s.need("v", gsl_slot_free.get(gslot))
                    ev = s.op("v", lambda e, half=half, gslot=gslot: e.tensor_copy(
                        GSL[:, gslot:gslot + 128], pbh(half)[:, 0:128]), inc="bv")
                    tp_half_free[half] = ev
                    s.need("q", ev)
                    gsl_slot_free[gslot] = s.dma(
                        "q", lambda e, m=m, c0=c0, gslot=gslot: e.dma_start(
                            out=sc_ygnT[m, :, c0:c0 + 128],
                            in_=GSL[:, gslot:gslot + 128]),
                        sem="dg%d" % (gslot // 128))
                gsl_dma_done = ("d", s.cnt["d"])
                prev_epi_v = ("bv", s.cnt["bv"])
                prev_pbh = dict(tp_half_free)
                if "e" in HB:
                    s.barrier()
            s.barrier()

            if not do_f:
                continue
            wb_free = {}
            wy_free = {}
            out_dma = None
            for tt in range(NTT):
                s.need("q", out_dma)
                xr = s.dma("q", lambda e, tt=tt: e.dma_start(
                    out=XY[:, 0:D], in_=sc_xpxn[tt * 128:(tt + 1) * 128, :]),
                    sem="dx0")
                pa_ev = None
                if q_i == DBG_Q and tt == 0:
                    s.need("q", xr)
                    tap("dbg_fx", XY[:, 0:D])
                for kt in range(OKT):
                    sA = (kt % 2) * 2048
                    sy = (kt % 2) * TQ
                    s.need("q", wb_free.get(sA))
                    wt = s.dma("q", lambda e, kt=kt, sA=sA: e.dma_start(
                        out=WB[:, sA:sA + D], in_=sc_o[kt * 128:(kt + 1) * 128, :]),
                        sem="db0" if sA == 0 else "db1")
                    s.need("q", wy_free.get(sy))
                    yt = s.dma("q", lambda e, kt=kt, sy=sy: e.dma_start(
                        out=WY[:, sy:sy + TQ], in_=sc_ygnT[kt, :, :]),
                        sem="dy0" if sy == 0 else "dy1")
                    s.need("t", wt)
                    s.need("t", yt)
                    if kt == 0:
                        s.need("t", pa_ev)
                    ocl = [(i * 512, min(512, D - i * 512))
                           for i in range((D + 511) // 512)]
                    for oi, (o0, olen) in enumerate(ocl):
                        tok = s.op("t", lambda e, kt=kt, tt=tt, sA=sA, sy=sy, o0=o0,
                                   olen=olen: e.matmul(
                                       PA[:, o0:o0 + olen],
                                       WY[:, sy + tt * 128:sy + (tt + 1) * 128],
                                       WB[:, sA + o0:sA + o0 + olen],
                                       start=(kt == 0), stop=(kt == OKT - 1),
                                       skip_group_check=True),
                                   inc=("bt" if oi == len(ocl) - 1 else None))
                    wb_free[sA] = tok
                    wy_free[sy] = tok
                    if "f" in HB:
                        s.barrier()
                s.need("v", tok)
                s.need("v", xr)
                fv = s.op("v", lambda e: e.scalar_tensor_tensor(
                    out=XY[:, D:2 * D], in0=PA[:, 0:D], scalar=1.0,
                    in1=XY[:, 0:D], op0=ALU.mult, op1=ALU.add), inc="bv")
                s.need("q", fv)
                out_dma = s.dma("q", lambda e, tt=tt, T0=T0: e.dma_start(
                    out=out_t[T0 + tt * 128:T0 + (tt + 1) * 128, :],
                    in_=XY[:, D:2 * D]), sem="do")
                s.need("v", out_dma)
                pa_ev = fv
            s.barrier()

        if dump:
            s.barrier()
            dmp = {
                "dmp_gate": sc_gate, "dmp_xpxn": sc_xpxn,
                "dmp_cumhl": sc_cumhl, "dmp_cend": sc_cend,
            }
            dmp_sb = {
                "dmp_XBCB": XBCB, "dmp_XPb": None, "dmp_AR": AR,
                "dmp_cum": sb["b_cum"], "dmp_dt": sb["b_dt"],
                "dmp_dtraw": sb["b_dtraw"], "dmp_Etok": sb["b_Etok"],
                "dmp_Lam": sb["b_Lam"], "dmp_tok": sb["b_tok"],
                "dmp_ST": ST, "dmp_Gm": sb["b_Gm"], "dmp_XY": XY,
                "dmp_cumhl_sb": sb["b_cumhl"], "dmp_cendhl": sb["b_cendhl"],
                "dmp_cscr": sb["b_cscr"],
            }
            for nm, t_ in dmp.items():
                o_ = nc.dram_tensor(nm, list(t_.tensor.shape),
                                    t_.tensor.dtype, kind="ExternalOutput").ap()
                s.dma("q", lambda e, o_=o_, t_=t_: e.dma_start(out=o_, in_=t_))
            for nm, t_ in dmp_sb.items():
                if t_ is None:
                    t_ = XP[:, :].bitcast(BF16)
                else:
                    t_ = t_[:, :]
                o_ = nc.dram_tensor(nm, [t_.shape[0], t_.shape[1]],
                                    t_.dtype, kind="ExternalOutput").ap()
                s.dma("q", lambda e, o_=o_, t_=t_: e.dma_start(out=o_, in_=t_))
        for nm_ in s.q_dma_sems():
            s.wait("q", nm_)
        global _LAST_SCH
        _LAST_SCH = s
        s.simulate()
        s.emit(nc, sems)
    return nc


def kernel(L0_attn_hidden_states, L0_attn_norm_weight, L0_attn_in_proj_w,
           L0_attn_conv_w, L0_attn_conv_b, L0_attn_dt_bias, L0_attn_A_log,
           L0_attn_D, L0_attn_gate_norm_weight, L0_attn_out_proj_w,
           L0_ffn_gate_w, L0_ffn_up_w, L0_ffn_down_w, L0_ffn_gate_s,
           L0_ffn_up_s, L0_ffn_down_s):
    c = derive(full_cfg())
    w = dict(in_proj_w=np.asarray(L0_attn_in_proj_w, np.float32),
             conv_w=np.asarray(L0_attn_conv_w, np.float32),
             conv_b=np.asarray(L0_attn_conv_b, np.float32),
             dt_bias=np.asarray(L0_attn_dt_bias, np.float32),
             A_log=np.asarray(L0_attn_A_log, np.float32),
             D=np.asarray(L0_attn_D, np.float32),
             gate_norm_weight=np.asarray(L0_attn_gate_norm_weight, np.float32),
             out_proj_w=np.asarray(L0_attn_out_proj_w, np.float32),
             norm_weight=np.asarray(L0_attn_norm_weight, np.float32),
             gate_w=np.asarray(L0_ffn_gate_w, np.float32),
             up_w=np.asarray(L0_ffn_up_w, np.float32),
             down_w=np.asarray(L0_ffn_down_w, np.float32),
             gate_s=np.asarray(L0_ffn_gate_s, np.float32),
             up_s=np.asarray(L0_ffn_up_s, np.float32),
             down_s=np.asarray(L0_ffn_down_s, np.float32))
    consts = host_arrays(c, w)
    x = np.asarray(L0_attn_hidden_states, np.float32)
    nc = build(c)
    in_maps = []
    for b in range(8):
        m = dict(consts)
        m["x"] = np.ascontiguousarray(x[b])
        in_maps.append(m)
    res = run_bass_kernel_spmd(nc, in_maps, core_ids=list(range(8)))
    out = np.stack([np.asarray(res.results[b]["out"]) for b in range(8)], axis=0)
    return out.astype(np.float32)


# version 03
# speedup vs baseline: 1.1895x; 1.1895x over previous
import contextlib
from collections import defaultdict

import numpy as np
import concourse.bass as bass
import concourse.mybir as mybir
from concourse.bass_utils import run_bass_kernel_spmd

F32 = mybir.dt.float32
BF16 = mybir.dt.bfloat16
AF = mybir.ActivationFunctionType
ALU = mybir.AluOpType

PH = 64
NST = 128
KCV = 4
Q = 128
EPS = 1e-5
LNDT_CLAMP = -30.0
ENGS = ("t", "v", "s", "p", "q")
_LAST_SCH = None
DBG_Q = 0
HB = ""
ZINC = False
DPSPLIT = True


def K_HBX():
    return HB
DBG_PH = 'ABCDE0'


def full_cfg():
    return dict(D=2048, T=2048, H=64, FI=5632)


def derive(cfg):
    c = dict(cfg)
    D, T, H, FI = c["D"], c["T"], c["H"], c["FI"]
    c["G2"] = H * PH
    assert c["G2"] == 2 * D
    c["CONVD"] = c["G2"] + 2 * NST
    c["PROJ"] = c["G2"] + c["CONVD"] + H
    c["PROJP"] = c["G2"] + c["CONVD"] + 128
    c["KT"] = D // 128
    c["CT"] = c["G2"] // 128
    c["OCH"] = c["CONVD"] // 128
    c["NBLK"] = c["PROJP"] // 128
    c["ICH"] = FI // 128
    c["TQ"] = T // 4
    c["NTT"] = c["TQ"] // 128
    c["NCHQ"] = c["TQ"] // Q
    c["AGH"] = min(16, H)
    c["YGH"] = min(8, H)
    c["OKT"] = c["G2"] // 128
    fg = min(22, c["ICH"])
    while c["ICH"] % fg:
        fg -= 1
    c["FGRP"] = fg
    assert c["TQ"] <= 512 and c["CT"] % 4 == 0 and c["D"] % 128 == 0
    return c


def host_arrays(c, w):
    D, H, FI = c["D"], c["H"], c["FI"]
    CONVD, PROJ, PROJP = c["CONVD"], c["PROJ"], c["PROJP"]
    KT, CT, OCH, NBLK, ICH, OKT = (c["KT"], c["CT"], c["OCH"], c["NBLK"],
                                   c["ICH"], c["OKT"])
    o = {}
    o["c_tri"] = (np.arange(128)[:, None] <= np.arange(128)[None, :]).astype(np.float32)
    o["c_I"] = np.eye(128, dtype=np.float32)
    o["c_up"] = np.where(np.arange(128)[:, None] > np.arange(128)[None, :],
                         np.float32(-1.0e6), np.float32(0.0))
    o["c_ones2"] = np.ones((2, 128), np.float32)
    wp = np.zeros((PROJP, D), np.float32)
    wp[:PROJ] = w["in_proj_w"]
    G2 = 2 * D
    gate_part = np.ascontiguousarray(
        wp[:G2].reshape(G2 // 512, 512, KT, 128).transpose(0, 3, 2, 1))
    rest_part = np.ascontiguousarray(
        wp[G2:].reshape(NBLK - G2 // 128, 128, KT, 128).transpose(0, 3, 2, 1))
    o["w_in"] = np.concatenate([gate_part.reshape(-1), rest_part.reshape(-1)])
    for nm, src in (("w_g", w["gate_w"]), ("w_u", w["up_w"])):
        o[nm] = np.ascontiguousarray(
            src.reshape(ICH, 128, KT, 128).transpose(0, 3, 2, 1)).reshape(-1)
    o["w_d"] = np.ascontiguousarray(w["down_w"])
    o["w_out"] = np.ascontiguousarray(w["out_proj_w"].T)
    cw = w["conv_w"].reshape(CONVD, KCV)
    cd = np.zeros((128, OCH * KCV * 128), np.float32)
    for m in range(OCH):
        for k in range(KCV):
            blk = (m * KCV + k) * 128
            cd[np.arange(128), blk + np.arange(128)] = cw[m * 128:(m + 1) * 128, k]
    o["c_convd"] = cd
    dd = np.zeros((128, CT * 128), np.float32)
    for m in range(CT):
        ch = (m * 128 + np.arange(128)) // PH
        dd[np.arange(128), m * 128 + np.arange(128)] = w["D"][ch]
    o["c_Dd"] = dd
    o["c_convb"] = np.ascontiguousarray(w["conv_b"].reshape(OCH, 128).T)
    o["c_nw"] = np.ascontiguousarray(w["norm_weight"].reshape(KT, 128).T)
    o["c_dtb"] = w["dt_bias"].reshape(H, 1).astype(np.float32)
    o["c_alog"] = w["A_log"].reshape(H, 1).astype(np.float32)
    o["c_gs"] = np.ascontiguousarray(w["gate_s"].reshape(ICH, 128).T)
    o["c_us"] = np.ascontiguousarray(w["up_s"].reshape(ICH, 128).T)
    o["c_ds"] = np.ascontiguousarray(w["down_s"].reshape(KT, 128).T)
    o["c_gnw"] = np.ascontiguousarray(w["gate_norm_weight"].reshape(OKT, 128).T)
    return o


class Sch:
    def __init__(self):
        self.ops = {e: [] for e in ENGS}
        self.cnt = defaultdict(int)
        self.bcnt = {e: 0 for e in ENGS}

    def op(self, e, fn, inc=None, n=1):
        if inc is None and ZINC and e in ("t", "v", "s", "p"):
            inc, n = self.OWN[e], 1
        incs = []
        if inc is not None:
            self.cnt[inc] += n
            incs.append((inc, n))
        self.ops[e].append(("op", fn, incs))
        return (inc, self.cnt[inc]) if inc is not None else None

    def dma(self, e, fn, sem=None):
        if sem is None:
            sem = "d" if (e == "q" or not DPSPLIT) else "dp"
        return self.op(e, fn, inc=sem, n=16)

    def wait(self, e, sem, val=None):
        v = self.cnt[sem] if val is None else val
        self.ops[e].append(("wait", sem, v))

    def selfedge(self, e):
        for item in reversed(self.ops[e]):
            if item[0] == "op":
                if not item[2]:
                    own = self.OWN[e]
                    self.cnt[own] += 1
                    item[2].append((own, 1))
                nm, n = item[2][0]
                self.ops[e].append(("wait", nm, self.cnt[nm]))
                return
            if item[0] == "wait":
                continue
            return

    def need(self, e, tok):
        if tok is not None:
            self.ops[e].append(("wait", tok[0], tok[1]))

    OWN = {"t": "bt", "v": "bv", "s": "bs", "p": "bp", "q": "d"}
    P_DMA = ("dp", "dp0", "dp1")

    def q_dma_sems(self):
        return [k for k in self.cnt
                if k.startswith("d") and k not in self.P_DMA and self.cnt[k]]

    def barrier(self, engines=ENGS):
        for e in engines:
            self.bcnt[e] += 1
            if e == "q":
                owns = self.q_dma_sems()
            elif e == "p":
                owns = ["bp", "dp", "dp0", "dp1"] if DPSPLIT else ["bp", "d"]
            else:
                owns = [self.OWN[e]]
            for own in owns:
                if self.cnt[own]:
                    self.ops[e].append(("wait", own, self.cnt[own]))
            self.ops[e].append(("barinc", e))
        for e in engines:
            for oth in engines:
                if oth != e:
                    self.ops[e].append(("wait", "B" + oth, self.bcnt[oth]))

    def simulate(self):
        pos = {e: 0 for e in ENGS}
        sem = defaultdict(int)
        progress = True
        while progress:
            progress = False
            for e in ENGS:
                while pos[e] < len(self.ops[e]):
                    item = self.ops[e][pos[e]]
                    if item[0] == "op":
                        for nm_, n_ in item[2]:
                            sem[nm_] += n_
                        pos[e] += 1
                        progress = True
                    elif item[0] == "wait":
                        if sem[item[1]] >= item[2]:
                            pos[e] += 1
                            progress = True
                        else:
                            break
                    else:
                        sem["B" + item[1]] += 1
                        pos[e] += 1
                        progress = True
        stuck = {e: pos[e] for e in ENGS if pos[e] < len(self.ops[e])}
        if stuck:
            msg = []
            for e, p in stuck.items():
                it = self.ops[e][p]
                msg.append(f"{e}@{p}/{len(self.ops[e])}: wait {it[1]} >= {it[2]} "
                           f"(sim has {dict(sem).get(it[1], 0)})")
            raise RuntimeError("DEADLOCK in schedule:\n" + "\n".join(msg))

    def emit(self, nc, sems):
        finals = dict(self.cnt)
        for e in ENGS:
            finals["B" + e] = self.bcnt[e]
        for e in ENGS:
            for item in self.ops[e]:
                if item[0] == "wait":
                    assert item[2] <= finals.get(item[1], 0), (
                        f"unreachable wait {item[1]} >= {item[2]} (final "
                        f"{finals.get(item[1], 0)}) on engine {e}")

        def run(eng, stream):
            for item in self.ops[stream]:
                if item[0] == "op":
                    _, fn, incs = item
                    inst = fn(eng)
                    for nm, n in incs:
                        inst = inst.then_inc(sems[nm], n)
                elif item[0] == "wait":
                    eng.wait_ge(sems[item[1]], item[2])
                else:
                    eng.nop().then_inc(sems["B" + item[1]], 1)

        with nc.Block() as block:
            @block.tensor
            def _(e):
                run(e, "t")

            @block.vector
            def _(e):
                run(e, "v")

            @block.scalar
            def _(e):
                run(e, "s")

            @block.gpsimd
            def _(e):
                run(e, "p")

            @block.sync
            def _(e):
                run(e, "q")


def build(c, limit=99, dbg=False, dump=False):
    nc = bass.Bass("TRN2", target_bir_lowering=False, debug=False)
    D, T, H, FI = c["D"], c["T"], c["H"], c["FI"]
    G2, CONVD = c["G2"], c["CONVD"]
    KT, CT, OCH, NBLK, ICH = c["KT"], c["CT"], c["OCH"], c["NBLK"], c["ICH"]
    TQ, NTT, NCHQ = c["TQ"], c["NTT"], c["NCHQ"]
    AGH, YGH, OKT, FGRP = c["AGH"], c["YGH"], c["OKT"], c["FGRP"]
    NYG, NAG, NFG = H // YGH, H // AGH, ICH // FGRP

    ap = {}
    for nm, shp in [
        ("x", (T, D)), ("w_in", (NBLK * KT * 128 * 128,)),
        ("w_g", (D * FI,)), ("w_u", (D * FI,)),
        ("w_d", (D, FI)), ("w_out", (G2, D)),
        ("c_convd", (128, OCH * KCV * 128)), ("c_Dd", (128, CT * 128)),
        ("c_convb", (128, OCH)), ("c_nw", (128, KT)), ("c_dtb", (H, 1)),
        ("c_alog", (H, 1)), ("c_gs", (128, ICH)), ("c_us", (128, ICH)),
        ("c_ds", (128, KT)), ("c_gnw", (128, OKT)), ("c_tri", (128, 128)),
        ("c_I", (128, 128)), ("c_up", (128, 128)), ("c_ones2", (2, 128)),
    ]:
        ap[nm] = nc.dram_tensor(nm, list(shp), F32, kind="ExternalInput").ap()
    out_t = nc.dram_tensor("out", [T, D], F32, kind="ExternalOutput").ap()
    dbg_t = {}
    if dbg:
        for nm, shp, dt_ in [
            ("dbg_xn", (128, D), BF16), ("dbg_xnm", (128, D), BF16),
            ("dbg_xpxn", (128, D), F32), ("dbg_hdn", (128, FGRP * TQ), BF16),
            ("dbg_gate", (128, G2), BF16), ("dbg_dtraw", (H, TQ), F32),
            ("dbg_dt", (H, TQ), F32), ("dbg_cum", (H, TQ), F32),
            ("dbg_xbc0", (128, TQ), BF16), ("dbg_Bc", (128, TQ), BF16),
            ("dbg_tok", (128, 3 * H), BF16), ("dbg_Gm", (128, 128), BF16),
            ("dbg_gml", (128, H * 128), BF16), ("dbg_Etok", (128, H), F32),
            ("dbg_Lam", (128, H), F32), ("dbg_y", (128, 2 * D), F32),
            ("dbg_ygn", (128, G2), BF16), ("dbg_S", (128, G2), F32),
            ("dbg_xsT", (128, G2), BF16), ("dbg_WX", (128, G2), BF16),
            ("dbg_stat", (128, 16), F32), ("dbg_x", (128, D), F32),
            ("dbg_xnT", (128, KT * TQ), BF16), ("dbg_xnmT", (128, KT * TQ), BF16),
            ("dbg_wd", (128, D), BF16), ("dbg_win", (128, KT * 128), BF16),
            ("dbg_fx", (128, D), F32),
        ]:
            dbg_t[nm] = nc.dram_tensor(nm, list(shp), dt_, kind="ExternalOutput").ap()

    def dsc(nm, shp, dt=BF16):
        return nc.dram_tensor(nm, list(shp), dt, kind="Internal").ap()

    sc_in = dsc("sc_in", (NBLK * KT * 128 * 128,))
    sc_g = dsc("sc_g", (D * FI,))
    sc_u = dsc("sc_u", (D * FI,))
    sc_d = dsc("sc_d", (FI, D))
    sc_o = dsc("sc_o", (G2, D))
    sc_cv = dsc("sc_cv", (128, OCH * KCV * 128))
    sc_gate = dsc("sc_gate", (TQ, G2))
    sc_xpxn = dsc("sc_xpxn", (TQ, D), F32)
    sc_ygnT = dsc("sc_ygnT", (OKT, 128, TQ))
    sc_cumhl = dsc("sc_cumhl", (2, NCHQ * H * Q))
    sc_cend = dsc("sc_cend", (2, NCHQ * H))

    o_xnT, o_xnmT = 0, KT * TQ
    o_xsT, o_wx, o_gml = 0, G2, 2 * G2
    o_expl = o_gml + H * 128
    o_yg = o_expl + AGH * 128
    AR_COLS = max(2 * KT * TQ, o_yg + G2)

    es = contextlib.ExitStack()
    sb = {}

    def sbuf(name, shape, dt=BF16):
        sb[name] = es.enter_context(nc.sbuf_tensor("sb_" + name, list(shape), dt))

    with es:
        for nm in ["c_tri", "c_I", "c_up"]:
            sbuf(nm, (128, 128))
        sbuf("c_If", (128, 128), F32)
        sbuf("c_ones2", (2, 128))
        sbuf("c_Dd", (128, CT * 128))
        sbuf("c_convb", (128, OCH), F32)
        sbuf("c_nw", (128, KT), F32)
        sbuf("c_dtb", (H, 1), F32)
        sbuf("c_negA", (H, 1), F32)
        sbuf("c_gs", (128, ICH), F32)
        sbuf("c_us", (128, ICH), F32)
        sbuf("c_ds", (128, KT), F32)
        sbuf("c_gnw", (128, OKT), F32)
        sbuf("S", (128, G2), F32)
        sbuf("Sb", (128, G2))
        sbuf("AR", (128, AR_COLS))
        sbuf("XP", (128, NTT * D), F32)
        sbuf("XBCB", (128, 2 * TQ + OCH * (KCV - 1)))
        sbuf("XY", (128, 2 * D), F32)
        sbuf("HDN", (128, FGRP * TQ))
        sbuf("GSL", (128, max(G2, NTT * 512)))
        sbuf("WA", (128, 2 * 2048))
        sbuf("WB", (128, 2 * 2048))
        sbuf("WY", (128, 2 * TQ))
        for nm in ["b_dtraw", "b_dt", "b_cum", "b_t1"]:
            sbuf(nm, (H, TQ), F32)
        for nm in ["b_dtA", "b_dtAn", "b_lndt", "b_wdt", "b_chi", "b_clo"]:
            sbuf(nm, (H, TQ))
        sbuf("b_cendh", (H, NCHQ))
        sbuf("b_cendl", (H, NCHQ))
        sbuf("b_z", (H, Q), F32)
        sbuf("b_cumhl", (2, 2 * AGH * Q))
        sbuf("b_cendhl", (2, NCHQ * H))
        sbuf("b_tok", (128, 3 * H))
        sbuf("b_Etok", (128, H), F32)
        sbuf("b_Lam", (128, H), F32)
        sbuf("b_Gm", (128, 128))
        sbuf("b_BT", (128, NST))
        sbuf("b_Sd", (128, YGH * PH), F32)
        sbuf("b_stat", (128, 16), F32)
        sbuf("b_cscr", (H, 2 * NCHQ), F32)
        sbuf("c_bias", (128, 2), F32)
        nc.const_aps.register(F32, float(EPS), sb["c_bias"][:, 0:1])
        nc.const_aps.register(F32, 1.0, sb["c_bias"][:, 1:2])

        PA = es.enter_context(nc.psum_tensor("PA", [128, 2048], F32))
        PB = es.enter_context(nc.psum_tensor("PB", [128, 1024], F32))
        PC = es.enter_context(nc.psum_tensor("PC", [128, 512], F32))
        PD = es.enter_context(nc.psum_tensor("PD", [128, 512], F32))

        sems = {}
        for nm in (["d", "dc", "dp", "dp0", "dp1", "dw0", "dw1", "da0", "da1",
                    "db0", "db1", "dy0", "dy1", "dx0", "dx1", "dgt", "do", "do2"]
                   + ["dch0", "dch1"]
                   + ["dg%d" % k for k in range(8)]
                   + ["b" + e for e in ENGS] + ["B" + e for e in ENGS]):
            sems[nm] = es.enter_context(nc.semaphore(nm))

        s = Sch()

        PHOF = {"dbg_stat": "A", "dbg_x": "A", "dbg_xn": "A", "dbg_xnm": "A",
                "dbg_xnT": "B", "dbg_xnmT": "B", "dbg_hdn": "B", "dbg_xpxn": "B",
                "dbg_wd": "B", "dbg_win": "C", "dbg_dtraw": "C", "dbg_gate": "E",
                "dbg_xbc0": "D", "dbg_Bc": "D", "dbg_dt": "0", "dbg_cum": "0",
                "dbg_tok": "E", "dbg_Etok": "E", "dbg_Lam": "E", "dbg_xsT": "E",
                "dbg_WX": "E", "dbg_Gm": "E", "dbg_gml": "E", "dbg_y": "E",
                "dbg_S": "E", "dbg_ygn": "E", "dbg_fx": "F"}

        def tap(nm, src_ap):
            if dbg and nm in dbg_t and PHOF.get(nm, "?") in DBG_PH:
                s.wait("q", "bv")
                s.wait("q", "bs")
                s.wait("q", "bt")
                s.dma("q", lambda e, nm=nm, src_ap=src_ap: e.dma_start(
                    out=dbg_t[nm], in_=src_ap))

        AR, XP, XY, XBCB = sb["AR"], sb["XP"], sb["XY"], sb["XBCB"]
        HDN, GSL, WA, WB, WY = sb["HDN"], sb["GSL"], sb["WA"], sb["WB"], sb["WY"]
        ST, SBb = sb["S"], sb["Sb"]
        HIST0 = 2 * TQ
        IDb = sb["c_I"][:, :]

        def xbca(lo, hi):
            return XP[:, :].bitcast(BF16)[:, lo:hi]

        def pbh(i):
            return PB[:, i * 512:(i + 1) * 512].bitcast(BF16)

        for nm in ["c_tri", "c_I", "c_up", "c_ones2", "c_Dd"]:
            s.dma("p", lambda e, d=sb[nm], sr=ap[nm]: e.dma_start(out=d[:, :], in_=sr))
        s.dma("q", lambda e: e.dma_start(out=sb["c_If"][:, :], in_=ap["c_I"]),
              sem="dc")
        for nm in ["c_convb", "c_nw", "c_dtb", "c_gs", "c_us", "c_ds", "c_gnw"]:
            s.dma("q", lambda e, d=sb[nm], sr=ap[nm]: e.dma_start(out=d[:, :], in_=sr),
                  sem="dc")
        s.dma("q", lambda e: e.dma_start(out=sb["c_negA"][:, :], in_=ap["c_alog"]),
              sem="dc")
        s.wait("s", "dc")
        s.wait("s", "dp")
        s.op("s", lambda e: e.activation(out=sb["c_negA"][:, :], in_=sb["c_negA"][:, :],
                                         func=AF.Exp), inc="bs")
        s.wait("v", "bs")
        s.op("v", lambda e: e.tensor_scalar_mul(sb["c_negA"][:, :], sb["c_negA"][:, :], -1.0))
        s.op("v", lambda e: e.memset(sb["b_z"][:, :], 0.0))
        s.op("v", lambda e: e.memset(sb["c_bias"][:, 0:1], float(EPS)))
        s.op("v", lambda e: e.memset(sb["c_bias"][:, 1:2], 1.0))
        s.op("v", lambda e: e.memset(ST[:, :], 0.0))
        s.op("v", lambda e: e.memset(SBb[:, :], 0.0))
        s.op("v", lambda e: e.memset(XBCB[:, HIST0:HIST0 + OCH * (KCV - 1)], 0.0),
             inc="bv")

        CH = 2048
        wb_done = {}

        def plain_cast(src_ap, dst_ap, total_elems, src2d=False, dst2d=False):
            cols = total_elems // 128
            assert cols * 128 == total_elems
            c0_ = 0
            i = 0
            while c0_ < cols:
                w_ = min(CH, cols - c0_)
                slot = (i % 2) * 2048
                s.need("p", wb_done.get(slot))
                if src2d:
                    sview = src_ap[:, c0_:c0_ + w_]
                else:
                    sview = src_ap.rearrange("(p m) -> p m", p=128)[:, c0_:c0_ + w_]
                tok = s.dma("p", lambda e, sview=sview, slot=slot, w_=w_: e.dma_start(
                    out=WA[:, slot:slot + w_], in_=sview),
                    sem="dp0" if slot == 0 else "dp1")
                s.need("q", tok)
                if dst2d:
                    dview = dst_ap[:, c0_:c0_ + w_]
                else:
                    dview = dst_ap.rearrange("(p m) -> p m", p=128)[:, c0_:c0_ + w_]
                wb_done[slot] = s.dma("q", lambda e, dview=dview, slot=slot, w_=w_:
                                      e.dma_start(out=dview, in_=WA[:, slot:slot + w_]),
                                      sem="dw0" if slot == 0 else "dw1")
                c0_ += w_
                i += 1

        if limit >= 1:
            plain_cast(ap["w_in"], sc_in, NBLK * KT * 128 * 128)
        if limit >= 1:
            plain_cast(ap["w_g"], sc_g, D * FI)
            plain_cast(ap["w_u"], sc_u, D * FI)
            plain_cast(ap["c_convd"], sc_cv, 128 * OCH * KCV * 128, src2d=True, dst2d=True)
        for it in range(OKT):
            slot = (it % 2) * 2048
            s.need("p", wb_done.get(slot))
            tok = s.dma("p", lambda e, it=it, slot=slot: e.dma_start(
                out=WA[:, slot:slot + D], in_=ap["w_out"][it * 128:(it + 1) * 128, :]),
                sem="dp0" if slot == 0 else "dp1")
            s.need("v", tok)
            vt = s.op("v", lambda e, it=it, slot=slot: e.tensor_scalar_mul(
                WA[:, slot:slot + D], WA[:, slot:slot + D],
                sb["c_gnw"][:, it:it + 1]), inc="bv")
            s.need("q", vt)
            wb_done[slot] = s.dma("q", lambda e, it=it, slot=slot: e.dma_start(
                out=sc_o[it * 128:(it + 1) * 128, :], in_=WA[:, slot:slot + D]),
                sem="dw0" if slot == 0 else "dw1")
        wb_done[("d", 0)] = wb_done.get(0)
        wb_done[("d", 2048)] = wb_done.get(2048)
        gsl_done = {}
        half_free = {0: None, 1: None}
        for it in range(KT if limit >= 3 else 0):
            nfc = (FI + 2047) // 2048
            for fc in range(nfc):
                w0 = fc * 2048
                wlen = min(2048, FI - w0)
                slot = (fc % 2) * 2048
                s.need("p", wb_done.get(("d", slot)))
                tok = s.dma("p", lambda e, it=it, w0=w0, wlen=wlen, slot=slot: e.dma_start(
                    out=WA[:, slot:slot + wlen],
                    in_=ap["w_d"][it * 128:(it + 1) * 128, w0:w0 + wlen]),
                    sem="dp0" if slot == 0 else "dp1")
                s.need("v", tok)
                vt = s.op("v", lambda e, wlen=wlen, slot=slot, it=it: e.tensor_scalar_mul(
                    WA[:, slot:slot + wlen], WA[:, slot:slot + wlen],
                    sb["c_ds"][:, it:it + 1]), inc="bv")
                wb_done[("d", slot)] = vt
                s.need("t", vt)
                nblk = wlen // 128
                for bi in range(nblk):
                    half = bi % 2
                    gslot = (bi % 8) * 128
                    s.need("t", half_free[half])
                    tt_ = s.op("t", lambda e, slot=slot, bi=bi, half=half: e.transpose(
                        pbh(half)[:, (bi % 2) * 0:128] if False else pbh(half)[:, 0:128],
                        WA[:, slot + bi * 128:slot + (bi + 1) * 128],
                        identity=IDb), inc="bt")
                    s.need("v", tt_)
                    s.need("v", gsl_done.get(gslot))
                    vt2 = s.op("v", lambda e, half=half, gslot=gslot: e.tensor_copy(
                        GSL[:, gslot:gslot + 128], pbh(half)[:, 0:128]), inc="bv")
                    half_free[half] = vt2
                    s.need("q", vt2)
                    gsl_done[gslot] = s.dma("q", lambda e, it=it, w0=w0, bi=bi,
                                            gslot=gslot: e.dma_start(
                        out=sc_d[w0 + bi * 128:w0 + (bi + 1) * 128,
                                 it * 128:(it + 1) * 128],
                        in_=GSL[:, gslot:gslot + 128]), sem="dg%d" % (gslot // 128))
                wb_done[("d", slot)] = ("bt", s.cnt["bt"])
        s.barrier()

        for q_i in range(4 if limit >= 4 else 0):
            T0 = q_i * TQ

            st = sb["b_stat"]
            do_b = limit >= 5
            do_c = limit >= 6 and limit != 55
            do_d = limit >= 7 and limit != 55
            do_d0 = limit >= 8 and limit != 55
            do_e = limit >= 9 and limit != 55
            do_f = limit >= 10 or limit == 55
            tp_half_free = {0: None, 1: None}
            prev_tt_done = {}
            for tt in range(NTT):
                xoff = (tt % 2) * D
                s.need("q", prev_tt_done.get(tt % 2))
                xtok = s.dma("q", lambda e, tt=tt, xoff=xoff, T0=T0: e.dma_start(
                    out=XY[:, xoff:xoff + D],
                    in_=ap["x"][T0 + tt * 128:T0 + (tt + 1) * 128, :]),
                    sem="dx%d" % (tt % 2))
                s.need("v", xtok)
                xsl = XY[:, xoff:xoff + D]
                s.op("v", lambda e, xsl=xsl: e.reduce_sum(
                    st[:, 0:1], xsl, axis=mybir.AxisListType.X), inc="bv")
                s.selfedge("v")
                s.op("v", lambda e: e.tensor_scalar_mul(st[:, 2:3], st[:, 0:1], 1.0 / D))
                s.op("v", lambda e: e.tensor_scalar_mul(st[:, 3:4], st[:, 0:1],
                                                        -1.0 / D), inc="bv")
                s.wait("s", "bv")
                s.wait("s", "bt")
                s.op("s", lambda e, xsl=xsl: e.activation(
                    out=GSL[:, 0:D], in_=xsl, func=AF.Square,
                    bias=st[:, 3:4], accum_out=st[:, 4:5]), inc="bs")
                s.wait("v", "bs")
                s.op("v", lambda e: e.tensor_scalar_mul(st[:, 5:6], st[:, 4:5], 1.0 / D),
                     inc="bv")
                s.wait("s", "bv")
                s.op("s", lambda e: e.activation(out=st[:, 6:7], in_=st[:, 5:6],
                                                 func=AF.Ln, bias=float(EPS)))
                s.op("s", lambda e: e.activation(out=st[:, 7:8], in_=st[:, 6:7],
                                                 func=AF.Exp, scale=-0.5), inc="bs")
                s.wait("v", "bs")
                s.op("v", lambda e: e.tensor_scalar(
                    out=st[:, 8:9], in0=st[:, 5:6], scalar1=st[:, 7:8],
                    scalar2=st[:, 7:8], op0=ALU.mult, op1=ALU.mult), inc="bv")
                s.wait("s", "bv")
                s.op("s", lambda e: e.activation(out=st[:, 9:10], in_=st[:, 8:9],
                                                 func=AF.Ln, bias=float(EPS)))
                s.op("s", lambda e: e.activation(out=st[:, 10:11], in_=st[:, 9:10],
                                                 func=AF.Exp, scale=-0.5), inc="bs")
                s.wait("v", "bs")
                s.op("v", lambda e: e.tensor_scalar(
                    out=st[:, 11:12], in0=st[:, 7:8], scalar1=st[:, 10:11],
                    scalar2=None, op0=ALU.mult))
                s.op("v", lambda e: e.tensor_scalar_add(st[:, 12:13], st[:, 7:8], 1.0))
                s.op("v", lambda e: e.tensor_scalar(
                    out=st[:, 13:14], in0=st[:, 2:3], scalar1=st[:, 7:8],
                    scalar2=None, op0=ALU.mult))
                s.selfedge("v")
                s.wait("v", "bt")
                s.op("v", lambda e, xsl=xsl: e.tensor_scalar(
                    out=GSL[:, 0:D], in0=xsl, scalar1=st[:, 2:3], scalar2=st[:, 7:8],
                    op0=ALU.subtract, op1=ALU.mult))
                s.op("v", lambda e, xsl=xsl: e.tensor_scalar(
                    out=GSL[:, D:2 * D], in0=xsl, scalar1=st[:, 2:3],
                    scalar2=st[:, 11:12], op0=ALU.subtract, op1=ALU.mult))
                vlast = s.op("v", lambda e, xsl=xsl, tt=tt: e.tensor_scalar(
                    out=XP[:, tt * D:(tt + 1) * D], in0=xsl, scalar1=st[:, 12:13],
                    scalar2=st[:, 13:14], op0=ALU.mult, op1=ALU.subtract), inc="bv")
                prev_tt_done[tt % 2] = vlast
                if q_i == DBG_Q and tt == 0:
                    tap("dbg_stat", sb["b_stat"][:, :])
                    tap("dbg_x", XY[:, 0:D])
                    tap("dbg_xn", GSL[:, 0:D])
                    tap("dbg_xnm", GSL[:, D:2 * D])
                    tap("dbg_xpxn", XP[:, 0:D]) if False else None
                if limit == 43:
                    continue
                s.need("t", vlast)
                for blk in range(2 * KT):
                    ht, which = blk // 2, blk % 2
                    half = blk % 2
                    s.need("t", tp_half_free[half])
                    src = GSL[:, which * D + ht * 128: which * D + (ht + 1) * 128]
                    ttok = s.op("t", lambda e, src=src, half=half: e.transpose(
                        pbh(half)[:, 0:128], src, identity=IDb), inc="bt")
                    dst_off = (o_xnT if which == 0 else o_xnmT) + ht * TQ + tt * 128
                    if which == 0:
                        s.need("s", ttok)
                        ev = s.op("s", lambda e, half=half, dst_off=dst_off: e.copy(
                            AR[:, dst_off:dst_off + 128], pbh(half)[:, 0:128]),
                            inc="bs")
                    else:
                        s.need("v", ttok)
                        ev = s.op("v", lambda e, half=half, dst_off=dst_off, ht=ht:
                                  e.tensor_scalar_mul(
                                      AR[:, dst_off:dst_off + 128],
                                      pbh(half)[:, 0:128],
                                      sb["c_nw"][:, ht:ht + 1]), inc="bv")
                    tp_half_free[half] = ev
                if "a" in HB:
                    s.barrier()
            s.barrier()

            wa_free = {}
            if not do_b:
                continue
            b_sub = limit if limit in (51, 52, 53) else (limit - 470 if limit >= 520 else 99)
            wbslot_free = {}
            pbg_free = None
            pbu_free = None
            for fg in range(NFG):
                for icl in range(FGRP):
                    ic = fg * FGRP + icl
                    sA = (ic % 2) * 2048
                    s.need("q", wa_free.get(sA))
                    gt = s.dma("q", lambda e, ic=ic, sA=sA: e.dma_start(
                        out=WA[:, sA:sA + KT * 128],
                        in_=sc_g.rearrange("(n p m) -> n p m", p=128, m=KT * 128)[ic]),
                        sem="da0" if sA == 0 else "da1")
                    s.need("q", wbslot_free.get(sA))
                    ut = s.dma("q", lambda e, ic=ic, sA=sA: e.dma_start(
                        out=WB[:, sA:sA + KT * 128],
                        in_=sc_u.rearrange("(n p m) -> n p m", p=128, m=KT * 128)[ic]),
                        sem="db0" if sA == 0 else "db1")
                    s.need("t", gt)
                    s.need("t", ut)
                    if "y" in K_HBX():
                        s.wait("t", "d")
                    s.need("t", pbg_free)
                    s.need("t", pbu_free)
                    for kt in range(KT):
                        lastg = s.op("t", lambda e, kt=kt, sA=sA: e.matmul(
                            PB[:, 0:TQ],
                            WA[:, sA + kt * 128:sA + (kt + 1) * 128],
                            AR[:, o_xnT + kt * TQ:o_xnT + (kt + 1) * TQ],
                            start=(kt == 0), stop=(kt == KT - 1),
                            skip_group_check=True),
                            inc=("bt" if kt == KT - 1 else None))
                    for kt in range(KT):
                        tok = s.op("t", lambda e, kt=kt, sA=sA: e.matmul(
                            PB[:, 512:512 + TQ],
                            WB[:, sA + kt * 128:sA + (kt + 1) * 128],
                            AR[:, o_xnT + kt * TQ:o_xnT + (kt + 1) * TQ],
                            start=(kt == 0), stop=(kt == KT - 1),
                            skip_group_check=True),
                            inc=("bt" if kt == KT - 1 else None))
                    wa_free[sA] = tok
                    wbslot_free[sA] = tok
                    if b_sub == 51:
                        pbg_free = tok
                        pbu_free = tok
                        continue
                    s.need("s", lastg)
                    if "x" in K_HBX():
                        s.need("s", tok)
                    s.need("s", pbu_free)
                    if b_sub == 61:
                        sg = s.op("s", lambda e: e.copy(
                            GSL[:, 0:TQ], PB[:, 0:TQ]), inc="bs")
                    elif b_sub == 62:
                        sg = s.op("s", lambda e: e.activation(
                            out=GSL[:, 0:TQ], in_=PB[:, 0:TQ], func=AF.Silu), inc="bs")
                    else:
                        sg = s.op("s", lambda e, ic=ic: e.activation(
                            out=GSL[:, 0:TQ], in_=PB[:, 0:TQ], func=AF.Silu,
                            scale=sb["c_gs"][:, ic:ic + 1]), inc="bs")
                    pbg_free = sg
                    s.need("v", sg)
                    s.need("v", tok)
                    if b_sub == 63:
                        hv = s.op("v", lambda e, icl=icl: e.tensor_mul(
                            HDN[:, icl * TQ:(icl + 1) * TQ], PB[:, 512:512 + TQ],
                            GSL[:, 0:TQ]), inc="bv")
                    elif b_sub == 64:
                        hv = s.op("v", lambda e, icl=icl: e.tensor_copy(
                            HDN[:, icl * TQ:(icl + 1) * TQ], PB[:, 512:512 + TQ]),
                            inc="bv")
                    else:
                        hv = s.op("v", lambda e, ic=ic, icl=icl: e.scalar_tensor_tensor(
                            out=HDN[:, icl * TQ:(icl + 1) * TQ], in0=PB[:, 512:512 + TQ],
                            scalar=sb["c_us"][:, ic:ic + 1], in1=GSL[:, 0:TQ],
                            op0=ALU.mult, op1=ALU.mult), inc="bv")
                    pbu_free = hv
                    if "b" in HB:
                        s.barrier()
                if b_sub in (51, 52, 61, 62, 63, 64):
                    continue
                hdn_ready = hv
                wbslot_free[("dn", 0)] = wbslot_free.get(0)
                wbslot_free[("dn", 2048)] = wbslot_free.get(2048)
                pa_free = None
                reg2 = [(PB, 0, 512), (PB, 512, 512), (PC, 0, 512), (PD, 0, 512)]
                for tp in range(0, NTT, 2):
                    for icl in range(FGRP):
                        ic = fg * FGRP + icl
                        sB = (icl % 2) * 2048
                        s.need("q", wbslot_free.get(("dn", sB)))
                        dt_ = s.dma("q", lambda e, ic=ic, sB=sB: e.dma_start(
                            out=WB[:, sB:sB + D],
                            in_=sc_d[ic * 128:(ic + 1) * 128, :]),
                            sem="db0" if sB == 0 else "db1")
                        s.need("t", dt_)
                        if q_i == DBG_Q and fg == 0 and tp == 0 and icl == 0:
                            s.need("q", dt_)
                            tap("dbg_wd", WB[:, sB:sB + D])
                        if icl == 0:
                            s.need("t", hdn_ready)
                            s.need("t", pa_free)
                        ocl = [(i * 512, min(512, D - i * 512))
                               for i in range((D + 511) // 512)]
                        for oi, (o0, olen) in enumerate(ocl):
                            s.op("t", lambda e, icl=icl, tp=tp, sB=sB, o0=o0,
                                 olen=olen: e.matmul(
                                     PA[:, o0:o0 + olen],
                                     HDN[:, icl * TQ + tp * 128:
                                         icl * TQ + (tp + 1) * 128],
                                     WB[:, sB + o0:sB + o0 + olen],
                                     start=(icl == 0), stop=(icl == FGRP - 1),
                                     skip_group_check=True))
                        for oi, (o0, olen) in enumerate(ocl):
                            pt, po, plen = reg2[oi]
                            tok = s.op("t", lambda e, icl=icl, tp=tp, sB=sB, o0=o0,
                                       olen=olen, pt=pt, po=po: e.matmul(
                                           pt[:, po:po + olen],
                                           HDN[:, icl * TQ + (tp + 1) * 128:
                                               icl * TQ + (tp + 2) * 128],
                                           WB[:, sB + o0:sB + o0 + olen],
                                           start=(icl == 0), stop=(icl == FGRP - 1),
                                           skip_group_check=True),
                                       inc=("bt" if oi == len(ocl) - 1 else None))
                        wbslot_free[("dn", sB)] = tok
                    s.need("v", tok)
                    s.op("v", lambda e, tp=tp: e.scalar_tensor_tensor(
                        out=XP[:, tp * D:(tp + 1) * D], in0=PA[:, 0:D], scalar=1.0,
                        in1=XP[:, tp * D:(tp + 1) * D], op0=ALU.mult, op1=ALU.add))
                    s.selfedge("v")
                    for oi, (o0, olen) in enumerate(ocl):
                        pt, po, plen = reg2[oi]
                        pa_free = s.op("v", lambda e, tp=tp, o0=o0, olen=olen, pt=pt,
                                       po=po: e.scalar_tensor_tensor(
                                           out=XP[:, (tp + 1) * D + o0:
                                                  (tp + 1) * D + o0 + olen],
                                           in0=pt[:, po:po + olen], scalar=1.0,
                                           in1=XP[:, (tp + 1) * D + o0:
                                                  (tp + 1) * D + o0 + olen],
                                           op0=ALU.mult, op1=ALU.add), inc="bv")
                        s.selfedge("v")
                    if "d" in HB:
                        s.barrier()
                pbg_free = pa_free
                pbu_free = pa_free
                wbslot_free[0] = wbslot_free.get(("dn", 0), wbslot_free.get(0))
                wbslot_free[2048] = wbslot_free.get(("dn", 2048),
                                                    wbslot_free.get(2048))
            if q_i == DBG_Q:
                tap("dbg_xnT", AR[:, o_xnT:o_xnT + KT * TQ])
                tap("dbg_xnmT", AR[:, o_xnmT:o_xnmT + KT * TQ])
                tap("dbg_hdn", HDN[:, :])
                tap("dbg_xpxn", XP[:, 0:D])
            if b_sub == 53:
                s.barrier()
                continue
            s.wait("q", "bv")
            for tt in range(NTT):
                s.dma("q", lambda e, tt=tt: e.dma_start(
                    out=sc_xpxn[tt * 128:(tt + 1) * 128, :],
                    in_=XP[:, tt * D:(tt + 1) * D]))
            s.barrier()

            if not do_c:
                continue
            blocks = ([("xbc", CT + i) for i in range(OCH)] +
                      [("dt", CT + OCH)])
            wa_free = {}
            pb_free = {0: None, 1: None}
            pc_free = None
            pa_gate_free = None
            gate_store_tok = {}
            pbreg = 0
            for bi_, (sect, oc) in enumerate(blocks):
                sA = (bi_ % 2) * 2048
                s.need("q", wa_free.get(sA))
                wt = s.dma("q", lambda e, oc=oc, sA=sA: e.dma_start(
                    out=WA[:, sA:sA + KT * 128],
                    in_=sc_in.rearrange("(n p m) -> n p m", p=128, m=KT * 128)[oc]),
                    sem="da0" if sA == 0 else "da1")
                s.need("t", wt)
                if q_i == DBG_Q and bi_ == 0:
                    s.need("q", wt)
                    tap("dbg_win", WA[:, sA:sA + KT * 128])
                if sect == "xbc":
                    m = oc - CT
                    reg = pbreg
                    pbreg ^= 1
                    s.need("t", pb_free[reg])
                    for kt in range(KT):
                        tok = s.op("t", lambda e, kt=kt, sA=sA, reg=reg: e.matmul(
                            PB[:, reg * 512:reg * 512 + TQ],
                            WA[:, sA + kt * 128:sA + (kt + 1) * 128],
                            AR[:, o_xnmT + kt * TQ:o_xnmT + (kt + 1) * TQ],
                            start=(kt == 0), stop=(kt == KT - 1),
                            skip_group_check=True),
                            inc=("bt" if kt == KT - 1 else None))
                    wa_free[sA] = tok
                    s.need("v", tok)
                    if m < CT:
                        ddst = xbca(m * TQ, (m + 1) * TQ)
                    else:
                        ddst = XBCB[:, (m - CT) * TQ:(m - CT + 1) * TQ]
                    pb_free[reg] = s.op("v", lambda e, reg=reg, ddst=ddst: e.tensor_copy(
                        ddst, PB[:, reg * 512:reg * 512 + TQ]), inc="bv")
                elif sect == "dt":
                    s.need("t", pc_free)
                    for kt in range(KT):
                        tok = s.op("t", lambda e, kt=kt, sA=sA: e.matmul(
                            PC[0:H, 0:TQ],
                            WA[:, sA + kt * 128:sA + kt * 128 + H],
                            AR[:, o_xnmT + kt * TQ:o_xnmT + (kt + 1) * TQ],
                            start=(kt == 0), stop=(kt == KT - 1),
                            skip_group_check=True),
                            inc=("bt" if kt == KT - 1 else None))
                    wa_free[sA] = tok
                    s.need("v", tok)
                    pc_free = s.op("v", lambda e: e.tensor_copy(
                        sb["b_dtraw"][:, :], PC[0:H, 0:TQ]), inc="bv")
                else:
                    raise AssertionError("unused branch")
            KTL = min(4, KT)
            ld_i = len(blocks)
            gblk = 128 * KT * 512
            for gg in range(G2 // 512):
                gtoks = []
                for ktq in range(KT // KTL):
                    sA = (ld_i % 2) * 2048
                    ld_i += 1
                    s.need("q", wa_free.get(sA))
                    wt = s.dma("q", lambda e, gg=gg, ktq=ktq, sA=sA: e.dma_start(
                        out=WA[:, sA:sA + KTL * 512],
                        in_=sc_in[gg * gblk:(gg + 1) * gblk]
                        .rearrange("(p m) -> p m", p=128)
                        [:, ktq * KTL * 512:(ktq + 1) * KTL * 512]),
                        sem="da0" if sA == 0 else "da1")
                    s.need("t", wt)
                    if ktq == 0:
                        s.need("t", pa_gate_free)
                    for kt4 in range(KTL):
                        kt = ktq * KTL + kt4
                        for tt in range(NTT):
                            tok = s.op("t", lambda e, kt=kt, kt4=kt4, tt=tt, sA=sA:
                                       e.matmul(
                                           PA[:, tt * 512:(tt + 1) * 512],
                                           AR[:, o_xnmT + kt * TQ + tt * 128:
                                              o_xnmT + kt * TQ + (tt + 1) * 128],
                                           WA[:, sA + kt4 * 512:sA + (kt4 + 1) * 512],
                                           start=(kt == 0), stop=(kt == KT - 1),
                                           skip_group_check=True),
                                       inc=("bt" if (kt4 == KTL - 1 and tt == NTT - 1)
                                            else None))
                    wa_free[sA] = tok
                s.need("s", tok)
                for tt in range(NTT):
                    s.need("s", gate_store_tok.get(tt))
                evs = []
                for tt in range(NTT):
                    evs.append(s.op("s", lambda e, tt=tt: e.activation(
                        out=GSL[:, tt * 512:(tt + 1) * 512],
                        in_=PA[:, tt * 512:(tt + 1) * 512],
                        func=AF.Silu), inc="bs"))
                pa_gate_free = evs[-1]
                s.need("q", evs[-1])
                g0 = gg * 512
                for tt in range(NTT):
                    gate_store_tok[tt] = s.dma(
                        "q", lambda e, tt=tt, g0=g0: e.dma_start(
                            out=sc_gate[tt * 128:(tt + 1) * 128, g0:g0 + 512],
                            in_=GSL[:, tt * 512:(tt + 1) * 512]),
                        sem="dg%d" % tt)
            if q_i == DBG_Q:
                tap("dbg_dtraw", sb["b_dtraw"][:, :])
            s.barrier()

            if not do_d:
                continue
            wb_free = {}
            pb_free = {0: None, 1: None}
            pbreg = 0
            for m in range(OCH):
                sA = (m % 2) * 1024
                s.need("q", wb_free.get(sA))
                wt = s.dma("q", lambda e, m=m, sA=sA: e.dma_start(
                    out=WB[:, sA:sA + KCV * 128],
                    in_=sc_cv[:, m * KCV * 128:(m + 1) * KCV * 128]),
                    sem="db0" if sA == 0 else "db1")
                if m < CT:
                    srcmain = lambda lo, hi, m=m: xbca(m * TQ + lo, m * TQ + hi)
                else:
                    srcmain = lambda lo, hi, mb=m - CT: XBCB[:, mb * TQ + lo:mb * TQ + hi]
                hist = XBCB[:, HIST0 + m * (KCV - 1):HIST0 + (m + 1) * (KCV - 1)]
                reg = pbreg
                pbreg ^= 1
                s.need("t", wt)
                s.need("t", pb_free[reg])
                first = True
                for k in range(KCV):
                    nh = (KCV - 1) - k
                    if nh > 0:
                        s.op("t", lambda e, k=k, nh=nh, sA=sA, reg=reg, hist=hist,
                             first=first: e.matmul(
                                 PB[:, reg * 512:reg * 512 + nh],
                                 WB[:, sA + k * 128:sA + (k + 1) * 128],
                                 hist[:, k:k + nh],
                                 start=first, stop=False, skip_group_check=True))
                        first = False
                    tok = s.op("t", lambda e, k=k, nh=nh, sA=sA, reg=reg,
                               srcmain=srcmain, first=first: e.matmul(
                                   PB[:, reg * 512 + nh:reg * 512 + TQ],
                                   WB[:, sA + k * 128:sA + (k + 1) * 128],
                                   srcmain(0, TQ - nh),
                                   start=first, stop=(k == KCV - 1),
                                   skip_group_check=True),
                               inc=("bt" if k == KCV - 1 else None))
                    first = False
                wb_free[sA] = tok
                s.need("v", tok)
                hv = s.op("v", lambda e, hist=hist, srcmain=srcmain: e.tensor_copy(
                    hist[:, :], srcmain(TQ - (KCV - 1), TQ)), inc="bv")
                s.need("s", hv)
                pb_free[reg] = s.op("s", lambda e, m=m, reg=reg, srcmain=srcmain:
                                    e.activation(
                                        out=srcmain(0, TQ),
                                        in_=PB[:, reg * 512:reg * 512 + TQ],
                                        func=AF.Silu,
                                        bias=sb["c_convb"][:, m:m + 1]), inc="bs")
            if q_i == DBG_Q:
                tap("dbg_xbc0", xbca(0, TQ))
                tap("dbg_Bc", XBCB[:, 0:TQ])
            s.barrier()

            if not do_d0:
                continue
            dtb = sb["c_dtb"][:, 0:1]
            B_ = {k: sb[k] for k in ["b_dtraw", "b_dt", "b_cum", "b_t1", "b_dtA",
                                     "b_dtAn", "b_lndt", "b_wdt", "b_chi", "b_clo"]}
            s.op("s", lambda e: e.activation(out=B_["b_t1"][:, :], in_=B_["b_dtraw"][:, :],
                                             func=AF.Abs, bias=dtb))
            s.op("s", lambda e: e.activation(out=B_["b_t1"][:, :], in_=B_["b_t1"][:, :],
                                             func=AF.Exp, scale=-1.0))
            s.op("s", lambda e: e.activation(out=B_["b_t1"][:, :], in_=B_["b_t1"][:, :],
                                             func=AF.Ln, bias=1.0))
            s.op("s", lambda e: e.activation(out=B_["b_dt"][:, :], in_=B_["b_dtraw"][:, :],
                                             func=AF.Relu, bias=dtb), inc="bs")
            s.wait("v", "bs")
            s.op("v", lambda e: e.tensor_add(B_["b_dt"][:, :], B_["b_dt"][:, :],
                                             B_["b_t1"][:, :]))
            s.selfedge("v")
            s.op("v", lambda e: e.tensor_scalar_mul(B_["b_dtA"][:, :], B_["b_dt"][:, :],
                                                    sb["c_negA"][:, 0:1]))
            s.selfedge("v")
            s.op("v", lambda e: e.tensor_scalar_mul(B_["b_dtAn"][:, :], B_["b_dtA"][:, :],
                                                    -1.0))
            for ch in range(NCHQ):
                s.op("v", lambda e, ch=ch: e.tensor_tensor_scan(
                    out=B_["b_cum"][:, ch * Q:(ch + 1) * Q],
                    data0=B_["b_dtA"][:, ch * Q:(ch + 1) * Q],
                    data1=sb["b_z"][:, :], initial=0.0,
                    op0=ALU.add, op1=ALU.add))
            s.selfedge("v")
            s.op("v", lambda e: e.tensor_copy(B_["b_chi"][:, :], B_["b_cum"][:, :]))
            for ch in range(NCHQ):
                s.op("v", lambda e, ch=ch: e.tensor_copy(
                    sb["b_cendh"][:, ch:ch + 1],
                    B_["b_cum"][:, (ch + 1) * Q - 1:(ch + 1) * Q]))
            s.selfedge("v")
            s.op("v", lambda e: e.tensor_sub(B_["b_clo"][:, :], B_["b_cum"][:, :],
                                             B_["b_chi"][:, :]))
            for ch in range(NCHQ):
                s.op("v", lambda e, ch=ch: e.tensor_sub(
                    sb["b_cscr"][:, ch:ch + 1],
                    B_["b_cum"][:, (ch + 1) * Q - 1:(ch + 1) * Q],
                    sb["b_cendh"][:, ch:ch + 1]))
            s.selfedge("v")
            s.op("v", lambda e: e.tensor_copy(sb["b_cendl"][:, :],
                                              sb["b_cscr"][:, 0:NCHQ]))
            for ch in range(NCHQ):
                s.op("v", lambda e, ch=ch: e.tensor_scalar_mul(
                    sb["b_cscr"][:, NCHQ + ch:NCHQ + ch + 1],
                    B_["b_cum"][:, (ch + 1) * Q - 1:(ch + 1) * Q], 1.0))
            vb = s.op("v", lambda e: e.tensor_copy(
                sb["b_cscr"][:, 0:NCHQ], sb["b_cscr"][:, 0:NCHQ]), inc="bv")
            s.wait("s", "bv")
            for ch in range(NCHQ):
                s.op("s", lambda e, ch=ch: e.activation(
                    out=B_["b_wdt"][:, ch * Q:(ch + 1) * Q],
                    in_=B_["b_cum"][:, ch * Q:(ch + 1) * Q], func=AF.Exp, scale=-1.0,
                    bias=sb["b_cscr"][:, NCHQ + ch:NCHQ + ch + 1]))
            s.op("s", lambda e: e.activation(out=B_["b_t1"][:, :], in_=B_["b_dt"][:, :],
                                             func=AF.Ln), inc="bs")
            s.wait("v", "bs")
            s.op("v", lambda e: e.tensor_scalar_max(B_["b_lndt"][:, :], B_["b_t1"][:, :],
                                                    LNDT_CLAMP))
            s.op("v", lambda e: e.tensor_mul(B_["b_wdt"][:, :], B_["b_wdt"][:, :],
                                             B_["b_dt"][:, :]), inc="bv")
            s.selfedge("v")
            s.wait("q", "bv")
            for hl, buf in ((0, B_["b_chi"]), (1, B_["b_clo"])):
                for ch in range(NCHQ):
                    s.dma("q", lambda e, hl=hl, buf=buf, ch=ch: e.dma_start(
                        out=sc_cumhl[hl:hl + 1, ch * H * Q:(ch + 1) * H * Q]
                        .rearrange("a (h i) -> (a h) i", h=H),
                        in_=buf[:, ch * Q:(ch + 1) * Q]))
            for hl, buf in ((0, sb["b_cendh"]), (1, sb["b_cendl"])):
                s.dma("q", lambda e, hl=hl, buf=buf: e.dma_start(
                    out=bass.AP(sc_cend.tensor, hl * NCHQ * H, [[1, H], [H, NCHQ]]),
                    in_=buf[:, :], allow_slow_non_contiguous=True))
            s.wait("q", "d")
            s.dma("q", lambda e: e.dma_start(out=sb["b_cendhl"][:, :], in_=sc_cend[:, :]))
            if q_i == DBG_Q:
                tap("dbg_dt", sb["b_dt"][:, :])
                tap("dbg_cum", sb["b_cum"][:, :])
            s.barrier()

            if not do_e:
                continue
            gsl_dma_done = None
            y_done = None
            prev_lam = None
            prev_pbh = {0: None, 1: None}
            gsl_slot_free = {}
            prev_epi_v = None
            cumhl_slot_free = {0: None, 1: None}
            agi_ctr = 0
            for ch in range(NCHQ):
                prev_ydone = y_done
                c0 = ch * Q
                s.need("q", prev_epi_v)
                if gsl_dma_done is not None:
                    for k in range(8):
                        if s.cnt["dg%d" % k]:
                            s.wait("q", "dg%d" % k)
                gtok = s.dma("q", lambda e, c0=c0: e.dma_start(
                    out=GSL[:, 0:G2], in_=sc_gate[c0:c0 + 128, :]), sem="dgt")
                s.need("t", prev_pbh[0])
                btokv = None
                for i_, nm in enumerate(("b_dtAn", "b_lndt", "b_wdt")):
                    s.need("t", btokv)
                    tk = s.op("t", lambda e, nm=nm, c0=c0: e.transpose(
                        pbh(0)[:, 0:H], sb[nm][:, c0:c0 + Q],
                        identity=sb["c_I"][0:H, 0:H]), inc="bt")
                    s.need("v", tk)
                    btokv = s.op("v", lambda e, i_=i_: e.tensor_copy(
                        sb["b_tok"][:, i_ * H:(i_ + 1) * H], pbh(0)[:, 0:H]), inc="bv")
                s.need("t", prev_lam)
                ctok = s.op("t", lambda e, c0=c0: e.transpose(
                    PD[:, 0:H], sb["b_cum"][:, c0:c0 + Q],
                    identity=sb["c_If"][0:H, 0:H]), inc="bt")
                s.need("s", ctok)
                s.need("s", prev_ydone)
                etok = s.op("s", lambda e: e.activation(
                    out=sb["b_Etok"][:, :], in_=PD[:, 0:H], func=AF.Exp), inc="bs")
                s.need("t", etok)
                ltok = s.op("t", lambda e, ch=ch: e.matmul(
                    PD[:, 128:128 + H], sb["c_ones2"][:, :],
                    sb["b_cendhl"][:, ch * H:(ch + 1) * H],
                    start=True, stop=True, skip_group_check=True), inc="bt")
                s.need("s", ltok)
                lamtok = s.op("s", lambda e: e.activation(
                    out=sb["b_Lam"][:, :], in_=PD[:, 128:128 + H], func=AF.Exp),
                    inc="bs")
                prev_lam = lamtok
                tp_half_free = {0: btokv, 1: prev_pbh[1]}
                for m in range(CT):
                    half = m % 2
                    s.need("t", tp_half_free[half])
                    ttok = s.op("t", lambda e, m=m, half=half, c0=c0: e.transpose(
                        pbh(half)[:, 0:128], xbca(m * TQ + c0, m * TQ + c0 + 128),
                        identity=IDb), inc="bt")
                    s.need("v", ttok)
                    tp_half_free[half] = s.op(
                        "v", lambda e, m=m, half=half: e.tensor_copy(
                            AR[:, o_xsT + m * 128:o_xsT + (m + 1) * 128],
                            pbh(half)[:, 0:128]), inc="bv")
                half = CT % 2
                s.need("t", tp_half_free[half])
                ttok = s.op("t", lambda e, c0=c0, half=half: e.transpose(
                    pbh(half)[:, 0:128], XBCB[:, c0:c0 + 128], identity=IDb), inc="bt")
                s.need("v", ttok)
                btv = s.op("v", lambda e, half=half: e.tensor_copy(
                    sb["b_BT"][:, :], pbh(half)[:, 0:128]), inc="bv")
                tp_half_free[half] = btv
                xsT_ready = ("bv", s.cnt["bv"])
                s.need("p", xsT_ready)
                wxtok = s.op("p", lambda e: e.tensor_tensor(
                    AR[:, o_wx:o_wx + G2].rearrange("p (h f) -> p h f", h=H),
                    AR[:, o_xsT:o_xsT + G2].rearrange("p (h f) -> p h f", h=H),
                    sb["b_tok"][:, 2 * H:3 * H, None].broadcast_to([128, H, PH]),
                    op=ALU.mult), inc="bp")
                if q_i == DBG_Q and ch == 0:
                    tap("dbg_tok", sb["b_tok"][:, :])
                    tap("dbg_Etok", sb["b_Etok"][:, :])
                    tap("dbg_Lam", sb["b_Lam"][:, :])
                    tap("dbg_xsT", AR[:, o_xsT:o_xsT + G2])
                    s.wait("q", "bp")
                    tap("dbg_WX", AR[:, o_wx:o_wx + G2])
                gmt = s.op("t", lambda e, c0=c0: e.matmul(
                    PC[:, 0:128], XBCB[:, c0:c0 + 128],
                    XBCB[:, TQ + c0:TQ + c0 + 128],
                    start=True, stop=True, skip_group_check=True), inc="bt")
                s.need("v", gmt)
                gmv = s.op("v", lambda e: e.tensor_mul(
                    sb["b_Gm"][:, :], PC[:, 0:128], sb["c_tri"][:, :]), inc="bv")

                if "h" in HB:
                    s.barrier()
                exp_done = None
                gml_done = None
                for ag in range(NAG):
                    h0 = ag * AGH
                    cslot = agi_ctr % 2
                    agi_ctr += 1
                    s.need("q", cumhl_slot_free[cslot])
                    chl = s.dma("q", lambda e, ch=ch, h0=h0, cslot=cslot: e.dma_start(
                        out=sb["b_cumhl"][:, cslot * AGH * Q:(cslot + 1) * AGH * Q],
                        in_=sc_cumhl[:, (ch * H + h0) * Q:(ch * H + h0 + AGH) * Q]),
                        sem="dch%d" % cslot)
                    s.need("t", exp_done)
                    s.need("t", chl)
                    for hh in range(AGH):
                        s.op("t", lambda e, hh=hh, cslot=cslot: e.matmul(
                            PA[:, hh * 128:(hh + 1) * 128], sb["c_ones2"][:, :],
                            sb["b_cumhl"][:, (cslot * AGH + hh) * Q:
                                          (cslot * AGH + hh + 1) * Q],
                            start=(hh % 4 == 0), stop=False, skip_group_check=True))
                    for hh in range(AGH):
                        h = h0 + hh
                        s.op("t", lambda e, h=h, hh=hh: e.matmul(
                            PA[:, hh * 128:(hh + 1) * 128], sb["c_tri"][:, :],
                            sb["b_tok"][:, None, h:h + 1].broadcast_to([128, 128, 1]),
                            start=False, stop=False, skip_group_check=True))
                    for hh in range(AGH):
                        h = h0 + hh
                        s.op("t", lambda e, h=h, hh=hh: e.matmul(
                            PA[:, hh * 128:(hh + 1) * 128], sb["c_I"][:, :],
                            sb["b_tok"][:, None, H + h:H + h + 1]
                            .broadcast_to([128, 128, 1]),
                            start=False, stop=False, skip_group_check=True))
                    nmi = AGH * 128 // 512
                    for mi in range(nmi):
                        tok = s.op("t", lambda e, mi=mi: e.matmul(
                            PA[:, mi * 512:(mi + 1) * 512], sb["c_I"][:, :],
                            sb["c_up"][:, None, :].broadcast_to([128, AGH, 128])
                            [:, mi * 4:(mi + 1) * 4, :],
                            start=False, stop=True, skip_group_check=True),
                            inc=("bt" if mi == nmi - 1 else None))
                    cumhl_slot_free[cslot] = tok
                    s.need("s", tok)
                    exp_done = s.op("s", lambda e: e.activation(
                        out=AR[:, o_expl:o_expl + AGH * 128], in_=PA[:, 0:AGH * 128],
                        func=AF.Exp), inc="bs")
                    s.need("v", exp_done)
                    s.need("v", gmv)
                    gml_done = s.op("v", lambda e, h0=h0: e.tensor_tensor(
                        out=AR[:, o_gml + h0 * 128:o_gml + (h0 + AGH) * 128]
                        .rearrange("p (h f) -> p h f", h=AGH),
                        in0=AR[:, o_expl:o_expl + AGH * 128]
                        .rearrange("p (h f) -> p h f", h=AGH),
                        in1=sb["b_Gm"][:, None, :].broadcast_to([128, AGH, 128]),
                        op=ALU.mult), inc="bv")
                    exp_done = gml_done
                if "g" in HB:
                    s.barrier()
                for yg in range(NYG):
                    h0 = yg * YGH
                    s.need("t", y_done)
                    s.need("t", gml_done)
                    itok = s.op("t", lambda e, yg=yg, c0=c0: e.matmul(
                        PB[:, 0:512], XBCB[:, TQ + c0:TQ + c0 + 128],
                        SBb[:, yg * YGH * PH:(yg + 1) * YGH * PH],
                        start=True, stop=True, skip_group_check=True), inc="bt")
                    for hh in range(YGH):
                        h = h0 + hh
                        s.op("t", lambda e, h=h, hh=hh: e.matmul(
                            PB[:, 512 + hh * PH:512 + (hh + 1) * PH],
                            AR[:, o_gml + h * 128:o_gml + (h + 1) * 128],
                            AR[:, o_xsT + h * PH:o_xsT + (h + 1) * PH],
                            start=(hh == 0), stop=False, skip_group_check=True))
                    nmt = YGH * PH // 128
                    for mm in range(nmt):
                        mt = yg * nmt + mm
                        tok = s.op("t", lambda e, mt=mt, mm=mm, c0=c0: e.matmul(
                            PB[:, 512 + mm * 128:512 + (mm + 1) * 128],
                            xbca(mt * TQ + c0, mt * TQ + c0 + 128),
                            sb["c_Dd"][:, mt * 128:(mt + 1) * 128],
                            start=False, stop=(mm == nmt - 1),
                            skip_group_check=True),
                            inc=("bt" if mm == nmt - 1 else None))
                    s.need("v", tok)
                    s.op("v", lambda e, yg=yg, h0=h0: e.tensor_tensor(
                        out=XY[:, yg * YGH * PH:(yg + 1) * YGH * PH]
                        .rearrange("p (h f) -> p h f", h=YGH),
                        in0=PB[:, 0:YGH * PH].rearrange("p (h f) -> p h f", h=YGH),
                        in1=sb["b_Etok"][:, h0:h0 + YGH, None]
                        .broadcast_to([128, YGH, PH]), op=ALU.mult))
                    s.selfedge("v")
                    yv = s.op("v", lambda e, yg=yg: e.scalar_tensor_tensor(
                        out=XY[:, yg * YGH * PH:(yg + 1) * YGH * PH],
                        in0=PB[:, 512:512 + YGH * PH], scalar=1.0,
                        in1=XY[:, yg * YGH * PH:(yg + 1) * YGH * PH],
                        op0=ALU.mult, op1=ALU.add), inc="bv")
                    y_done = yv
                    s.need("t", wxtok)
                    s.need("t", yv)
                    dstok = s.op("t", lambda e, yg=yg: e.matmul(
                        PC[:, 0:512], sb["b_BT"][:, :],
                        AR[:, o_wx + yg * YGH * PH:o_wx + (yg + 1) * YGH * PH],
                        start=True, stop=True, skip_group_check=True), inc="bt")
                    s.need("v", dstok)
                    s.op("v", lambda e, yg=yg, h0=h0: e.tensor_tensor(
                        out=sb["b_Sd"][:, :].rearrange("p (h f) -> p h f", h=YGH),
                        in0=ST[:, yg * YGH * PH:(yg + 1) * YGH * PH]
                        .rearrange("p (h f) -> p h f", h=YGH),
                        in1=sb["b_Lam"][:, h0:h0 + YGH, None]
                        .broadcast_to([128, YGH, PH]), op=ALU.mult))
                    s.selfedge("v")
                    s.op("v", lambda e, yg=yg: e.scalar_tensor_tensor(
                        out=ST[:, yg * YGH * PH:(yg + 1) * YGH * PH],
                        in0=PC[:, 0:512], scalar=1.0, in1=sb["b_Sd"][:, :],
                        op0=ALU.mult, op1=ALU.add))
                    s.selfedge("v")
                    y_done = s.op("v", lambda e, yg=yg: e.tensor_copy(
                        SBb[:, yg * YGH * PH:(yg + 1) * YGH * PH],
                        ST[:, yg * YGH * PH:(yg + 1) * YGH * PH]), inc="bv")
                if q_i == DBG_Q and ch == 0:
                    tap("dbg_Gm", sb["b_Gm"][:, :])
                    tap("dbg_gml", AR[:, o_gml:o_gml + H * 128])
                    tap("dbg_y", XY[:, :])
                    tap("dbg_S", ST[:, :])
                    tap("dbg_gate", GSL[:, 0:G2])
                s.need("v", gtok)
                ygv = s.op("v", lambda e: e.tensor_mul(
                    AR[:, o_yg:o_yg + G2], XY[:, 0:G2], GSL[:, 0:G2]), inc="bv")
                s.need("s", ygv)
                sqv = s.op("s", lambda e: e.activation(
                    out=AR[:, o_wx:o_wx + G2], in_=AR[:, o_yg:o_yg + G2],
                    func=AF.Square, accum_out=st[:, 14:15]), inc="bs")
                s.selfedge("s")
                s.op("s", lambda e: e.activation(out=st[:, 15:16], in_=st[:, 14:15],
                                                 func=AF.Ln, scale=1.0 / G2,
                                                 bias=float(EPS)))
                rstok = s.op("s", lambda e: e.activation(
                    out=st[:, 15:16], in_=st[:, 15:16], func=AF.Exp, scale=-0.5),
                    inc="bs")
                s.need("v", rstok)
                s.selfedge("v")
                ygn = s.op("v", lambda e: e.tensor_scalar_mul(
                    AR[:, o_yg:o_yg + G2], AR[:, o_yg:o_yg + G2],
                    st[:, 15:16]), inc="bv")
                s.selfedge("v")
                if q_i == DBG_Q and ch == 0:
                    tap("dbg_ygn", AR[:, o_yg:o_yg + G2])
                s.need("t", ygn)
                tp_half_free = {0: None, 1: None}
                for m in range(OKT):
                    half = m % 2
                    gslot = (m % 8) * 128
                    s.need("t", tp_half_free[half])
                    ttok = s.op("t", lambda e, m=m, half=half: e.transpose(
                        pbh(half)[:, 0:128], AR[:, o_yg + m * 128:o_yg + (m + 1) * 128],
                        identity=IDb), inc="bt")
                    s.need("v", ttok)
                    s.need("v", gsl_slot_free.get(gslot))
                    ev = s.op("v", lambda e, half=half, gslot=gslot: e.tensor_copy(
                        GSL[:, gslot:gslot + 128], pbh(half)[:, 0:128]), inc="bv")
                    tp_half_free[half] = ev
                    s.need("q", ev)
                    gsl_slot_free[gslot] = s.dma(
                        "q", lambda e, m=m, c0=c0, gslot=gslot: e.dma_start(
                            out=sc_ygnT[m, :, c0:c0 + 128],
                            in_=GSL[:, gslot:gslot + 128]),
                        sem="dg%d" % (gslot // 128))
                gsl_dma_done = ("d", s.cnt["d"])
                prev_epi_v = ("bv", s.cnt["bv"])
                prev_pbh = dict(tp_half_free)
                if "e" in HB:
                    s.barrier()
            s.barrier()

            if not do_f:
                continue
            wb_free = {}
            wy_free = {}
            out_done = {0: None, 1: None}
            prev_fv = {0: None, 1: None}
            GSLf = GSL[:, :].bitcast(F32)
            WAf = WA[:, :].bitcast(F32)
            reg2 = [(PB, 0), (PB, 512), (PC, 0), (PD, 0)]
            ocl = [(i * 512, min(512, D - i * 512))
                   for i in range((D + 511) // 512)]
            for tp in range(0, NTT, 2):
                s.need("q", prev_fv[0])
                xr0 = s.dma("q", lambda e, tp=tp: e.dma_start(
                    out=XY[:, 0:D], in_=sc_xpxn[tp * 128:(tp + 1) * 128, :]),
                    sem="dx0")
                s.need("q", prev_fv[1])
                xr1 = s.dma("q", lambda e, tp=tp: e.dma_start(
                    out=XY[:, D:2 * D],
                    in_=sc_xpxn[(tp + 1) * 128:(tp + 2) * 128, :]), sem="dx1")
                if q_i == DBG_Q and tp == 0:
                    s.need("q", xr0)
                    tap("dbg_fx", XY[:, 0:D])
                for kt in range(OKT):
                    sA = (kt % 2) * 2048
                    sy = (kt % 2) * 256
                    s.need("q", wb_free.get(sA))
                    wt = s.dma("q", lambda e, kt=kt, sA=sA: e.dma_start(
                        out=WB[:, sA:sA + D], in_=sc_o[kt * 128:(kt + 1) * 128, :]),
                        sem="db0" if sA == 0 else "db1")
                    s.need("q", wy_free.get(sy))
                    yt = s.dma("q", lambda e, kt=kt, sy=sy, tp=tp: e.dma_start(
                        out=WY[:, sy:sy + 256],
                        in_=sc_ygnT[kt, :, tp * 128:(tp + 2) * 128]),
                        sem="dy0" if sy == 0 else "dy1")
                    s.need("t", wt)
                    s.need("t", yt)
                    if kt == 0:
                        s.need("t", prev_fv[0])
                        s.need("t", prev_fv[1])
                    for oi, (o0, olen) in enumerate(ocl):
                        s.op("t", lambda e, kt=kt, sA=sA, sy=sy, o0=o0,
                             olen=olen: e.matmul(
                                 PA[:, o0:o0 + olen],
                                 WY[:, sy:sy + 128],
                                 WB[:, sA + o0:sA + o0 + olen],
                                 start=(kt == 0), stop=(kt == OKT - 1),
                                 skip_group_check=True))
                    for oi, (o0, olen) in enumerate(ocl):
                        pt, po = reg2[oi]
                        tok = s.op("t", lambda e, kt=kt, sA=sA, sy=sy, o0=o0,
                                   olen=olen, pt=pt, po=po: e.matmul(
                                       pt[:, po:po + olen],
                                       WY[:, sy + 128:sy + 256],
                                       WB[:, sA + o0:sA + o0 + olen],
                                       start=(kt == 0), stop=(kt == OKT - 1),
                                       skip_group_check=True),
                                   inc=("bt" if oi == len(ocl) - 1 else None))
                    wb_free[sA] = tok
                    wy_free[sy] = tok
                    if "f" in HB:
                        s.barrier()
                s.need("v", tok)
                s.need("v", xr0)
                s.need("v", xr1)
                s.need("v", out_done[0])
                s.need("v", out_done[1])
                fv0 = s.op("v", lambda e: e.scalar_tensor_tensor(
                    out=GSLf[:, 0:D], in0=PA[:, 0:D], scalar=1.0,
                    in1=XY[:, 0:D], op0=ALU.mult, op1=ALU.add), inc="bv")
                fv1 = None
                for oi, (o0, olen) in enumerate(ocl):
                    pt, po = reg2[oi]
                    fv1 = s.op("v", lambda e, o0=o0, olen=olen, pt=pt, po=po:
                               e.scalar_tensor_tensor(
                                   out=WAf[:, o0:o0 + olen],
                                   in0=pt[:, po:po + olen], scalar=1.0,
                                   in1=XY[:, D + o0:D + o0 + olen],
                                   op0=ALU.mult, op1=ALU.add), inc="bv")
                prev_fv = {0: fv0, 1: fv1}
                s.need("q", fv0)
                o0_ = s.dma("q", lambda e, tp=tp, T0=T0: e.dma_start(
                    out=out_t[T0 + tp * 128:T0 + (tp + 1) * 128, :],
                    in_=GSLf[:, 0:D]), sem="do")
                s.need("q", fv1)
                o1_ = s.dma("q", lambda e, tp=tp, T0=T0: e.dma_start(
                    out=out_t[T0 + (tp + 1) * 128:T0 + (tp + 2) * 128, :],
                    in_=WAf[:, 0:D]), sem="do2")
                out_done = {0: o0_, 1: o1_}
            s.barrier()

        if dump:
            s.barrier()
            dmp = {
                "dmp_gate": sc_gate, "dmp_xpxn": sc_xpxn,
                "dmp_cumhl": sc_cumhl, "dmp_cend": sc_cend,
            }
            dmp_sb = {
                "dmp_XBCB": XBCB, "dmp_XPb": None, "dmp_AR": AR,
                "dmp_cum": sb["b_cum"], "dmp_dt": sb["b_dt"],
                "dmp_dtraw": sb["b_dtraw"], "dmp_Etok": sb["b_Etok"],
                "dmp_Lam": sb["b_Lam"], "dmp_tok": sb["b_tok"],
                "dmp_ST": ST, "dmp_Gm": sb["b_Gm"], "dmp_XY": XY,
                "dmp_cumhl_sb": sb["b_cumhl"], "dmp_cendhl": sb["b_cendhl"],
                "dmp_cscr": sb["b_cscr"],
            }
            for nm, t_ in dmp.items():
                o_ = nc.dram_tensor(nm, list(t_.tensor.shape),
                                    t_.tensor.dtype, kind="ExternalOutput").ap()
                s.dma("q", lambda e, o_=o_, t_=t_: e.dma_start(out=o_, in_=t_))
            for nm, t_ in dmp_sb.items():
                if t_ is None:
                    t_ = XP[:, :].bitcast(BF16)
                else:
                    t_ = t_[:, :]
                o_ = nc.dram_tensor(nm, [t_.shape[0], t_.shape[1]],
                                    t_.dtype, kind="ExternalOutput").ap()
                s.dma("q", lambda e, o_=o_, t_=t_: e.dma_start(out=o_, in_=t_))
        for nm_ in s.q_dma_sems():
            s.wait("q", nm_)
        global _LAST_SCH
        _LAST_SCH = s
        s.simulate()
        s.emit(nc, sems)
    return nc


def kernel(L0_attn_hidden_states, L0_attn_norm_weight, L0_attn_in_proj_w,
           L0_attn_conv_w, L0_attn_conv_b, L0_attn_dt_bias, L0_attn_A_log,
           L0_attn_D, L0_attn_gate_norm_weight, L0_attn_out_proj_w,
           L0_ffn_gate_w, L0_ffn_up_w, L0_ffn_down_w, L0_ffn_gate_s,
           L0_ffn_up_s, L0_ffn_down_s):
    c = derive(full_cfg())
    w = dict(in_proj_w=np.asarray(L0_attn_in_proj_w, np.float32),
             conv_w=np.asarray(L0_attn_conv_w, np.float32),
             conv_b=np.asarray(L0_attn_conv_b, np.float32),
             dt_bias=np.asarray(L0_attn_dt_bias, np.float32),
             A_log=np.asarray(L0_attn_A_log, np.float32),
             D=np.asarray(L0_attn_D, np.float32),
             gate_norm_weight=np.asarray(L0_attn_gate_norm_weight, np.float32),
             out_proj_w=np.asarray(L0_attn_out_proj_w, np.float32),
             norm_weight=np.asarray(L0_attn_norm_weight, np.float32),
             gate_w=np.asarray(L0_ffn_gate_w, np.float32),
             up_w=np.asarray(L0_ffn_up_w, np.float32),
             down_w=np.asarray(L0_ffn_down_w, np.float32),
             gate_s=np.asarray(L0_ffn_gate_s, np.float32),
             up_s=np.asarray(L0_ffn_up_s, np.float32),
             down_s=np.asarray(L0_ffn_down_s, np.float32))
    consts = host_arrays(c, w)
    x = np.asarray(L0_attn_hidden_states, np.float32)
    nc = build(c)
    in_maps = []
    for b in range(8):
        m = dict(consts)
        m["x"] = np.ascontiguousarray(x[b])
        in_maps.append(m)
    res = run_bass_kernel_spmd(nc, in_maps, core_ids=list(range(8)))
    out = np.stack([np.asarray(res.results[b]["out"]) for b in range(8)], axis=0)
    return out.astype(np.float32)


# version 05
# speedup vs baseline: 1.2206x; 1.0261x over previous
import contextlib
from collections import defaultdict

import numpy as np
import concourse.bass as bass
import concourse.mybir as mybir
from concourse.bass_utils import run_bass_kernel_spmd

F32 = mybir.dt.float32
BF16 = mybir.dt.bfloat16
AF = mybir.ActivationFunctionType
ALU = mybir.AluOpType

PH = 64
NST = 128
KCV = 4
Q = 128
EPS = 1e-5
LNDT_CLAMP = -30.0
ENGS = ("t", "v", "s", "p", "q")
_LAST_SCH = None
DBG_Q = 0
HB = ""
ZINC = False
DPSPLIT = True


def K_HBX():
    return HB
DBG_PH = 'ABCDE0'


def full_cfg():
    return dict(D=2048, T=2048, H=64, FI=5632)


def derive(cfg):
    c = dict(cfg)
    D, T, H, FI = c["D"], c["T"], c["H"], c["FI"]
    c["G2"] = H * PH
    assert c["G2"] == 2 * D
    c["CONVD"] = c["G2"] + 2 * NST
    c["PROJ"] = c["G2"] + c["CONVD"] + H
    c["PROJP"] = c["G2"] + c["CONVD"] + 128
    c["KT"] = D // 128
    c["CT"] = c["G2"] // 128
    c["OCH"] = c["CONVD"] // 128
    c["NBLK"] = c["PROJP"] // 128
    c["ICH"] = FI // 128
    c["TQ"] = T // 4
    c["NTT"] = c["TQ"] // 128
    c["NCHQ"] = c["TQ"] // Q
    c["AGH"] = min(16, H)
    c["YGH"] = min(8, H)
    c["OKT"] = c["G2"] // 128
    fg = min(22, c["ICH"])
    while c["ICH"] % fg:
        fg -= 1
    c["FGRP"] = fg
    assert c["TQ"] <= 512 and c["CT"] % 4 == 0 and c["D"] % 128 == 0
    return c


def host_arrays(c, w):
    D, H, FI = c["D"], c["H"], c["FI"]
    CONVD, PROJ, PROJP = c["CONVD"], c["PROJ"], c["PROJP"]
    KT, CT, OCH, NBLK, ICH, OKT = (c["KT"], c["CT"], c["OCH"], c["NBLK"],
                                   c["ICH"], c["OKT"])
    o = {}
    o["c_tri"] = (np.arange(128)[:, None] <= np.arange(128)[None, :]).astype(np.float32)
    o["c_I"] = np.eye(128, dtype=np.float32)
    o["c_up"] = np.where(np.arange(128)[:, None] > np.arange(128)[None, :],
                         np.float32(-1.0e6), np.float32(0.0))
    o["c_ones2"] = np.ones((2, 128), np.float32)
    wp = np.zeros((PROJP, D), np.float32)
    wp[:PROJ] = w["in_proj_w"]
    G2 = 2 * D
    gate_part = np.ascontiguousarray(
        wp[:G2].reshape(G2 // 512, 512, KT, 128).transpose(0, 3, 2, 1))
    rest_part = np.ascontiguousarray(
        wp[G2:].reshape(NBLK - G2 // 128, 128, KT, 128).transpose(0, 3, 2, 1))
    o["w_in"] = np.concatenate([gate_part.reshape(-1), rest_part.reshape(-1)])
    for nm, src in (("w_g", w["gate_w"]), ("w_u", w["up_w"])):
        o[nm] = np.ascontiguousarray(
            src.reshape(ICH, 128, KT, 128).transpose(0, 3, 2, 1)).reshape(-1)
    o["w_d"] = np.ascontiguousarray(w["down_w"])
    o["w_out"] = np.ascontiguousarray(w["out_proj_w"].T)
    cw = w["conv_w"].reshape(CONVD, KCV)
    cd = np.zeros((128, OCH * KCV * 128), np.float32)
    for m in range(OCH):
        for k in range(KCV):
            blk = (m * KCV + k) * 128
            cd[np.arange(128), blk + np.arange(128)] = cw[m * 128:(m + 1) * 128, k]
    o["c_convd"] = cd
    dd = np.zeros((128, CT * 128), np.float32)
    for m in range(CT):
        ch = (m * 128 + np.arange(128)) // PH
        dd[np.arange(128), m * 128 + np.arange(128)] = w["D"][ch]
    o["c_Dd"] = dd
    o["c_convb"] = np.ascontiguousarray(w["conv_b"].reshape(OCH, 128).T)
    o["c_nw"] = np.ascontiguousarray(w["norm_weight"].reshape(KT, 128).T)
    o["c_dtb"] = w["dt_bias"].reshape(H, 1).astype(np.float32)
    o["c_alog"] = w["A_log"].reshape(H, 1).astype(np.float32)
    o["c_gs"] = np.ascontiguousarray(w["gate_s"].reshape(ICH, 128).T)
    o["c_us"] = np.ascontiguousarray(w["up_s"].reshape(ICH, 128).T)
    o["c_ds"] = np.ascontiguousarray(w["down_s"].reshape(KT, 128).T)
    o["c_gnw"] = np.ascontiguousarray(w["gate_norm_weight"].reshape(OKT, 128).T)
    return o


class Sch:
    def __init__(self):
        self.ops = {e: [] for e in ENGS}
        self.cnt = defaultdict(int)
        self.bcnt = {e: 0 for e in ENGS}

    def op(self, e, fn, inc=None, n=1):
        if inc is None and ZINC and e in ("t", "v", "s", "p"):
            inc, n = self.OWN[e], 1
        incs = []
        if inc is not None:
            self.cnt[inc] += n
            incs.append((inc, n))
        self.ops[e].append(("op", fn, incs))
        return (inc, self.cnt[inc]) if inc is not None else None

    def dma(self, e, fn, sem=None):
        if sem is None:
            sem = "d" if (e == "q" or not DPSPLIT) else "dp"
        return self.op(e, fn, inc=sem, n=16)

    def wait(self, e, sem, val=None):
        v = self.cnt[sem] if val is None else val
        self.ops[e].append(("wait", sem, v))

    def selfedge(self, e):
        for item in reversed(self.ops[e]):
            if item[0] == "op":
                if not item[2]:
                    own = self.OWN[e]
                    self.cnt[own] += 1
                    item[2].append((own, 1))
                nm, n = item[2][0]
                self.ops[e].append(("wait", nm, self.cnt[nm]))
                return
            if item[0] == "wait":
                continue
            return

    def need(self, e, tok):
        if tok is not None:
            self.ops[e].append(("wait", tok[0], tok[1]))

    OWN = {"t": "bt", "v": "bv", "s": "bs", "p": "bp", "q": "d"}
    P_DMA = ("dp", "dp0", "dp1")

    def q_dma_sems(self):
        return [k for k in self.cnt
                if k.startswith("d") and k not in self.P_DMA and self.cnt[k]]

    def barrier(self, engines=ENGS):
        for e in engines:
            self.bcnt[e] += 1
            if e == "q":
                owns = self.q_dma_sems()
            elif e == "p":
                owns = ["bp", "dp", "dp0", "dp1"] if DPSPLIT else ["bp", "d"]
            else:
                owns = [self.OWN[e]]
            for own in owns:
                if self.cnt[own]:
                    self.ops[e].append(("wait", own, self.cnt[own]))
            self.ops[e].append(("barinc", e))
        for e in engines:
            for oth in engines:
                if oth != e:
                    self.ops[e].append(("wait", "B" + oth, self.bcnt[oth]))

    def simulate(self):
        pos = {e: 0 for e in ENGS}
        sem = defaultdict(int)
        progress = True
        while progress:
            progress = False
            for e in ENGS:
                while pos[e] < len(self.ops[e]):
                    item = self.ops[e][pos[e]]
                    if item[0] == "op":
                        for nm_, n_ in item[2]:
                            sem[nm_] += n_
                        pos[e] += 1
                        progress = True
                    elif item[0] == "wait":
                        if sem[item[1]] >= item[2]:
                            pos[e] += 1
                            progress = True
                        else:
                            break
                    else:
                        sem["B" + item[1]] += 1
                        pos[e] += 1
                        progress = True
        stuck = {e: pos[e] for e in ENGS if pos[e] < len(self.ops[e])}
        if stuck:
            msg = []
            for e, p in stuck.items():
                it = self.ops[e][p]
                msg.append(f"{e}@{p}/{len(self.ops[e])}: wait {it[1]} >= {it[2]} "
                           f"(sim has {dict(sem).get(it[1], 0)})")
            raise RuntimeError("DEADLOCK in schedule:\n" + "\n".join(msg))

    def emit(self, nc, sems):
        finals = dict(self.cnt)
        for e in ENGS:
            finals["B" + e] = self.bcnt[e]
        for e in ENGS:
            for item in self.ops[e]:
                if item[0] == "wait":
                    assert item[2] <= finals.get(item[1], 0), (
                        f"unreachable wait {item[1]} >= {item[2]} (final "
                        f"{finals.get(item[1], 0)}) on engine {e}")

        def run(eng, stream):
            for item in self.ops[stream]:
                if item[0] == "op":
                    _, fn, incs = item
                    inst = fn(eng)
                    for nm, n in incs:
                        inst = inst.then_inc(sems[nm], n)
                elif item[0] == "wait":
                    eng.wait_ge(sems[item[1]], item[2])
                else:
                    eng.nop().then_inc(sems["B" + item[1]], 1)

        with nc.Block() as block:
            @block.tensor
            def _(e):
                run(e, "t")

            @block.vector
            def _(e):
                run(e, "v")

            @block.scalar
            def _(e):
                run(e, "s")

            @block.gpsimd
            def _(e):
                run(e, "p")

            @block.sync
            def _(e):
                run(e, "q")



def _pr4(PA, PB, PC, PD, NTT):
    def pr(tt4, oo):
        if NTT != 4:
            return ([(PA, 0), (PB, 0)][tt4][0], oo)
        if tt4 == 0:
            return PA, oo
        if tt4 == 1:
            return PA, 1024 + oo
        if tt4 == 2:
            return PB, oo
        return (PC, oo) if oo < 512 else (PD, oo - 512)
    return pr

def build(c, limit=99, dbg=False, dump=False):
    nc = bass.Bass("TRN2", target_bir_lowering=False, debug=False)
    D, T, H, FI = c["D"], c["T"], c["H"], c["FI"]
    G2, CONVD = c["G2"], c["CONVD"]
    KT, CT, OCH, NBLK, ICH = c["KT"], c["CT"], c["OCH"], c["NBLK"], c["ICH"]
    TQ, NTT, NCHQ = c["TQ"], c["NTT"], c["NCHQ"]
    AGH, YGH, OKT, FGRP = c["AGH"], c["YGH"], c["OKT"], c["FGRP"]
    NYG, NAG, NFG = H // YGH, H // AGH, ICH // FGRP

    ap = {}
    for nm, shp in [
        ("x", (T, D)), ("w_in", (NBLK * KT * 128 * 128,)),
        ("w_g", (D * FI,)), ("w_u", (D * FI,)),
        ("w_d", (D, FI)), ("w_out", (G2, D)),
        ("c_convd", (128, OCH * KCV * 128)), ("c_Dd", (128, CT * 128)),
        ("c_convb", (128, OCH)), ("c_nw", (128, KT)), ("c_dtb", (H, 1)),
        ("c_alog", (H, 1)), ("c_gs", (128, ICH)), ("c_us", (128, ICH)),
        ("c_ds", (128, KT)), ("c_gnw", (128, OKT)), ("c_tri", (128, 128)),
        ("c_I", (128, 128)), ("c_up", (128, 128)), ("c_ones2", (2, 128)),
    ]:
        ap[nm] = nc.dram_tensor(nm, list(shp), F32, kind="ExternalInput").ap()
    out_t = nc.dram_tensor("out", [T, D], F32, kind="ExternalOutput").ap()
    dbg_t = {}
    if dbg:
        for nm, shp, dt_ in [
            ("dbg_xn", (128, D), BF16), ("dbg_xnm", (128, D), BF16),
            ("dbg_xpxn", (128, D), F32), ("dbg_hdn", (128, FGRP * TQ), BF16),
            ("dbg_gate", (128, G2), BF16), ("dbg_dtraw", (H, TQ), F32),
            ("dbg_dt", (H, TQ), F32), ("dbg_cum", (H, TQ), F32),
            ("dbg_xbc0", (128, TQ), BF16), ("dbg_Bc", (128, TQ), BF16),
            ("dbg_tok", (128, 3 * H), BF16), ("dbg_Gm", (128, 128), BF16),
            ("dbg_gml", (128, H * 128), BF16), ("dbg_Etok", (128, H), F32),
            ("dbg_Lam", (128, H), F32), ("dbg_y", (128, 2 * D), F32),
            ("dbg_ygn", (128, G2), BF16), ("dbg_S", (128, G2), F32),
            ("dbg_xsT", (128, G2), BF16), ("dbg_WX", (128, G2), BF16),
            ("dbg_stat", (128, 16), F32), ("dbg_x", (128, D), F32),
            ("dbg_xnT", (128, KT * TQ), BF16), ("dbg_xnmT", (128, KT * TQ), BF16),
            ("dbg_wd", (128, D), BF16), ("dbg_win", (128, KT * 128), BF16),
            ("dbg_fx", (128, D), F32),
        ]:
            dbg_t[nm] = nc.dram_tensor(nm, list(shp), dt_, kind="ExternalOutput").ap()

    def dsc(nm, shp, dt=BF16):
        return nc.dram_tensor(nm, list(shp), dt, kind="Internal").ap()

    sc_in = dsc("sc_in", (NBLK * KT * 128 * 128,))
    sc_g = dsc("sc_g", (D * FI,))
    sc_u = dsc("sc_u", (D * FI,))
    sc_d = dsc("sc_d", (FI, D))
    sc_o = dsc("sc_o", (G2, D))
    sc_cv = dsc("sc_cv", (128, OCH * KCV * 128))
    sc_gate = dsc("sc_gate", (TQ, G2))
    sc_xpxn = dsc("sc_xpxn", (TQ, D), F32)
    sc_ygnT = dsc("sc_ygnT", (OKT, 128, TQ))
    sc_cumhl = dsc("sc_cumhl", (2, NCHQ * H * Q))
    sc_cend = dsc("sc_cend", (2, NCHQ * H))

    o_xnT, o_xnmT = 0, KT * TQ
    o_xsT, o_wx, o_gml = 0, G2, 2 * G2
    o_expl = o_gml + H * 128
    o_yg = o_expl + AGH * 128
    AR_COLS = max(2 * KT * TQ, o_yg + G2)

    es = contextlib.ExitStack()
    sb = {}

    def sbuf(name, shape, dt=BF16):
        sb[name] = es.enter_context(nc.sbuf_tensor("sb_" + name, list(shape), dt))

    with es:
        for nm in ["c_tri", "c_I", "c_up"]:
            sbuf(nm, (128, 128))
        sbuf("c_If", (128, 128), F32)
        sbuf("c_ones2", (2, 128))
        sbuf("c_Dd", (128, CT * 128))
        sbuf("c_convb", (128, OCH), F32)
        sbuf("c_nw", (128, KT), F32)
        sbuf("c_dtb", (H, 1), F32)
        sbuf("c_negA", (H, 1), F32)
        sbuf("c_gs", (128, ICH), F32)
        sbuf("c_us", (128, ICH), F32)
        sbuf("c_ds", (128, KT), F32)
        sbuf("c_gnw", (128, OKT), F32)
        sbuf("S", (128, G2), F32)
        sbuf("Sb", (128, G2))
        sbuf("AR", (128, AR_COLS))
        sbuf("XP", (128, NTT * D), F32)
        sbuf("XBCB", (128, 2 * TQ + OCH * (KCV - 1)))
        sbuf("XY", (128, 2 * D), F32)
        sbuf("HDN", (128, FGRP * TQ))
        sbuf("GSL", (128, max(G2, NTT * 512)))
        sbuf("WA", (128, 2 * 2048))
        sbuf("WB", (128, 2 * 2048))
        sbuf("WY", (128, 2 * TQ))
        for nm in ["b_dtraw", "b_dt", "b_cum", "b_t1"]:
            sbuf(nm, (H, TQ), F32)
        for nm in ["b_dtA", "b_dtAn", "b_lndt", "b_wdt", "b_chi", "b_clo"]:
            sbuf(nm, (H, TQ))
        sbuf("b_cendh", (H, NCHQ))
        sbuf("b_cendl", (H, NCHQ))
        sbuf("b_z", (H, Q), F32)
        sbuf("b_cumhl", (2, 2 * AGH * Q))
        sbuf("b_cendhl", (2, NCHQ * H))
        sbuf("b_tok", (128, 3 * H))
        sbuf("b_Etok", (128, H), F32)
        sbuf("b_Lam", (128, H), F32)
        sbuf("b_Gm", (128, 128))
        sbuf("b_BT", (128, NST))
        sbuf("b_Sd", (128, YGH * PH), F32)
        sbuf("b_stat", (128, 16), F32)
        sbuf("b_cscr", (H, 2 * NCHQ), F32)
        sbuf("c_bias", (128, 2), F32)
        nc.const_aps.register(F32, float(EPS), sb["c_bias"][:, 0:1])
        nc.const_aps.register(F32, 1.0, sb["c_bias"][:, 1:2])

        PA = es.enter_context(nc.psum_tensor("PA", [128, 2048], F32))
        PB = es.enter_context(nc.psum_tensor("PB", [128, 1024], F32))
        PC = es.enter_context(nc.psum_tensor("PC", [128, 512], F32))
        PD = es.enter_context(nc.psum_tensor("PD", [128, 512], F32))

        sems = {}
        for nm in (["d", "dc", "dp", "dp0", "dp1", "dw0", "dw1", "da0", "da1",
                    "db0", "db1", "dy0", "dy1", "dx0", "dx1", "dx2", "dx3",
                    "dgt", "do0", "do1", "do2", "do3"]
                   + ["dch0", "dch1"]
                   + ["dg%d" % k for k in range(8)]
                   + ["b" + e for e in ENGS] + ["B" + e for e in ENGS]):
            sems[nm] = es.enter_context(nc.semaphore(nm))

        s = Sch()

        PHOF = {"dbg_stat": "A", "dbg_x": "A", "dbg_xn": "A", "dbg_xnm": "A",
                "dbg_xnT": "B", "dbg_xnmT": "B", "dbg_hdn": "B", "dbg_xpxn": "B",
                "dbg_wd": "B", "dbg_win": "C", "dbg_dtraw": "C", "dbg_gate": "E",
                "dbg_xbc0": "D", "dbg_Bc": "D", "dbg_dt": "0", "dbg_cum": "0",
                "dbg_tok": "E", "dbg_Etok": "E", "dbg_Lam": "E", "dbg_xsT": "E",
                "dbg_WX": "E", "dbg_Gm": "E", "dbg_gml": "E", "dbg_y": "E",
                "dbg_S": "E", "dbg_ygn": "E", "dbg_fx": "F"}

        def tap(nm, src_ap):
            if dbg and nm in dbg_t and PHOF.get(nm, "?") in DBG_PH:
                s.wait("q", "bv")
                s.wait("q", "bs")
                s.wait("q", "bt")
                s.dma("q", lambda e, nm=nm, src_ap=src_ap: e.dma_start(
                    out=dbg_t[nm], in_=src_ap))

        AR, XP, XY, XBCB = sb["AR"], sb["XP"], sb["XY"], sb["XBCB"]
        HDN, GSL, WA, WB, WY = sb["HDN"], sb["GSL"], sb["WA"], sb["WB"], sb["WY"]
        ST, SBb = sb["S"], sb["Sb"]
        HIST0 = 2 * TQ
        IDb = sb["c_I"][:, :]

        def xbca(lo, hi):
            return XP[:, :].bitcast(BF16)[:, lo:hi]

        def pbh(i):
            return PB[:, i * 512:(i + 1) * 512].bitcast(BF16)

        for nm in ["c_tri", "c_I", "c_up", "c_ones2", "c_Dd"]:
            s.dma("p", lambda e, d=sb[nm], sr=ap[nm]: e.dma_start(out=d[:, :], in_=sr))
        s.dma("q", lambda e: e.dma_start(out=sb["c_If"][:, :], in_=ap["c_I"]),
              sem="dc")
        for nm in ["c_convb", "c_nw", "c_dtb", "c_gs", "c_us", "c_ds", "c_gnw"]:
            s.dma("q", lambda e, d=sb[nm], sr=ap[nm]: e.dma_start(out=d[:, :], in_=sr),
                  sem="dc")
        s.dma("q", lambda e: e.dma_start(out=sb["c_negA"][:, :], in_=ap["c_alog"]),
              sem="dc")
        s.wait("s", "dc")
        s.wait("s", "dp")
        s.op("s", lambda e: e.activation(out=sb["c_negA"][:, :], in_=sb["c_negA"][:, :],
                                         func=AF.Exp), inc="bs")
        s.wait("v", "bs")
        s.op("v", lambda e: e.tensor_scalar_mul(sb["c_negA"][:, :], sb["c_negA"][:, :], -1.0))
        s.op("v", lambda e: e.memset(sb["b_z"][:, :], 0.0))
        s.op("v", lambda e: e.memset(sb["c_bias"][:, 0:1], float(EPS)))
        s.op("v", lambda e: e.memset(sb["c_bias"][:, 1:2], 1.0))
        s.op("v", lambda e: e.memset(ST[:, :], 0.0))
        s.op("v", lambda e: e.memset(SBb[:, :], 0.0))
        s.op("v", lambda e: e.memset(XBCB[:, HIST0:HIST0 + OCH * (KCV - 1)], 0.0),
             inc="bv")

        CH = 2048
        wb_done = {}

        def plain_cast(src_ap, dst_ap, total_elems, src2d=False, dst2d=False):
            cols = total_elems // 128
            assert cols * 128 == total_elems
            c0_ = 0
            i = 0
            while c0_ < cols:
                w_ = min(CH, cols - c0_)
                slot = (i % 2) * 2048
                s.need("p", wb_done.get(slot))
                if src2d:
                    sview = src_ap[:, c0_:c0_ + w_]
                else:
                    sview = src_ap.rearrange("(p m) -> p m", p=128)[:, c0_:c0_ + w_]
                tok = s.dma("p", lambda e, sview=sview, slot=slot, w_=w_: e.dma_start(
                    out=WA[:, slot:slot + w_], in_=sview),
                    sem="dp0" if slot == 0 else "dp1")
                s.need("q", tok)
                if dst2d:
                    dview = dst_ap[:, c0_:c0_ + w_]
                else:
                    dview = dst_ap.rearrange("(p m) -> p m", p=128)[:, c0_:c0_ + w_]
                wb_done[slot] = s.dma("q", lambda e, dview=dview, slot=slot, w_=w_:
                                      e.dma_start(out=dview, in_=WA[:, slot:slot + w_]),
                                      sem="dw0" if slot == 0 else "dw1")
                c0_ += w_
                i += 1

        if limit >= 1:
            plain_cast(ap["w_in"], sc_in, NBLK * KT * 128 * 128)
        if limit >= 1:
            plain_cast(ap["w_g"], sc_g, D * FI)
            plain_cast(ap["w_u"], sc_u, D * FI)
            plain_cast(ap["c_convd"], sc_cv, 128 * OCH * KCV * 128, src2d=True, dst2d=True)
        for it in range(OKT):
            slot = (it % 2) * 2048
            s.need("p", wb_done.get(slot))
            tok = s.dma("p", lambda e, it=it, slot=slot: e.dma_start(
                out=WA[:, slot:slot + D], in_=ap["w_out"][it * 128:(it + 1) * 128, :]),
                sem="dp0" if slot == 0 else "dp1")
            s.need("v", tok)
            vt = s.op("v", lambda e, it=it, slot=slot: e.tensor_scalar_mul(
                WA[:, slot:slot + D], WA[:, slot:slot + D],
                sb["c_gnw"][:, it:it + 1]), inc="bv")
            s.need("q", vt)
            wb_done[slot] = s.dma("q", lambda e, it=it, slot=slot: e.dma_start(
                out=sc_o[it * 128:(it + 1) * 128, :], in_=WA[:, slot:slot + D]),
                sem="dw0" if slot == 0 else "dw1")
        wb_done[("d", 0)] = wb_done.get(0)
        wb_done[("d", 2048)] = wb_done.get(2048)
        gsl_done = {}
        half_free = {0: None, 1: None}
        for it in range(KT if limit >= 3 else 0):
            nfc = (FI + 2047) // 2048
            for fc in range(nfc):
                w0 = fc * 2048
                wlen = min(2048, FI - w0)
                slot = (fc % 2) * 2048
                s.need("p", wb_done.get(("d", slot)))
                tok = s.dma("p", lambda e, it=it, w0=w0, wlen=wlen, slot=slot: e.dma_start(
                    out=WA[:, slot:slot + wlen],
                    in_=ap["w_d"][it * 128:(it + 1) * 128, w0:w0 + wlen]),
                    sem="dp0" if slot == 0 else "dp1")
                s.need("v", tok)
                vt = s.op("v", lambda e, wlen=wlen, slot=slot, it=it: e.tensor_scalar_mul(
                    WA[:, slot:slot + wlen], WA[:, slot:slot + wlen],
                    sb["c_ds"][:, it:it + 1]), inc="bv")
                wb_done[("d", slot)] = vt
                s.need("t", vt)
                nblk = wlen // 128
                for bi in range(nblk):
                    half = bi % 2
                    gslot = (bi % 8) * 128
                    s.need("t", half_free[half])
                    tt_ = s.op("t", lambda e, slot=slot, bi=bi, half=half: e.transpose(
                        pbh(half)[:, (bi % 2) * 0:128] if False else pbh(half)[:, 0:128],
                        WA[:, slot + bi * 128:slot + (bi + 1) * 128],
                        identity=IDb), inc="bt")
                    s.need("v", tt_)
                    s.need("v", gsl_done.get(gslot))
                    vt2 = s.op("v", lambda e, half=half, gslot=gslot: e.tensor_copy(
                        GSL[:, gslot:gslot + 128], pbh(half)[:, 0:128]), inc="bv")
                    half_free[half] = vt2
                    s.need("q", vt2)
                    gsl_done[gslot] = s.dma("q", lambda e, it=it, w0=w0, bi=bi,
                                            gslot=gslot: e.dma_start(
                        out=sc_d[w0 + bi * 128:w0 + (bi + 1) * 128,
                                 it * 128:(it + 1) * 128],
                        in_=GSL[:, gslot:gslot + 128]), sem="dg%d" % (gslot // 128))
                wb_done[("d", slot)] = ("bt", s.cnt["bt"])
        s.barrier()

        for q_i in range(4 if limit >= 4 else 0):
            T0 = q_i * TQ

            st = sb["b_stat"]
            do_b = limit >= 5
            do_c = limit >= 6 and limit != 55
            do_d = limit >= 7 and limit != 55
            do_d0 = limit >= 8 and limit != 55
            do_e = limit >= 9 and limit != 55
            do_f = limit >= 10 or limit == 55
            tp_half_free = {0: None, 1: None}
            prev_tt_done = {}
            for tt in range(NTT):
                xoff = (tt % 2) * D
                s.need("q", prev_tt_done.get(tt % 2))
                xtok = s.dma("q", lambda e, tt=tt, xoff=xoff, T0=T0: e.dma_start(
                    out=XY[:, xoff:xoff + D],
                    in_=ap["x"][T0 + tt * 128:T0 + (tt + 1) * 128, :]),
                    sem="dx%d" % (tt % 2))
                s.need("v", xtok)
                xsl = XY[:, xoff:xoff + D]
                s.op("v", lambda e, xsl=xsl: e.reduce_sum(
                    st[:, 0:1], xsl, axis=mybir.AxisListType.X), inc="bv")
                s.selfedge("v")
                s.op("v", lambda e: e.tensor_scalar_mul(st[:, 2:3], st[:, 0:1], 1.0 / D))
                s.op("v", lambda e: e.tensor_scalar_mul(st[:, 3:4], st[:, 0:1],
                                                        -1.0 / D), inc="bv")
                s.wait("s", "bv")
                s.wait("s", "bt")
                s.op("s", lambda e, xsl=xsl: e.activation(
                    out=GSL[:, 0:D], in_=xsl, func=AF.Square,
                    bias=st[:, 3:4], accum_out=st[:, 4:5]), inc="bs")
                s.wait("v", "bs")
                s.op("v", lambda e: e.tensor_scalar_mul(st[:, 5:6], st[:, 4:5], 1.0 / D),
                     inc="bv")
                s.wait("s", "bv")
                s.op("s", lambda e: e.activation(out=st[:, 6:7], in_=st[:, 5:6],
                                                 func=AF.Ln, bias=float(EPS)))
                s.op("s", lambda e: e.activation(out=st[:, 7:8], in_=st[:, 6:7],
                                                 func=AF.Exp, scale=-0.5), inc="bs")
                s.wait("v", "bs")
                s.op("v", lambda e: e.tensor_scalar(
                    out=st[:, 8:9], in0=st[:, 5:6], scalar1=st[:, 7:8],
                    scalar2=st[:, 7:8], op0=ALU.mult, op1=ALU.mult), inc="bv")
                s.wait("s", "bv")
                s.op("s", lambda e: e.activation(out=st[:, 9:10], in_=st[:, 8:9],
                                                 func=AF.Ln, bias=float(EPS)))
                s.op("s", lambda e: e.activation(out=st[:, 10:11], in_=st[:, 9:10],
                                                 func=AF.Exp, scale=-0.5), inc="bs")
                s.wait("v", "bs")
                s.op("v", lambda e: e.tensor_scalar(
                    out=st[:, 11:12], in0=st[:, 7:8], scalar1=st[:, 10:11],
                    scalar2=None, op0=ALU.mult))
                s.op("v", lambda e: e.tensor_scalar_add(st[:, 12:13], st[:, 7:8], 1.0))
                s.op("v", lambda e: e.tensor_scalar(
                    out=st[:, 13:14], in0=st[:, 2:3], scalar1=st[:, 7:8],
                    scalar2=None, op0=ALU.mult))
                s.selfedge("v")
                s.wait("v", "bt")
                s.op("v", lambda e, xsl=xsl: e.tensor_scalar(
                    out=GSL[:, 0:D], in0=xsl, scalar1=st[:, 2:3], scalar2=st[:, 7:8],
                    op0=ALU.subtract, op1=ALU.mult))
                s.op("v", lambda e, xsl=xsl: e.tensor_scalar(
                    out=GSL[:, D:2 * D], in0=xsl, scalar1=st[:, 2:3],
                    scalar2=st[:, 11:12], op0=ALU.subtract, op1=ALU.mult))
                vlast = s.op("v", lambda e, xsl=xsl, tt=tt: e.tensor_scalar(
                    out=XP[:, tt * D:(tt + 1) * D], in0=xsl, scalar1=st[:, 12:13],
                    scalar2=st[:, 13:14], op0=ALU.mult, op1=ALU.subtract), inc="bv")
                prev_tt_done[tt % 2] = vlast
                if q_i == DBG_Q and tt == 0:
                    tap("dbg_stat", sb["b_stat"][:, :])
                    tap("dbg_x", XY[:, 0:D])
                    tap("dbg_xn", GSL[:, 0:D])
                    tap("dbg_xnm", GSL[:, D:2 * D])
                    tap("dbg_xpxn", XP[:, 0:D]) if False else None
                if limit == 43:
                    continue
                s.need("t", vlast)
                for blk in range(2 * KT):
                    ht, which = blk // 2, blk % 2
                    half = blk % 2
                    s.need("t", tp_half_free[half])
                    src = GSL[:, which * D + ht * 128: which * D + (ht + 1) * 128]
                    ttok = s.op("t", lambda e, src=src, half=half: e.transpose(
                        pbh(half)[:, 0:128], src, identity=IDb), inc="bt")
                    dst_off = (o_xnT if which == 0 else o_xnmT) + ht * TQ + tt * 128
                    if which == 0:
                        s.need("s", ttok)
                        ev = s.op("s", lambda e, half=half, dst_off=dst_off: e.copy(
                            AR[:, dst_off:dst_off + 128], pbh(half)[:, 0:128]),
                            inc="bs")
                    else:
                        s.need("v", ttok)
                        ev = s.op("v", lambda e, half=half, dst_off=dst_off, ht=ht:
                                  e.tensor_scalar_mul(
                                      AR[:, dst_off:dst_off + 128],
                                      pbh(half)[:, 0:128],
                                      sb["c_nw"][:, ht:ht + 1]), inc="bv")
                    tp_half_free[half] = ev
                if "a" in HB:
                    s.barrier()
            s.barrier()

            wa_free = {}
            if not do_b:
                continue
            b_sub = limit if limit in (51, 52, 53) else (limit - 470 if limit >= 520 else 99)
            wbslot_free = {}
            pbg_free = None
            pbu_free = None
            for fg in range(NFG):
                for icl in range(FGRP):
                    ic = fg * FGRP + icl
                    sA = (ic % 2) * 2048
                    s.need("q", wa_free.get(sA))
                    gt = s.dma("q", lambda e, ic=ic, sA=sA: e.dma_start(
                        out=WA[:, sA:sA + KT * 128],
                        in_=sc_g.rearrange("(n p m) -> n p m", p=128, m=KT * 128)[ic]),
                        sem="da0" if sA == 0 else "da1")
                    s.need("q", wbslot_free.get(sA))
                    ut = s.dma("q", lambda e, ic=ic, sA=sA: e.dma_start(
                        out=WB[:, sA:sA + KT * 128],
                        in_=sc_u.rearrange("(n p m) -> n p m", p=128, m=KT * 128)[ic]),
                        sem="db0" if sA == 0 else "db1")
                    s.need("t", gt)
                    s.need("t", ut)
                    if "y" in K_HBX():
                        s.wait("t", "d")
                    s.need("t", pbg_free)
                    s.need("t", pbu_free)
                    for kt in range(KT):
                        lastg = s.op("t", lambda e, kt=kt, sA=sA: e.matmul(
                            PB[:, 0:TQ],
                            WA[:, sA + kt * 128:sA + (kt + 1) * 128],
                            AR[:, o_xnT + kt * TQ:o_xnT + (kt + 1) * TQ],
                            start=(kt == 0), stop=(kt == KT - 1),
                            skip_group_check=True),
                            inc=("bt" if kt == KT - 1 else None))
                    for kt in range(KT):
                        tok = s.op("t", lambda e, kt=kt, sA=sA: e.matmul(
                            PB[:, 512:512 + TQ],
                            WB[:, sA + kt * 128:sA + (kt + 1) * 128],
                            AR[:, o_xnT + kt * TQ:o_xnT + (kt + 1) * TQ],
                            start=(kt == 0), stop=(kt == KT - 1),
                            skip_group_check=True),
                            inc=("bt" if kt == KT - 1 else None))
                    wa_free[sA] = tok
                    wbslot_free[sA] = tok
                    if b_sub == 51:
                        pbg_free = tok
                        pbu_free = tok
                        continue
                    s.need("s", lastg)
                    if "x" in K_HBX():
                        s.need("s", tok)
                    s.need("s", pbu_free)
                    if b_sub == 61:
                        sg = s.op("s", lambda e: e.copy(
                            GSL[:, 0:TQ], PB[:, 0:TQ]), inc="bs")
                    elif b_sub == 62:
                        sg = s.op("s", lambda e: e.activation(
                            out=GSL[:, 0:TQ], in_=PB[:, 0:TQ], func=AF.Silu), inc="bs")
                    else:
                        sg = s.op("s", lambda e, ic=ic: e.activation(
                            out=GSL[:, 0:TQ], in_=PB[:, 0:TQ], func=AF.Silu,
                            scale=sb["c_gs"][:, ic:ic + 1]), inc="bs")
                    pbg_free = sg
                    s.need("v", sg)
                    s.need("v", tok)
                    if b_sub == 63:
                        hv = s.op("v", lambda e, icl=icl: e.tensor_mul(
                            HDN[:, icl * TQ:(icl + 1) * TQ], PB[:, 512:512 + TQ],
                            GSL[:, 0:TQ]), inc="bv")
                    elif b_sub == 64:
                        hv = s.op("v", lambda e, icl=icl: e.tensor_copy(
                            HDN[:, icl * TQ:(icl + 1) * TQ], PB[:, 512:512 + TQ]),
                            inc="bv")
                    else:
                        hv = s.op("v", lambda e, ic=ic, icl=icl: e.scalar_tensor_tensor(
                            out=HDN[:, icl * TQ:(icl + 1) * TQ], in0=PB[:, 512:512 + TQ],
                            scalar=sb["c_us"][:, ic:ic + 1], in1=GSL[:, 0:TQ],
                            op0=ALU.mult, op1=ALU.mult), inc="bv")
                    pbu_free = hv
                    if "b" in HB:
                        s.barrier()
                if b_sub in (51, 52, 61, 62, 63, 64):
                    continue
                hdn_ready = hv
                wbslot_free[("dn", 0)] = wbslot_free.get(0)
                wbslot_free[("dn", 2048)] = wbslot_free.get(2048)
                pa_free = None
                HW = D // 2
                pr = _pr4(PA, PB, PC, PD, NTT)
                for dh in range(2):
                    for icl in range(FGRP):
                        ic = fg * FGRP + icl
                        sB = (icl % 2) * 2048
                        s.need("q", wbslot_free.get(("dn", sB)))
                        dt_ = s.dma("q", lambda e, ic=ic, sB=sB, dh=dh: e.dma_start(
                            out=WB[:, sB:sB + HW],
                            in_=sc_d[ic * 128:(ic + 1) * 128, dh * HW:(dh + 1) * HW]),
                            sem="db0" if sB == 0 else "db1")
                        s.need("t", dt_)
                        if icl == 0:
                            s.need("t", hdn_ready)
                            s.need("t", pa_free)
                        ocs = [(i * 512, min(512, HW - i * 512))
                               for i in range((HW + 511) // 512)]
                        for tt4 in range(NTT):
                            for oc2, (oo, ol) in enumerate(ocs):
                                pt, po = pr(tt4, oo)
                                tok = s.op("t", lambda e, icl=icl, tt4=tt4, sB=sB,
                                           oo=oo, ol=ol, pt=pt, po=po: e.matmul(
                                               pt[:, po:po + ol],
                                               HDN[:, icl * TQ + tt4 * 128:
                                                   icl * TQ + (tt4 + 1) * 128],
                                               WB[:, sB + oo:sB + oo + ol],
                                               start=(icl == 0),
                                               stop=(icl == FGRP - 1),
                                               skip_group_check=True),
                                           inc=("bt" if (tt4 == NTT - 1 and
                                                         oc2 == len(ocs) - 1)
                                                else None))
                        wbslot_free[("dn", sB)] = tok
                    s.need("v", tok)
                    for tt4 in range(NTT):
                        for oo, ol in ocs:
                            pt, po = pr(tt4, oo)
                            pa_free = s.op(
                                "v", lambda e, tt4=tt4, dh=dh, pt=pt, po=po, oo=oo,
                                ol=ol: e.scalar_tensor_tensor(
                                    out=XP[:, tt4 * D + dh * HW + oo:
                                           tt4 * D + dh * HW + oo + ol],
                                    in0=pt[:, po:po + ol], scalar=1.0,
                                    in1=XP[:, tt4 * D + dh * HW + oo:
                                           tt4 * D + dh * HW + oo + ol],
                                    op0=ALU.mult, op1=ALU.add), inc="bv")
                            s.selfedge("v")
                    if "d" in HB:
                        s.barrier()
                pbg_free = pa_free
                pbu_free = pa_free
                wbslot_free[0] = wbslot_free.get(("dn", 0), wbslot_free.get(0))
                wbslot_free[2048] = wbslot_free.get(("dn", 2048),
                                                    wbslot_free.get(2048))
            if q_i == DBG_Q:
                tap("dbg_xnT", AR[:, o_xnT:o_xnT + KT * TQ])
                tap("dbg_xnmT", AR[:, o_xnmT:o_xnmT + KT * TQ])
                tap("dbg_hdn", HDN[:, :])
                tap("dbg_xpxn", XP[:, 0:D])
            if b_sub == 53:
                s.barrier()
                continue
            s.wait("q", "bv")
            for tt in range(NTT):
                s.dma("q", lambda e, tt=tt: e.dma_start(
                    out=sc_xpxn[tt * 128:(tt + 1) * 128, :],
                    in_=XP[:, tt * D:(tt + 1) * D]))
            s.barrier()

            if not do_c:
                continue
            blocks = ([("xbc", CT + i) for i in range(OCH)] +
                      [("dt", CT + OCH)])
            wa_free = {}
            pb_free = {0: None, 1: None}
            pc_free = None
            pa_gate_free = None
            gate_store_tok = {}
            pbreg = 0
            for bi_, (sect, oc) in enumerate(blocks):
                sA = (bi_ % 2) * 2048
                s.need("q", wa_free.get(sA))
                wt = s.dma("q", lambda e, oc=oc, sA=sA: e.dma_start(
                    out=WA[:, sA:sA + KT * 128],
                    in_=sc_in.rearrange("(n p m) -> n p m", p=128, m=KT * 128)[oc]),
                    sem="da0" if sA == 0 else "da1")
                s.need("t", wt)
                if q_i == DBG_Q and bi_ == 0:
                    s.need("q", wt)
                    tap("dbg_win", WA[:, sA:sA + KT * 128])
                if sect == "xbc":
                    m = oc - CT
                    reg = pbreg
                    pbreg ^= 1
                    s.need("t", pb_free[reg])
                    for kt in range(KT):
                        tok = s.op("t", lambda e, kt=kt, sA=sA, reg=reg: e.matmul(
                            PB[:, reg * 512:reg * 512 + TQ],
                            WA[:, sA + kt * 128:sA + (kt + 1) * 128],
                            AR[:, o_xnmT + kt * TQ:o_xnmT + (kt + 1) * TQ],
                            start=(kt == 0), stop=(kt == KT - 1),
                            skip_group_check=True),
                            inc=("bt" if kt == KT - 1 else None))
                    wa_free[sA] = tok
                    s.need("v", tok)
                    if m < CT:
                        ddst = xbca(m * TQ, (m + 1) * TQ)
                    else:
                        ddst = XBCB[:, (m - CT) * TQ:(m - CT + 1) * TQ]
                    pb_free[reg] = s.op("v", lambda e, reg=reg, ddst=ddst: e.tensor_copy(
                        ddst, PB[:, reg * 512:reg * 512 + TQ]), inc="bv")
                elif sect == "dt":
                    s.need("t", pc_free)
                    for kt in range(KT):
                        tok = s.op("t", lambda e, kt=kt, sA=sA: e.matmul(
                            PC[0:H, 0:TQ],
                            WA[:, sA + kt * 128:sA + kt * 128 + H],
                            AR[:, o_xnmT + kt * TQ:o_xnmT + (kt + 1) * TQ],
                            start=(kt == 0), stop=(kt == KT - 1),
                            skip_group_check=True),
                            inc=("bt" if kt == KT - 1 else None))
                    wa_free[sA] = tok
                    s.need("v", tok)
                    pc_free = s.op("v", lambda e: e.tensor_copy(
                        sb["b_dtraw"][:, :], PC[0:H, 0:TQ]), inc="bv")
                else:
                    raise AssertionError("unused branch")
            KTL = min(4, KT)
            ld_i = len(blocks)
            gblk = 128 * KT * 512
            for gg in range(G2 // 512):
                gtoks = []
                for ktq in range(KT // KTL):
                    sA = (ld_i % 2) * 2048
                    ld_i += 1
                    s.need("q", wa_free.get(sA))
                    wt = s.dma("q", lambda e, gg=gg, ktq=ktq, sA=sA: e.dma_start(
                        out=WA[:, sA:sA + KTL * 512],
                        in_=sc_in[gg * gblk:(gg + 1) * gblk]
                        .rearrange("(p m) -> p m", p=128)
                        [:, ktq * KTL * 512:(ktq + 1) * KTL * 512]),
                        sem="da0" if sA == 0 else "da1")
                    s.need("t", wt)
                    if ktq == 0:
                        s.need("t", pa_gate_free)
                    for kt4 in range(KTL):
                        kt = ktq * KTL + kt4
                        for tt in range(NTT):
                            tok = s.op("t", lambda e, kt=kt, kt4=kt4, tt=tt, sA=sA:
                                       e.matmul(
                                           PA[:, tt * 512:(tt + 1) * 512],
                                           AR[:, o_xnmT + kt * TQ + tt * 128:
                                              o_xnmT + kt * TQ + (tt + 1) * 128],
                                           WA[:, sA + kt4 * 512:sA + (kt4 + 1) * 512],
                                           start=(kt == 0), stop=(kt == KT - 1),
                                           skip_group_check=True),
                                       inc=("bt" if (kt4 == KTL - 1 and tt == NTT - 1)
                                            else None))
                    wa_free[sA] = tok
                s.need("s", tok)
                for tt in range(NTT):
                    s.need("s", gate_store_tok.get(tt))
                evs = []
                for tt in range(NTT):
                    evs.append(s.op("s", lambda e, tt=tt: e.activation(
                        out=GSL[:, tt * 512:(tt + 1) * 512],
                        in_=PA[:, tt * 512:(tt + 1) * 512],
                        func=AF.Silu), inc="bs"))
                pa_gate_free = evs[-1]
                s.need("q", evs[-1])
                g0 = gg * 512
                for tt in range(NTT):
                    gate_store_tok[tt] = s.dma(
                        "q", lambda e, tt=tt, g0=g0: e.dma_start(
                            out=sc_gate[tt * 128:(tt + 1) * 128, g0:g0 + 512],
                            in_=GSL[:, tt * 512:(tt + 1) * 512]),
                        sem="dg%d" % tt)
            if q_i == DBG_Q:
                tap("dbg_dtraw", sb["b_dtraw"][:, :])
            s.barrier()

            if not do_d:
                continue
            wb_free = {}
            pb_free = {0: None, 1: None}
            pbreg = 0
            for m in range(OCH):
                sA = (m % 2) * 1024
                s.need("q", wb_free.get(sA))
                wt = s.dma("q", lambda e, m=m, sA=sA: e.dma_start(
                    out=WB[:, sA:sA + KCV * 128],
                    in_=sc_cv[:, m * KCV * 128:(m + 1) * KCV * 128]),
                    sem="db0" if sA == 0 else "db1")
                if m < CT:
                    srcmain = lambda lo, hi, m=m: xbca(m * TQ + lo, m * TQ + hi)
                else:
                    srcmain = lambda lo, hi, mb=m - CT: XBCB[:, mb * TQ + lo:mb * TQ + hi]
                hist = XBCB[:, HIST0 + m * (KCV - 1):HIST0 + (m + 1) * (KCV - 1)]
                reg = pbreg
                pbreg ^= 1
                s.need("t", wt)
                s.need("t", pb_free[reg])
                first = True
                for k in range(KCV):
                    nh = (KCV - 1) - k
                    if nh > 0:
                        s.op("t", lambda e, k=k, nh=nh, sA=sA, reg=reg, hist=hist,
                             first=first: e.matmul(
                                 PB[:, reg * 512:reg * 512 + nh],
                                 WB[:, sA + k * 128:sA + (k + 1) * 128],
                                 hist[:, k:k + nh],
                                 start=first, stop=False, skip_group_check=True))
                        first = False
                    tok = s.op("t", lambda e, k=k, nh=nh, sA=sA, reg=reg,
                               srcmain=srcmain, first=first: e.matmul(
                                   PB[:, reg * 512 + nh:reg * 512 + TQ],
                                   WB[:, sA + k * 128:sA + (k + 1) * 128],
                                   srcmain(0, TQ - nh),
                                   start=first, stop=(k == KCV - 1),
                                   skip_group_check=True),
                               inc=("bt" if k == KCV - 1 else None))
                    first = False
                wb_free[sA] = tok
                s.need("v", tok)
                hv = s.op("v", lambda e, hist=hist, srcmain=srcmain: e.tensor_copy(
                    hist[:, :], srcmain(TQ - (KCV - 1), TQ)), inc="bv")
                s.need("s", hv)
                pb_free[reg] = s.op("s", lambda e, m=m, reg=reg, srcmain=srcmain:
                                    e.activation(
                                        out=srcmain(0, TQ),
                                        in_=PB[:, reg * 512:reg * 512 + TQ],
                                        func=AF.Silu,
                                        bias=sb["c_convb"][:, m:m + 1]), inc="bs")
            if q_i == DBG_Q:
                tap("dbg_xbc0", xbca(0, TQ))
                tap("dbg_Bc", XBCB[:, 0:TQ])
            s.barrier()

            if not do_d0:
                continue
            dtb = sb["c_dtb"][:, 0:1]
            B_ = {k: sb[k] for k in ["b_dtraw", "b_dt", "b_cum", "b_t1", "b_dtA",
                                     "b_dtAn", "b_lndt", "b_wdt", "b_chi", "b_clo"]}
            s.op("s", lambda e: e.activation(out=B_["b_t1"][:, :], in_=B_["b_dtraw"][:, :],
                                             func=AF.Abs, bias=dtb))
            s.op("s", lambda e: e.activation(out=B_["b_t1"][:, :], in_=B_["b_t1"][:, :],
                                             func=AF.Exp, scale=-1.0))
            s.op("s", lambda e: e.activation(out=B_["b_t1"][:, :], in_=B_["b_t1"][:, :],
                                             func=AF.Ln, bias=1.0))
            s.op("s", lambda e: e.activation(out=B_["b_dt"][:, :], in_=B_["b_dtraw"][:, :],
                                             func=AF.Relu, bias=dtb), inc="bs")
            s.wait("v", "bs")
            s.op("v", lambda e: e.tensor_add(B_["b_dt"][:, :], B_["b_dt"][:, :],
                                             B_["b_t1"][:, :]))
            s.selfedge("v")
            s.op("v", lambda e: e.tensor_scalar_mul(B_["b_dtA"][:, :], B_["b_dt"][:, :],
                                                    sb["c_negA"][:, 0:1]))
            s.selfedge("v")
            s.op("v", lambda e: e.tensor_scalar_mul(B_["b_dtAn"][:, :], B_["b_dtA"][:, :],
                                                    -1.0))
            for ch in range(NCHQ):
                s.op("v", lambda e, ch=ch: e.tensor_tensor_scan(
                    out=B_["b_cum"][:, ch * Q:(ch + 1) * Q],
                    data0=B_["b_dtA"][:, ch * Q:(ch + 1) * Q],
                    data1=sb["b_z"][:, :], initial=0.0,
                    op0=ALU.add, op1=ALU.add))
            s.selfedge("v")
            s.op("v", lambda e: e.tensor_copy(B_["b_chi"][:, :], B_["b_cum"][:, :]))
            for ch in range(NCHQ):
                s.op("v", lambda e, ch=ch: e.tensor_copy(
                    sb["b_cendh"][:, ch:ch + 1],
                    B_["b_cum"][:, (ch + 1) * Q - 1:(ch + 1) * Q]))
            s.selfedge("v")
            s.op("v", lambda e: e.tensor_sub(B_["b_clo"][:, :], B_["b_cum"][:, :],
                                             B_["b_chi"][:, :]))
            for ch in range(NCHQ):
                s.op("v", lambda e, ch=ch: e.tensor_sub(
                    sb["b_cscr"][:, ch:ch + 1],
                    B_["b_cum"][:, (ch + 1) * Q - 1:(ch + 1) * Q],
                    sb["b_cendh"][:, ch:ch + 1]))
            s.selfedge("v")
            s.op("v", lambda e: e.tensor_copy(sb["b_cendl"][:, :],
                                              sb["b_cscr"][:, 0:NCHQ]))
            for ch in range(NCHQ):
                s.op("v", lambda e, ch=ch: e.tensor_scalar_mul(
                    sb["b_cscr"][:, NCHQ + ch:NCHQ + ch + 1],
                    B_["b_cum"][:, (ch + 1) * Q - 1:(ch + 1) * Q], 1.0))
            vb = s.op("v", lambda e: e.tensor_copy(
                sb["b_cscr"][:, 0:NCHQ], sb["b_cscr"][:, 0:NCHQ]), inc="bv")
            s.wait("s", "bv")
            for ch in range(NCHQ):
                s.op("s", lambda e, ch=ch: e.activation(
                    out=B_["b_wdt"][:, ch * Q:(ch + 1) * Q],
                    in_=B_["b_cum"][:, ch * Q:(ch + 1) * Q], func=AF.Exp, scale=-1.0,
                    bias=sb["b_cscr"][:, NCHQ + ch:NCHQ + ch + 1]))
            s.op("s", lambda e: e.activation(out=B_["b_t1"][:, :], in_=B_["b_dt"][:, :],
                                             func=AF.Ln), inc="bs")
            s.wait("v", "bs")
            s.op("v", lambda e: e.tensor_scalar_max(B_["b_lndt"][:, :], B_["b_t1"][:, :],
                                                    LNDT_CLAMP))
            s.op("v", lambda e: e.tensor_mul(B_["b_wdt"][:, :], B_["b_wdt"][:, :],
                                             B_["b_dt"][:, :]), inc="bv")
            s.selfedge("v")
            s.wait("q", "bv")
            for hl, buf in ((0, B_["b_chi"]), (1, B_["b_clo"])):
                for ch in range(NCHQ):
                    s.dma("q", lambda e, hl=hl, buf=buf, ch=ch: e.dma_start(
                        out=sc_cumhl[hl:hl + 1, ch * H * Q:(ch + 1) * H * Q]
                        .rearrange("a (h i) -> (a h) i", h=H),
                        in_=buf[:, ch * Q:(ch + 1) * Q]))
            for hl, buf in ((0, sb["b_cendh"]), (1, sb["b_cendl"])):
                s.dma("q", lambda e, hl=hl, buf=buf: e.dma_start(
                    out=bass.AP(sc_cend.tensor, hl * NCHQ * H, [[1, H], [H, NCHQ]]),
                    in_=buf[:, :], allow_slow_non_contiguous=True))
            s.wait("q", "d")
            s.dma("q", lambda e: e.dma_start(out=sb["b_cendhl"][:, :], in_=sc_cend[:, :]))
            if q_i == DBG_Q:
                tap("dbg_dt", sb["b_dt"][:, :])
                tap("dbg_cum", sb["b_cum"][:, :])
            s.barrier()

            if not do_e:
                continue
            gsl_dma_done = None
            y_done = None
            prev_lam = None
            prev_pbh = {0: None, 1: None}
            gsl_slot_free = {}
            prev_epi_v = None
            cumhl_slot_free = {0: None, 1: None}
            agi_ctr = 0
            for ch in range(NCHQ):
                prev_ydone = y_done
                c0 = ch * Q
                s.need("q", prev_epi_v)
                if gsl_dma_done is not None:
                    for k in range(8):
                        if s.cnt["dg%d" % k]:
                            s.wait("q", "dg%d" % k)
                gtok = s.dma("q", lambda e, c0=c0: e.dma_start(
                    out=GSL[:, 0:G2], in_=sc_gate[c0:c0 + 128, :]), sem="dgt")
                s.need("t", prev_pbh[0])
                btokv = None
                for i_, nm in enumerate(("b_dtAn", "b_lndt", "b_wdt")):
                    s.need("t", btokv)
                    tk = s.op("t", lambda e, nm=nm, c0=c0: e.transpose(
                        pbh(0)[:, 0:H], sb[nm][:, c0:c0 + Q],
                        identity=sb["c_I"][0:H, 0:H]), inc="bt")
                    s.need("v", tk)
                    btokv = s.op("v", lambda e, i_=i_: e.tensor_copy(
                        sb["b_tok"][:, i_ * H:(i_ + 1) * H], pbh(0)[:, 0:H]), inc="bv")
                s.need("t", prev_lam)
                ctok = s.op("t", lambda e, c0=c0: e.transpose(
                    PD[:, 0:H], sb["b_cum"][:, c0:c0 + Q],
                    identity=sb["c_If"][0:H, 0:H]), inc="bt")
                s.need("s", ctok)
                s.need("s", prev_ydone)
                etok = s.op("s", lambda e: e.activation(
                    out=sb["b_Etok"][:, :], in_=PD[:, 0:H], func=AF.Exp), inc="bs")
                s.need("t", etok)
                ltok = s.op("t", lambda e, ch=ch: e.matmul(
                    PD[:, 128:128 + H], sb["c_ones2"][:, :],
                    sb["b_cendhl"][:, ch * H:(ch + 1) * H],
                    start=True, stop=True, skip_group_check=True), inc="bt")
                s.need("s", ltok)
                lamtok = s.op("s", lambda e: e.activation(
                    out=sb["b_Lam"][:, :], in_=PD[:, 128:128 + H], func=AF.Exp),
                    inc="bs")
                prev_lam = lamtok
                tp_half_free = {0: btokv, 1: prev_pbh[1]}
                for m in range(CT):
                    half = m % 2
                    s.need("t", tp_half_free[half])
                    ttok = s.op("t", lambda e, m=m, half=half, c0=c0: e.transpose(
                        pbh(half)[:, 0:128], xbca(m * TQ + c0, m * TQ + c0 + 128),
                        identity=IDb), inc="bt")
                    s.need("v", ttok)
                    tp_half_free[half] = s.op(
                        "v", lambda e, m=m, half=half: e.tensor_copy(
                            AR[:, o_xsT + m * 128:o_xsT + (m + 1) * 128],
                            pbh(half)[:, 0:128]), inc="bv")
                half = CT % 2
                s.need("t", tp_half_free[half])
                ttok = s.op("t", lambda e, c0=c0, half=half: e.transpose(
                    pbh(half)[:, 0:128], XBCB[:, c0:c0 + 128], identity=IDb), inc="bt")
                s.need("v", ttok)
                btv = s.op("v", lambda e, half=half: e.tensor_copy(
                    sb["b_BT"][:, :], pbh(half)[:, 0:128]), inc="bv")
                tp_half_free[half] = btv
                xsT_ready = ("bv", s.cnt["bv"])
                s.need("p", xsT_ready)
                wxtok = s.op("p", lambda e: e.tensor_tensor(
                    AR[:, o_wx:o_wx + G2].rearrange("p (h f) -> p h f", h=H),
                    AR[:, o_xsT:o_xsT + G2].rearrange("p (h f) -> p h f", h=H),
                    sb["b_tok"][:, 2 * H:3 * H, None].broadcast_to([128, H, PH]),
                    op=ALU.mult), inc="bp")
                if q_i == DBG_Q and ch == 0:
                    tap("dbg_tok", sb["b_tok"][:, :])
                    tap("dbg_Etok", sb["b_Etok"][:, :])
                    tap("dbg_Lam", sb["b_Lam"][:, :])
                    tap("dbg_xsT", AR[:, o_xsT:o_xsT + G2])
                    s.wait("q", "bp")
                    tap("dbg_WX", AR[:, o_wx:o_wx + G2])
                gmt = s.op("t", lambda e, c0=c0: e.matmul(
                    PC[:, 0:128], XBCB[:, c0:c0 + 128],
                    XBCB[:, TQ + c0:TQ + c0 + 128],
                    start=True, stop=True, skip_group_check=True), inc="bt")
                s.need("v", gmt)
                gmv = s.op("v", lambda e: e.tensor_mul(
                    sb["b_Gm"][:, :], PC[:, 0:128], sb["c_tri"][:, :]), inc="bv")

                if "h" in HB:
                    s.barrier()
                exp_done = None
                gml_done = None
                for ag in range(NAG):
                    h0 = ag * AGH
                    cslot = agi_ctr % 2
                    agi_ctr += 1
                    s.need("q", cumhl_slot_free[cslot])
                    chl = s.dma("q", lambda e, ch=ch, h0=h0, cslot=cslot: e.dma_start(
                        out=sb["b_cumhl"][:, cslot * AGH * Q:(cslot + 1) * AGH * Q],
                        in_=sc_cumhl[:, (ch * H + h0) * Q:(ch * H + h0 + AGH) * Q]),
                        sem="dch%d" % cslot)
                    s.need("t", exp_done)
                    s.need("t", chl)
                    for hh in range(AGH):
                        s.op("t", lambda e, hh=hh, cslot=cslot: e.matmul(
                            PA[:, hh * 128:(hh + 1) * 128], sb["c_ones2"][:, :],
                            sb["b_cumhl"][:, (cslot * AGH + hh) * Q:
                                          (cslot * AGH + hh + 1) * Q],
                            start=(hh % 4 == 0), stop=False, skip_group_check=True))
                    for hh in range(AGH):
                        h = h0 + hh
                        s.op("t", lambda e, h=h, hh=hh: e.matmul(
                            PA[:, hh * 128:(hh + 1) * 128], sb["c_tri"][:, :],
                            sb["b_tok"][:, None, h:h + 1].broadcast_to([128, 128, 1]),
                            start=False, stop=False, skip_group_check=True))
                    for hh in range(AGH):
                        h = h0 + hh
                        s.op("t", lambda e, h=h, hh=hh: e.matmul(
                            PA[:, hh * 128:(hh + 1) * 128], sb["c_I"][:, :],
                            sb["b_tok"][:, None, H + h:H + h + 1]
                            .broadcast_to([128, 128, 1]),
                            start=False, stop=False, skip_group_check=True))
                    nmi = AGH * 128 // 512
                    for mi in range(nmi):
                        tok = s.op("t", lambda e, mi=mi: e.matmul(
                            PA[:, mi * 512:(mi + 1) * 512], sb["c_I"][:, :],
                            sb["c_up"][:, None, :].broadcast_to([128, AGH, 128])
                            [:, mi * 4:(mi + 1) * 4, :],
                            start=False, stop=True, skip_group_check=True),
                            inc=("bt" if mi == nmi - 1 else None))
                    cumhl_slot_free[cslot] = tok
                    s.need("s", tok)
                    exp_done = s.op("s", lambda e: e.activation(
                        out=AR[:, o_expl:o_expl + AGH * 128], in_=PA[:, 0:AGH * 128],
                        func=AF.Exp), inc="bs")
                    s.need("v", exp_done)
                    s.need("v", gmv)
                    gml_done = s.op("v", lambda e, h0=h0: e.tensor_tensor(
                        out=AR[:, o_gml + h0 * 128:o_gml + (h0 + AGH) * 128]
                        .rearrange("p (h f) -> p h f", h=AGH),
                        in0=AR[:, o_expl:o_expl + AGH * 128]
                        .rearrange("p (h f) -> p h f", h=AGH),
                        in1=sb["b_Gm"][:, None, :].broadcast_to([128, AGH, 128]),
                        op=ALU.mult), inc="bv")
                    exp_done = gml_done
                if "g" in HB:
                    s.barrier()
                for yg in range(NYG):
                    h0 = yg * YGH
                    s.need("t", y_done)
                    s.need("t", gml_done)
                    itok = s.op("t", lambda e, yg=yg, c0=c0: e.matmul(
                        PB[:, 0:512], XBCB[:, TQ + c0:TQ + c0 + 128],
                        SBb[:, yg * YGH * PH:(yg + 1) * YGH * PH],
                        start=True, stop=True, skip_group_check=True), inc="bt")
                    for hh in range(YGH):
                        h = h0 + hh
                        s.op("t", lambda e, h=h, hh=hh: e.matmul(
                            PB[:, 512 + hh * PH:512 + (hh + 1) * PH],
                            AR[:, o_gml + h * 128:o_gml + (h + 1) * 128],
                            AR[:, o_xsT + h * PH:o_xsT + (h + 1) * PH],
                            start=(hh == 0), stop=False, skip_group_check=True))
                    nmt = YGH * PH // 128
                    for mm in range(nmt):
                        mt = yg * nmt + mm
                        tok = s.op("t", lambda e, mt=mt, mm=mm, c0=c0: e.matmul(
                            PB[:, 512 + mm * 128:512 + (mm + 1) * 128],
                            xbca(mt * TQ + c0, mt * TQ + c0 + 128),
                            sb["c_Dd"][:, mt * 128:(mt + 1) * 128],
                            start=False, stop=(mm == nmt - 1),
                            skip_group_check=True),
                            inc=("bt" if mm == nmt - 1 else None))
                    s.need("v", tok)
                    s.op("v", lambda e, yg=yg, h0=h0: e.tensor_tensor(
                        out=XY[:, yg * YGH * PH:(yg + 1) * YGH * PH]
                        .rearrange("p (h f) -> p h f", h=YGH),
                        in0=PB[:, 0:YGH * PH].rearrange("p (h f) -> p h f", h=YGH),
                        in1=sb["b_Etok"][:, h0:h0 + YGH, None]
                        .broadcast_to([128, YGH, PH]), op=ALU.mult))
                    s.selfedge("v")
                    yv = s.op("v", lambda e, yg=yg: e.scalar_tensor_tensor(
                        out=XY[:, yg * YGH * PH:(yg + 1) * YGH * PH],
                        in0=PB[:, 512:512 + YGH * PH], scalar=1.0,
                        in1=XY[:, yg * YGH * PH:(yg + 1) * YGH * PH],
                        op0=ALU.mult, op1=ALU.add), inc="bv")
                    y_done = yv
                    s.need("t", wxtok)
                    s.need("t", yv)
                    dstok = s.op("t", lambda e, yg=yg: e.matmul(
                        PC[:, 0:512], sb["b_BT"][:, :],
                        AR[:, o_wx + yg * YGH * PH:o_wx + (yg + 1) * YGH * PH],
                        start=True, stop=True, skip_group_check=True), inc="bt")
                    s.need("v", dstok)
                    s.op("v", lambda e, yg=yg, h0=h0: e.tensor_tensor(
                        out=sb["b_Sd"][:, :].rearrange("p (h f) -> p h f", h=YGH),
                        in0=ST[:, yg * YGH * PH:(yg + 1) * YGH * PH]
                        .rearrange("p (h f) -> p h f", h=YGH),
                        in1=sb["b_Lam"][:, h0:h0 + YGH, None]
                        .broadcast_to([128, YGH, PH]), op=ALU.mult))
                    s.selfedge("v")
                    s.op("v", lambda e, yg=yg: e.scalar_tensor_tensor(
                        out=ST[:, yg * YGH * PH:(yg + 1) * YGH * PH],
                        in0=PC[:, 0:512], scalar=1.0, in1=sb["b_Sd"][:, :],
                        op0=ALU.mult, op1=ALU.add))
                    s.selfedge("v")
                    y_done = s.op("v", lambda e, yg=yg: e.tensor_copy(
                        SBb[:, yg * YGH * PH:(yg + 1) * YGH * PH],
                        ST[:, yg * YGH * PH:(yg + 1) * YGH * PH]), inc="bv")
                if q_i == DBG_Q and ch == 0:
                    tap("dbg_Gm", sb["b_Gm"][:, :])
                    tap("dbg_gml", AR[:, o_gml:o_gml + H * 128])
                    tap("dbg_y", XY[:, :])
                    tap("dbg_S", ST[:, :])
                    tap("dbg_gate", GSL[:, 0:G2])
                s.need("v", gtok)
                ygv = s.op("v", lambda e: e.tensor_mul(
                    AR[:, o_yg:o_yg + G2], XY[:, 0:G2], GSL[:, 0:G2]), inc="bv")
                s.need("s", ygv)
                sqv = s.op("s", lambda e: e.activation(
                    out=AR[:, o_wx:o_wx + G2], in_=AR[:, o_yg:o_yg + G2],
                    func=AF.Square, accum_out=st[:, 14:15]), inc="bs")
                s.selfedge("s")
                s.op("s", lambda e: e.activation(out=st[:, 15:16], in_=st[:, 14:15],
                                                 func=AF.Ln, scale=1.0 / G2,
                                                 bias=float(EPS)))
                rstok = s.op("s", lambda e: e.activation(
                    out=st[:, 15:16], in_=st[:, 15:16], func=AF.Exp, scale=-0.5),
                    inc="bs")
                s.need("v", rstok)
                s.selfedge("v")
                ygn = s.op("v", lambda e: e.tensor_scalar_mul(
                    AR[:, o_yg:o_yg + G2], AR[:, o_yg:o_yg + G2],
                    st[:, 15:16]), inc="bv")
                s.selfedge("v")
                if q_i == DBG_Q and ch == 0:
                    tap("dbg_ygn", AR[:, o_yg:o_yg + G2])
                s.need("t", ygn)
                tp_half_free = {0: None, 1: None}
                for m in range(OKT):
                    half = m % 2
                    gslot = (m % 8) * 128
                    s.need("t", tp_half_free[half])
                    ttok = s.op("t", lambda e, m=m, half=half: e.transpose(
                        pbh(half)[:, 0:128], AR[:, o_yg + m * 128:o_yg + (m + 1) * 128],
                        identity=IDb), inc="bt")
                    s.need("v", ttok)
                    s.need("v", gsl_slot_free.get(gslot))
                    ev = s.op("v", lambda e, half=half, gslot=gslot: e.tensor_copy(
                        GSL[:, gslot:gslot + 128], pbh(half)[:, 0:128]), inc="bv")
                    tp_half_free[half] = ev
                    s.need("q", ev)
                    gsl_slot_free[gslot] = s.dma(
                        "q", lambda e, m=m, c0=c0, gslot=gslot: e.dma_start(
                            out=sc_ygnT[m, :, c0:c0 + 128],
                            in_=GSL[:, gslot:gslot + 128]),
                        sem="dg%d" % (gslot // 128))
                gsl_dma_done = ("d", s.cnt["d"])
                prev_epi_v = ("bv", s.cnt["bv"])
                prev_pbh = dict(tp_half_free)
                if "e" in HB:
                    s.barrier()
            s.barrier()

            if not do_f:
                continue
            wb_free = {}
            wy_free = {}
            HW = D // 2
            GSLf = GSL[:, :].bitcast(F32)
            WAf = WA[:, :].bitcast(F32)
            pr = _pr4(PA, PB, PC, PD, NTT)
            stg = [(GSLf, 0), (GSLf, HW), (WAf, 0), (WAf, HW)]
            ocs = [(i_ * 512, min(512, HW - i_ * 512))
                   for i_ in range((HW + 511) // 512)]
            out_done = [None] * NTT
            fv_prev_psum = []
            prev_fv = [None] * NTT
            for dh in range(2):
                for tt4 in range(NTT):
                    s.need("q", prev_fv[tt4])
                    xr = s.dma("q", lambda e, tt4=tt4, dh=dh: e.dma_start(
                        out=XY[:, tt4 * HW:(tt4 + 1) * HW],
                        in_=sc_xpxn[tt4 * 128:(tt4 + 1) * 128,
                                    dh * HW:(dh + 1) * HW]),
                        sem="dx%d" % tt4)
                    prev_fv[tt4] = xr
                xr_toks = list(prev_fv)
                for kt in range(OKT):
                    sA = (kt % 2) * 2048
                    sy = (kt % 2) * TQ
                    s.need("q", wb_free.get(sA))
                    wt = s.dma("q", lambda e, kt=kt, sA=sA, dh=dh: e.dma_start(
                        out=WB[:, sA:sA + HW],
                        in_=sc_o[kt * 128:(kt + 1) * 128, dh * HW:(dh + 1) * HW]),
                        sem="db0" if sA == 0 else "db1")
                    s.need("q", wy_free.get(sy))
                    yt = s.dma("q", lambda e, kt=kt, sy=sy: e.dma_start(
                        out=WY[:, sy:sy + TQ], in_=sc_ygnT[kt, :, :]),
                        sem="dy0" if sy == 0 else "dy1")
                    s.need("t", wt)
                    s.need("t", yt)
                    if kt == 0:
                        for f_ in fv_prev_psum:
                            s.need("t", f_)
                    for tt4 in range(NTT):
                        for oc2, (oo, ol) in enumerate(ocs):
                            pt, po = pr(tt4, oo)
                            tok = s.op("t", lambda e, kt=kt, sA=sA, sy=sy, tt4=tt4,
                                       oo=oo, ol=ol, pt=pt, po=po: e.matmul(
                                           pt[:, po:po + ol],
                                           WY[:, sy + tt4 * 128:sy + (tt4 + 1) * 128],
                                           WB[:, sA + oo:sA + oo + ol],
                                           start=(kt == 0), stop=(kt == OKT - 1),
                                           skip_group_check=True),
                                       inc=("bt" if (tt4 == NTT - 1 and
                                                     oc2 == len(ocs) - 1)
                                            else None))
                    wb_free[sA] = tok
                    wy_free[sy] = tok
                    if "f" in HB:
                        s.barrier()
                s.need("v", tok)
                for x_ in xr_toks:
                    s.need("v", x_)
                for o_ in out_done:
                    s.need("v", o_)
                fv_prev_psum = []
                for tt4 in range(NTT):
                    sg_, so_ = stg[tt4] if NTT == 4 else (
                        [(GSLf, 0), (WAf, 0)][tt4])
                    for oo, ol in ocs:
                        pt, po = pr(tt4, oo)
                        fv = s.op("v", lambda e, tt4=tt4, pt=pt, po=po, sg_=sg_,
                                  so_=so_, oo=oo, ol=ol: e.scalar_tensor_tensor(
                                      out=sg_[:, so_ + oo:so_ + oo + ol],
                                      in0=pt[:, po:po + ol], scalar=1.0,
                                      in1=XY[:, tt4 * HW + oo:tt4 * HW + oo + ol],
                                      op0=ALU.mult, op1=ALU.add), inc="bv")
                        s.selfedge("v")
                    prev_fv[tt4] = fv
                    fv_prev_psum.append(fv)
                    s.need("q", fv)
                    out_done[tt4] = s.dma(
                        "q", lambda e, tt4=tt4, dh=dh, T0=T0, sg_=sg_, so_=so_:
                        e.dma_start(
                            out=out_t[T0 + tt4 * 128:T0 + (tt4 + 1) * 128,
                                      dh * HW:(dh + 1) * HW],
                            in_=sg_[:, so_:so_ + HW]), sem="do%d" % tt4)
            s.barrier()

        if dump:
            s.barrier()
            dmp = {
                "dmp_gate": sc_gate, "dmp_xpxn": sc_xpxn,
                "dmp_cumhl": sc_cumhl, "dmp_cend": sc_cend,
            }
            dmp_sb = {
                "dmp_XBCB": XBCB, "dmp_XPb": None, "dmp_AR": AR,
                "dmp_cum": sb["b_cum"], "dmp_dt": sb["b_dt"],
                "dmp_dtraw": sb["b_dtraw"], "dmp_Etok": sb["b_Etok"],
                "dmp_Lam": sb["b_Lam"], "dmp_tok": sb["b_tok"],
                "dmp_ST": ST, "dmp_Gm": sb["b_Gm"], "dmp_XY": XY,
                "dmp_cumhl_sb": sb["b_cumhl"], "dmp_cendhl": sb["b_cendhl"],
                "dmp_cscr": sb["b_cscr"],
            }
            for nm, t_ in dmp.items():
                o_ = nc.dram_tensor(nm, list(t_.tensor.shape),
                                    t_.tensor.dtype, kind="ExternalOutput").ap()
                s.dma("q", lambda e, o_=o_, t_=t_: e.dma_start(out=o_, in_=t_))
            for nm, t_ in dmp_sb.items():
                if t_ is None:
                    t_ = XP[:, :].bitcast(BF16)
                else:
                    t_ = t_[:, :]
                o_ = nc.dram_tensor(nm, [t_.shape[0], t_.shape[1]],
                                    t_.dtype, kind="ExternalOutput").ap()
                s.dma("q", lambda e, o_=o_, t_=t_: e.dma_start(out=o_, in_=t_))
        for nm_ in s.q_dma_sems():
            s.wait("q", nm_)
        global _LAST_SCH
        _LAST_SCH = s
        s.simulate()
        s.emit(nc, sems)
    return nc


def kernel(L0_attn_hidden_states, L0_attn_norm_weight, L0_attn_in_proj_w,
           L0_attn_conv_w, L0_attn_conv_b, L0_attn_dt_bias, L0_attn_A_log,
           L0_attn_D, L0_attn_gate_norm_weight, L0_attn_out_proj_w,
           L0_ffn_gate_w, L0_ffn_up_w, L0_ffn_down_w, L0_ffn_gate_s,
           L0_ffn_up_s, L0_ffn_down_s):
    c = derive(full_cfg())
    w = dict(in_proj_w=np.asarray(L0_attn_in_proj_w, np.float32),
             conv_w=np.asarray(L0_attn_conv_w, np.float32),
             conv_b=np.asarray(L0_attn_conv_b, np.float32),
             dt_bias=np.asarray(L0_attn_dt_bias, np.float32),
             A_log=np.asarray(L0_attn_A_log, np.float32),
             D=np.asarray(L0_attn_D, np.float32),
             gate_norm_weight=np.asarray(L0_attn_gate_norm_weight, np.float32),
             out_proj_w=np.asarray(L0_attn_out_proj_w, np.float32),
             norm_weight=np.asarray(L0_attn_norm_weight, np.float32),
             gate_w=np.asarray(L0_ffn_gate_w, np.float32),
             up_w=np.asarray(L0_ffn_up_w, np.float32),
             down_w=np.asarray(L0_ffn_down_w, np.float32),
             gate_s=np.asarray(L0_ffn_gate_s, np.float32),
             up_s=np.asarray(L0_ffn_up_s, np.float32),
             down_s=np.asarray(L0_ffn_down_s, np.float32))
    consts = host_arrays(c, w)
    x = np.asarray(L0_attn_hidden_states, np.float32)
    nc = build(c)
    in_maps = []
    for b in range(8):
        m = dict(consts)
        m["x"] = np.ascontiguousarray(x[b])
        in_maps.append(m)
    res = run_bass_kernel_spmd(nc, in_maps, core_ids=list(range(8)))
    out = np.stack([np.asarray(res.results[b]["out"]) for b in range(8)], axis=0)
    return out.astype(np.float32)


# version 06 (final)
# speedup vs baseline: 1.2419x; 1.0175x over previous
import contextlib
from collections import defaultdict

import numpy as np
import concourse.bass as bass
import concourse.mybir as mybir
from concourse.bass_utils import run_bass_kernel_spmd

F32 = mybir.dt.float32
BF16 = mybir.dt.bfloat16
AF = mybir.ActivationFunctionType
ALU = mybir.AluOpType

PH = 64
NST = 128
KCV = 4
Q = 128
EPS = 1e-5
LNDT_CLAMP = -30.0
ENGS = ("t", "v", "s", "p", "q")
_LAST_SCH = None
DBG_Q = 0
HB = ""
ZINC = False
DPSPLIT = True


def K_HBX():
    return HB
DBG_PH = 'ABCDE0'


def full_cfg():
    return dict(D=2048, T=2048, H=64, FI=5632)


def derive(cfg):
    c = dict(cfg)
    D, T, H, FI = c["D"], c["T"], c["H"], c["FI"]
    c["G2"] = H * PH
    assert c["G2"] == 2 * D
    c["CONVD"] = c["G2"] + 2 * NST
    c["PROJ"] = c["G2"] + c["CONVD"] + H
    c["PROJP"] = c["G2"] + c["CONVD"] + 128
    c["KT"] = D // 128
    c["CT"] = c["G2"] // 128
    c["OCH"] = c["CONVD"] // 128
    c["NBLK"] = c["PROJP"] // 128
    c["ICH"] = FI // 128
    c["TQ"] = T // 4
    c["NTT"] = c["TQ"] // 128
    c["NCHQ"] = c["TQ"] // Q
    c["AGH"] = min(8, H)
    c["YGH"] = min(8, H)
    c["OKT"] = c["G2"] // 128
    fg = min(22, c["ICH"])
    while c["ICH"] % fg:
        fg -= 1
    c["FGRP"] = fg
    assert c["TQ"] <= 512 and c["CT"] % 4 == 0 and c["D"] % 128 == 0
    return c


def host_arrays(c, w):
    D, H, FI = c["D"], c["H"], c["FI"]
    CONVD, PROJ, PROJP = c["CONVD"], c["PROJ"], c["PROJP"]
    KT, CT, OCH, NBLK, ICH, OKT = (c["KT"], c["CT"], c["OCH"], c["NBLK"],
                                   c["ICH"], c["OKT"])
    o = {}
    o["c_tri"] = (np.arange(128)[:, None] <= np.arange(128)[None, :]).astype(np.float32)
    o["c_I"] = np.eye(128, dtype=np.float32)
    o["c_up"] = np.where(np.arange(128)[:, None] > np.arange(128)[None, :],
                         np.float32(-1.0e6), np.float32(0.0))
    o["c_ones2"] = np.ones((2, 128), np.float32)
    wp = np.zeros((PROJP, D), np.float32)
    wp[:PROJ] = w["in_proj_w"]
    G2 = 2 * D
    gate_part = np.ascontiguousarray(
        wp[:G2].reshape(G2 // 512, 512, KT, 128).transpose(0, 3, 2, 1))
    rest_part = np.ascontiguousarray(
        wp[G2:].reshape(NBLK - G2 // 128, 128, KT, 128).transpose(0, 3, 2, 1))
    o["w_in"] = np.concatenate([gate_part.reshape(-1), rest_part.reshape(-1)])
    for nm, src in (("w_g", w["gate_w"]), ("w_u", w["up_w"])):
        o[nm] = np.ascontiguousarray(
            src.reshape(ICH, 128, KT, 128).transpose(0, 3, 2, 1)).reshape(-1)
    o["w_d"] = np.ascontiguousarray(w["down_w"])
    o["w_out"] = np.ascontiguousarray(w["out_proj_w"].T)
    cw = w["conv_w"].reshape(CONVD, KCV)
    cd = np.zeros((128, OCH * KCV * 128), np.float32)
    for m in range(OCH):
        for k in range(KCV):
            blk = (m * KCV + k) * 128
            cd[np.arange(128), blk + np.arange(128)] = cw[m * 128:(m + 1) * 128, k]
    o["c_convd"] = cd
    dd = np.zeros((128, CT * 128), np.float32)
    for m in range(CT):
        ch = (m * 128 + np.arange(128)) // PH
        dd[np.arange(128), m * 128 + np.arange(128)] = w["D"][ch]
    o["c_Dd"] = dd
    o["c_convb"] = np.ascontiguousarray(w["conv_b"].reshape(OCH, 128).T)
    o["c_nw"] = np.ascontiguousarray(w["norm_weight"].reshape(KT, 128).T)
    o["c_dtb"] = w["dt_bias"].reshape(H, 1).astype(np.float32)
    o["c_alog"] = w["A_log"].reshape(H, 1).astype(np.float32)
    o["c_gs"] = np.ascontiguousarray(w["gate_s"].reshape(ICH, 128).T)
    o["c_us"] = np.ascontiguousarray(w["up_s"].reshape(ICH, 128).T)
    o["c_ds"] = np.ascontiguousarray(w["down_s"].reshape(KT, 128).T)
    o["c_gnw"] = np.ascontiguousarray(w["gate_norm_weight"].reshape(OKT, 128).T)
    return o


class Sch:
    def __init__(self):
        self.ops = {e: [] for e in ENGS}
        self.cnt = defaultdict(int)
        self.bcnt = {e: 0 for e in ENGS}

    def op(self, e, fn, inc=None, n=1):
        if inc is None and ZINC and e in ("t", "v", "s", "p"):
            inc, n = self.OWN[e], 1
        incs = []
        if inc is not None:
            self.cnt[inc] += n
            incs.append((inc, n))
        self.ops[e].append(("op", fn, incs))
        return (inc, self.cnt[inc]) if inc is not None else None

    def dma(self, e, fn, sem=None):
        if sem is None:
            sem = "d" if (e == "q" or not DPSPLIT) else "dp"
        return self.op(e, fn, inc=sem, n=16)

    def wait(self, e, sem, val=None):
        v = self.cnt[sem] if val is None else val
        self.ops[e].append(("wait", sem, v))

    def selfedge(self, e):
        for item in reversed(self.ops[e]):
            if item[0] == "op":
                if not item[2]:
                    own = self.OWN[e]
                    self.cnt[own] += 1
                    item[2].append((own, 1))
                nm, n = item[2][0]
                self.ops[e].append(("wait", nm, self.cnt[nm]))
                return
            if item[0] == "wait":
                continue
            return

    def need(self, e, tok):
        if tok is not None:
            self.ops[e].append(("wait", tok[0], tok[1]))

    OWN = {"t": "bt", "v": "bv", "s": "bs", "p": "bp", "q": "d"}
    P_DMA = ("dp", "dp0", "dp1")

    def q_dma_sems(self):
        return [k for k in self.cnt
                if k.startswith("d") and k not in self.P_DMA and self.cnt[k]]

    def barrier(self, engines=ENGS):
        for e in engines:
            self.bcnt[e] += 1
            if e == "q":
                owns = self.q_dma_sems()
            elif e == "p":
                owns = ["bp", "dp", "dp0", "dp1"] if DPSPLIT else ["bp", "d"]
            else:
                owns = [self.OWN[e]]
            for own in owns:
                if self.cnt[own]:
                    self.ops[e].append(("wait", own, self.cnt[own]))
            self.ops[e].append(("barinc", e))
        for e in engines:
            for oth in engines:
                if oth != e:
                    self.ops[e].append(("wait", "B" + oth, self.bcnt[oth]))

    def simulate(self):
        pos = {e: 0 for e in ENGS}
        sem = defaultdict(int)
        progress = True
        while progress:
            progress = False
            for e in ENGS:
                while pos[e] < len(self.ops[e]):
                    item = self.ops[e][pos[e]]
                    if item[0] == "op":
                        for nm_, n_ in item[2]:
                            sem[nm_] += n_
                        pos[e] += 1
                        progress = True
                    elif item[0] == "wait":
                        if sem[item[1]] >= item[2]:
                            pos[e] += 1
                            progress = True
                        else:
                            break
                    else:
                        sem["B" + item[1]] += 1
                        pos[e] += 1
                        progress = True
        stuck = {e: pos[e] for e in ENGS if pos[e] < len(self.ops[e])}
        if stuck:
            msg = []
            for e, p in stuck.items():
                it = self.ops[e][p]
                msg.append(f"{e}@{p}/{len(self.ops[e])}: wait {it[1]} >= {it[2]} "
                           f"(sim has {dict(sem).get(it[1], 0)})")
            raise RuntimeError("DEADLOCK in schedule:\n" + "\n".join(msg))

    def emit(self, nc, sems):
        finals = dict(self.cnt)
        for e in ENGS:
            finals["B" + e] = self.bcnt[e]
        for e in ENGS:
            for item in self.ops[e]:
                if item[0] == "wait":
                    assert item[2] <= finals.get(item[1], 0), (
                        f"unreachable wait {item[1]} >= {item[2]} (final "
                        f"{finals.get(item[1], 0)}) on engine {e}")

        def run(eng, stream):
            for item in self.ops[stream]:
                if item[0] == "op":
                    _, fn, incs = item
                    inst = fn(eng)
                    for nm, n in incs:
                        inst = inst.then_inc(sems[nm], n)
                elif item[0] == "wait":
                    eng.wait_ge(sems[item[1]], item[2])
                else:
                    eng.nop().then_inc(sems["B" + item[1]], 1)

        with nc.Block() as block:
            @block.tensor
            def _(e):
                run(e, "t")

            @block.vector
            def _(e):
                run(e, "v")

            @block.scalar
            def _(e):
                run(e, "s")

            @block.gpsimd
            def _(e):
                run(e, "p")

            @block.sync
            def _(e):
                run(e, "q")



def _pr4(PA, PB, PC, PD, NTT):
    def pr(tt4, oo):
        if NTT != 4:
            return ([(PA, 0), (PB, 0)][tt4][0], oo)
        if tt4 == 0:
            return PA, oo
        if tt4 == 1:
            return PA, 1024 + oo
        if tt4 == 2:
            return PB, oo
        return (PC, oo) if oo < 512 else (PD, oo - 512)
    return pr

def build(c, limit=99, dbg=False, dump=False):
    nc = bass.Bass("TRN2", target_bir_lowering=False, debug=False)
    D, T, H, FI = c["D"], c["T"], c["H"], c["FI"]
    G2, CONVD = c["G2"], c["CONVD"]
    KT, CT, OCH, NBLK, ICH = c["KT"], c["CT"], c["OCH"], c["NBLK"], c["ICH"]
    TQ, NTT, NCHQ = c["TQ"], c["NTT"], c["NCHQ"]
    AGH, YGH, OKT, FGRP = c["AGH"], c["YGH"], c["OKT"], c["FGRP"]
    NYG, NAG, NFG = H // YGH, H // AGH, ICH // FGRP

    ap = {}
    for nm, shp in [
        ("x", (T, D)), ("w_in", (NBLK * KT * 128 * 128,)),
        ("w_g", (D * FI,)), ("w_u", (D * FI,)),
        ("w_d", (D, FI)), ("w_out", (G2, D)),
        ("c_convd", (128, OCH * KCV * 128)), ("c_Dd", (128, CT * 128)),
        ("c_convb", (128, OCH)), ("c_nw", (128, KT)), ("c_dtb", (H, 1)),
        ("c_alog", (H, 1)), ("c_gs", (128, ICH)), ("c_us", (128, ICH)),
        ("c_ds", (128, KT)), ("c_gnw", (128, OKT)), ("c_tri", (128, 128)),
        ("c_I", (128, 128)), ("c_up", (128, 128)), ("c_ones2", (2, 128)),
    ]:
        ap[nm] = nc.dram_tensor(nm, list(shp), F32, kind="ExternalInput").ap()
    out_t = nc.dram_tensor("out", [T, D], F32, kind="ExternalOutput").ap()
    dbg_t = {}
    if dbg:
        for nm, shp, dt_ in [
            ("dbg_xn", (128, D), BF16), ("dbg_xnm", (128, D), BF16),
            ("dbg_xpxn", (128, D), F32), ("dbg_hdn", (128, FGRP * TQ), BF16),
            ("dbg_gate", (128, G2), BF16), ("dbg_dtraw", (H, TQ), F32),
            ("dbg_dt", (H, TQ), F32), ("dbg_cum", (H, TQ), F32),
            ("dbg_xbc0", (128, TQ), BF16), ("dbg_Bc", (128, TQ), BF16),
            ("dbg_tok", (128, 3 * H), BF16), ("dbg_Gm", (128, 128), BF16),
            ("dbg_gml", (128, H * 128), BF16), ("dbg_Etok", (128, H), F32),
            ("dbg_Lam", (128, H), F32), ("dbg_y", (128, 2 * D), F32),
            ("dbg_ygn", (128, G2), BF16), ("dbg_S", (128, G2), F32),
            ("dbg_xsT", (128, G2), BF16), ("dbg_WX", (128, G2), BF16),
            ("dbg_stat", (128, 16), F32), ("dbg_x", (128, D), F32),
            ("dbg_xnT", (128, KT * TQ), BF16), ("dbg_xnmT", (128, KT * TQ), BF16),
            ("dbg_wd", (128, D), BF16), ("dbg_win", (128, KT * 128), BF16),
            ("dbg_fx", (128, D), F32),
        ]:
            dbg_t[nm] = nc.dram_tensor(nm, list(shp), dt_, kind="ExternalOutput").ap()

    def dsc(nm, shp, dt=BF16):
        return nc.dram_tensor(nm, list(shp), dt, kind="Internal").ap()

    sc_in = dsc("sc_in", (NBLK * KT * 128 * 128,))
    sc_g = dsc("sc_g", (D * FI,))
    sc_u = dsc("sc_u", (D * FI,))
    sc_d = dsc("sc_d", (FI, D))
    sc_o = dsc("sc_o", (G2, D))
    sc_cv = dsc("sc_cv", (128, OCH * KCV * 128))
    sc_gate = dsc("sc_gate", (TQ, G2))
    sc_xpxn = dsc("sc_xpxn", (TQ, D), F32)
    sc_ygnT = dsc("sc_ygnT", (OKT, 128, TQ))
    sc_cumhl = dsc("sc_cumhl", (2, NCHQ * H * Q))
    sc_cend = dsc("sc_cend", (2, NCHQ * H))

    o_xnT, o_xnmT = 0, KT * TQ
    o_xsT, o_wx, o_gml = 0, G2, 2 * G2
    o_expl = o_gml + H * 128
    o_yg = o_expl + 2 * AGH * 128
    AR_COLS = max(2 * KT * TQ, o_yg + G2)

    es = contextlib.ExitStack()
    sb = {}

    def sbuf(name, shape, dt=BF16):
        sb[name] = es.enter_context(nc.sbuf_tensor("sb_" + name, list(shape), dt))

    with es:
        for nm in ["c_tri", "c_I", "c_up"]:
            sbuf(nm, (128, 128))
        sbuf("c_If", (128, 128), F32)
        sbuf("c_ones2", (2, 128))
        sbuf("c_Dd", (128, CT * 128))
        sbuf("c_convb", (128, OCH), F32)
        sbuf("c_nw", (128, KT), F32)
        sbuf("c_dtb", (H, 1), F32)
        sbuf("c_negA", (H, 1), F32)
        sbuf("c_gs", (128, ICH), F32)
        sbuf("c_us", (128, ICH), F32)
        sbuf("c_ds", (128, KT), F32)
        sbuf("c_gnw", (128, OKT), F32)
        sbuf("S", (128, G2), F32)
        sbuf("Sb", (128, G2))
        sbuf("AR", (128, AR_COLS))
        sbuf("XP", (128, NTT * D), F32)
        sbuf("XBCB", (128, 2 * TQ + OCH * (KCV - 1)))
        sbuf("XY", (128, 2 * D), F32)
        sbuf("HDN", (128, FGRP * TQ))
        sbuf("GSL", (128, max(G2, NTT * 512)))
        sbuf("WA", (128, 2 * 2048))
        sbuf("WB", (128, 2 * 2048))
        sbuf("WY", (128, 2 * TQ))
        for nm in ["b_dtraw", "b_dt", "b_cum", "b_t1"]:
            sbuf(nm, (H, TQ), F32)
        for nm in ["b_dtA", "b_dtAn", "b_lndt", "b_wdt", "b_chi", "b_clo"]:
            sbuf(nm, (H, TQ))
        sbuf("b_cendh", (H, NCHQ))
        sbuf("b_cendl", (H, NCHQ))
        sbuf("b_z", (H, Q), F32)
        sbuf("b_cumhl", (2, 2 * AGH * Q))
        sbuf("b_cendhl", (2, NCHQ * H))
        sbuf("b_tok", (128, 3 * H))
        sbuf("b_Etok", (128, H), F32)
        sbuf("b_Lam", (128, H), F32)
        sbuf("b_Gm", (128, 128))
        sbuf("b_BT", (128, NST))
        sbuf("b_Sd", (128, YGH * PH), F32)
        sbuf("b_stat", (128, 16), F32)
        sbuf("b_cscr", (H, 2 * NCHQ), F32)
        sbuf("c_bias", (128, 2), F32)
        nc.const_aps.register(F32, float(EPS), sb["c_bias"][:, 0:1])
        nc.const_aps.register(F32, 1.0, sb["c_bias"][:, 1:2])

        PA = es.enter_context(nc.psum_tensor("PA", [128, 2048], F32))
        PB = es.enter_context(nc.psum_tensor("PB", [128, 1024], F32))
        PC = es.enter_context(nc.psum_tensor("PC", [128, 512], F32))
        PD = es.enter_context(nc.psum_tensor("PD", [128, 512], F32))

        sems = {}
        for nm in (["d", "dc", "dp", "dp0", "dp1", "dw0", "dw1", "da0", "da1",
                    "db0", "db1", "dy0", "dy1", "dx0", "dx1", "dx2", "dx3",
                    "dgt", "do0", "do1", "do2", "do3"]
                   + ["dch0", "dch1"]
                   + ["dg%d" % k for k in range(8)]
                   + ["b" + e for e in ENGS] + ["B" + e for e in ENGS]):
            sems[nm] = es.enter_context(nc.semaphore(nm))

        s = Sch()

        PHOF = {"dbg_stat": "A", "dbg_x": "A", "dbg_xn": "A", "dbg_xnm": "A",
                "dbg_xnT": "B", "dbg_xnmT": "B", "dbg_hdn": "B", "dbg_xpxn": "B",
                "dbg_wd": "B", "dbg_win": "C", "dbg_dtraw": "C", "dbg_gate": "E",
                "dbg_xbc0": "D", "dbg_Bc": "D", "dbg_dt": "0", "dbg_cum": "0",
                "dbg_tok": "E", "dbg_Etok": "E", "dbg_Lam": "E", "dbg_xsT": "E",
                "dbg_WX": "E", "dbg_Gm": "E", "dbg_gml": "E", "dbg_y": "E",
                "dbg_S": "E", "dbg_ygn": "E", "dbg_fx": "F"}

        def tap(nm, src_ap):
            if dbg and nm in dbg_t and PHOF.get(nm, "?") in DBG_PH:
                s.wait("q", "bv")
                s.wait("q", "bs")
                s.wait("q", "bt")
                s.dma("q", lambda e, nm=nm, src_ap=src_ap: e.dma_start(
                    out=dbg_t[nm], in_=src_ap))

        AR, XP, XY, XBCB = sb["AR"], sb["XP"], sb["XY"], sb["XBCB"]
        HDN, GSL, WA, WB, WY = sb["HDN"], sb["GSL"], sb["WA"], sb["WB"], sb["WY"]
        ST, SBb = sb["S"], sb["Sb"]
        HIST0 = 2 * TQ
        IDb = sb["c_I"][:, :]

        def xbca(lo, hi):
            return XP[:, :].bitcast(BF16)[:, lo:hi]

        def pbh(i):
            return PB[:, i * 512:(i + 1) * 512].bitcast(BF16)

        for nm in ["c_tri", "c_I", "c_up", "c_ones2", "c_Dd"]:
            s.dma("p", lambda e, d=sb[nm], sr=ap[nm]: e.dma_start(out=d[:, :], in_=sr))
        s.dma("q", lambda e: e.dma_start(out=sb["c_If"][:, :], in_=ap["c_I"]),
              sem="dc")
        for nm in ["c_convb", "c_nw", "c_dtb", "c_gs", "c_us", "c_ds", "c_gnw"]:
            s.dma("q", lambda e, d=sb[nm], sr=ap[nm]: e.dma_start(out=d[:, :], in_=sr),
                  sem="dc")
        s.dma("q", lambda e: e.dma_start(out=sb["c_negA"][:, :], in_=ap["c_alog"]),
              sem="dc")
        s.wait("s", "dc")
        s.wait("s", "dp")
        s.op("s", lambda e: e.activation(out=sb["c_negA"][:, :], in_=sb["c_negA"][:, :],
                                         func=AF.Exp), inc="bs")
        s.wait("v", "bs")
        s.op("v", lambda e: e.tensor_scalar_mul(sb["c_negA"][:, :], sb["c_negA"][:, :], -1.0))
        s.op("v", lambda e: e.memset(sb["b_z"][:, :], 0.0))
        s.op("v", lambda e: e.memset(sb["c_bias"][:, 0:1], float(EPS)))
        s.op("v", lambda e: e.memset(sb["c_bias"][:, 1:2], 1.0))
        s.op("v", lambda e: e.memset(ST[:, :], 0.0))
        s.op("v", lambda e: e.memset(SBb[:, :], 0.0))
        s.op("v", lambda e: e.memset(XBCB[:, HIST0:HIST0 + OCH * (KCV - 1)], 0.0),
             inc="bv")

        CH = 2048
        wb_done = {}

        def plain_cast(src_ap, dst_ap, total_elems, src2d=False, dst2d=False):
            cols = total_elems // 128
            assert cols * 128 == total_elems
            c0_ = 0
            i = 0
            while c0_ < cols:
                w_ = min(CH, cols - c0_)
                slot = (i % 2) * 2048
                s.need("p", wb_done.get(slot))
                if src2d:
                    sview = src_ap[:, c0_:c0_ + w_]
                else:
                    sview = src_ap.rearrange("(p m) -> p m", p=128)[:, c0_:c0_ + w_]
                tok = s.dma("p", lambda e, sview=sview, slot=slot, w_=w_: e.dma_start(
                    out=WA[:, slot:slot + w_], in_=sview),
                    sem="dp0" if slot == 0 else "dp1")
                s.need("q", tok)
                if dst2d:
                    dview = dst_ap[:, c0_:c0_ + w_]
                else:
                    dview = dst_ap.rearrange("(p m) -> p m", p=128)[:, c0_:c0_ + w_]
                wb_done[slot] = s.dma("q", lambda e, dview=dview, slot=slot, w_=w_:
                                      e.dma_start(out=dview, in_=WA[:, slot:slot + w_]),
                                      sem="dw0" if slot == 0 else "dw1")
                c0_ += w_
                i += 1

        if limit >= 1:
            plain_cast(ap["w_in"], sc_in, NBLK * KT * 128 * 128)
        if limit >= 1:
            plain_cast(ap["w_g"], sc_g, D * FI)
            plain_cast(ap["w_u"], sc_u, D * FI)
            plain_cast(ap["c_convd"], sc_cv, 128 * OCH * KCV * 128, src2d=True, dst2d=True)
        for it in range(OKT):
            slot = (it % 2) * 2048
            s.need("p", wb_done.get(slot))
            tok = s.dma("p", lambda e, it=it, slot=slot: e.dma_start(
                out=WA[:, slot:slot + D], in_=ap["w_out"][it * 128:(it + 1) * 128, :]),
                sem="dp0" if slot == 0 else "dp1")
            s.need("v", tok)
            vt = s.op("v", lambda e, it=it, slot=slot: e.tensor_scalar_mul(
                WA[:, slot:slot + D], WA[:, slot:slot + D],
                sb["c_gnw"][:, it:it + 1]), inc="bv")
            s.need("q", vt)
            wb_done[slot] = s.dma("q", lambda e, it=it, slot=slot: e.dma_start(
                out=sc_o[it * 128:(it + 1) * 128, :], in_=WA[:, slot:slot + D]),
                sem="dw0" if slot == 0 else "dw1")
        wb_done[("d", 0)] = wb_done.get(0)
        wb_done[("d", 2048)] = wb_done.get(2048)
        gsl_done = {}
        half_free = {0: None, 1: None}
        for it in range(KT if limit >= 3 else 0):
            nfc = (FI + 2047) // 2048
            for fc in range(nfc):
                w0 = fc * 2048
                wlen = min(2048, FI - w0)
                slot = (fc % 2) * 2048
                s.need("p", wb_done.get(("d", slot)))
                tok = s.dma("p", lambda e, it=it, w0=w0, wlen=wlen, slot=slot: e.dma_start(
                    out=WA[:, slot:slot + wlen],
                    in_=ap["w_d"][it * 128:(it + 1) * 128, w0:w0 + wlen]),
                    sem="dp0" if slot == 0 else "dp1")
                s.need("v", tok)
                vt = s.op("v", lambda e, wlen=wlen, slot=slot, it=it: e.tensor_scalar_mul(
                    WA[:, slot:slot + wlen], WA[:, slot:slot + wlen],
                    sb["c_ds"][:, it:it + 1]), inc="bv")
                wb_done[("d", slot)] = vt
                s.need("t", vt)
                nblk = wlen // 128
                for bi in range(nblk):
                    half = bi % 2
                    gslot = (bi % 8) * 128
                    s.need("t", half_free[half])
                    tt_ = s.op("t", lambda e, slot=slot, bi=bi, half=half: e.transpose(
                        pbh(half)[:, (bi % 2) * 0:128] if False else pbh(half)[:, 0:128],
                        WA[:, slot + bi * 128:slot + (bi + 1) * 128],
                        identity=IDb), inc="bt")
                    s.need("v", tt_)
                    s.need("v", gsl_done.get(gslot))
                    vt2 = s.op("v", lambda e, half=half, gslot=gslot: e.tensor_copy(
                        GSL[:, gslot:gslot + 128], pbh(half)[:, 0:128]), inc="bv")
                    half_free[half] = vt2
                    s.need("q", vt2)
                    gsl_done[gslot] = s.dma("q", lambda e, it=it, w0=w0, bi=bi,
                                            gslot=gslot: e.dma_start(
                        out=sc_d[w0 + bi * 128:w0 + (bi + 1) * 128,
                                 it * 128:(it + 1) * 128],
                        in_=GSL[:, gslot:gslot + 128]), sem="dg%d" % (gslot // 128))
                wb_done[("d", slot)] = ("bt", s.cnt["bt"])
        s.barrier()

        for q_i in range(4 if limit >= 4 else 0):
            T0 = q_i * TQ

            st = sb["b_stat"]
            do_b = limit >= 5
            do_c = limit >= 6 and limit != 55
            do_d = limit >= 7 and limit != 55
            do_d0 = limit >= 8 and limit != 55
            do_e = limit >= 9 and limit != 55
            do_f = limit >= 10 or limit == 55
            tp_half_free = {0: None, 1: None}
            prev_tt_done = {}
            for tt in range(NTT):
                xoff = (tt % 2) * D
                s.need("q", prev_tt_done.get(tt % 2))
                xtok = s.dma("q", lambda e, tt=tt, xoff=xoff, T0=T0: e.dma_start(
                    out=XY[:, xoff:xoff + D],
                    in_=ap["x"][T0 + tt * 128:T0 + (tt + 1) * 128, :]),
                    sem="dx%d" % (tt % 2))
                s.need("v", xtok)
                xsl = XY[:, xoff:xoff + D]
                s.op("v", lambda e, xsl=xsl: e.reduce_sum(
                    st[:, 0:1], xsl, axis=mybir.AxisListType.X), inc="bv")
                s.selfedge("v")
                s.op("v", lambda e: e.tensor_scalar_mul(st[:, 2:3], st[:, 0:1], 1.0 / D))
                s.op("v", lambda e: e.tensor_scalar_mul(st[:, 3:4], st[:, 0:1],
                                                        -1.0 / D), inc="bv")
                s.wait("s", "bv")
                s.wait("s", "bt")
                s.op("s", lambda e, xsl=xsl: e.activation(
                    out=GSL[:, 0:D], in_=xsl, func=AF.Square,
                    bias=st[:, 3:4], accum_out=st[:, 4:5]), inc="bs")
                s.wait("v", "bs")
                s.op("v", lambda e: e.tensor_scalar_mul(st[:, 5:6], st[:, 4:5], 1.0 / D),
                     inc="bv")
                s.wait("s", "bv")
                s.op("s", lambda e: e.activation(out=st[:, 6:7], in_=st[:, 5:6],
                                                 func=AF.Ln, bias=float(EPS)))
                s.op("s", lambda e: e.activation(out=st[:, 7:8], in_=st[:, 6:7],
                                                 func=AF.Exp, scale=-0.5), inc="bs")
                s.wait("v", "bs")
                s.op("v", lambda e: e.tensor_scalar(
                    out=st[:, 8:9], in0=st[:, 5:6], scalar1=st[:, 7:8],
                    scalar2=st[:, 7:8], op0=ALU.mult, op1=ALU.mult), inc="bv")
                s.wait("s", "bv")
                s.op("s", lambda e: e.activation(out=st[:, 9:10], in_=st[:, 8:9],
                                                 func=AF.Ln, bias=float(EPS)))
                s.op("s", lambda e: e.activation(out=st[:, 10:11], in_=st[:, 9:10],
                                                 func=AF.Exp, scale=-0.5), inc="bs")
                s.wait("v", "bs")
                s.op("v", lambda e: e.tensor_scalar(
                    out=st[:, 11:12], in0=st[:, 7:8], scalar1=st[:, 10:11],
                    scalar2=None, op0=ALU.mult))
                s.op("v", lambda e: e.tensor_scalar_add(st[:, 12:13], st[:, 7:8], 1.0))
                s.op("v", lambda e: e.tensor_scalar(
                    out=st[:, 13:14], in0=st[:, 2:3], scalar1=st[:, 7:8],
                    scalar2=None, op0=ALU.mult))
                s.selfedge("v")
                s.wait("v", "bt")
                s.op("v", lambda e, xsl=xsl: e.tensor_scalar(
                    out=GSL[:, 0:D], in0=xsl, scalar1=st[:, 2:3], scalar2=st[:, 7:8],
                    op0=ALU.subtract, op1=ALU.mult))
                s.op("v", lambda e, xsl=xsl: e.tensor_scalar(
                    out=GSL[:, D:2 * D], in0=xsl, scalar1=st[:, 2:3],
                    scalar2=st[:, 11:12], op0=ALU.subtract, op1=ALU.mult))
                vlast = s.op("v", lambda e, xsl=xsl, tt=tt: e.tensor_scalar(
                    out=XP[:, tt * D:(tt + 1) * D], in0=xsl, scalar1=st[:, 12:13],
                    scalar2=st[:, 13:14], op0=ALU.mult, op1=ALU.subtract), inc="bv")
                prev_tt_done[tt % 2] = vlast
                if q_i == DBG_Q and tt == 0:
                    tap("dbg_stat", sb["b_stat"][:, :])
                    tap("dbg_x", XY[:, 0:D])
                    tap("dbg_xn", GSL[:, 0:D])
                    tap("dbg_xnm", GSL[:, D:2 * D])
                    tap("dbg_xpxn", XP[:, 0:D]) if False else None
                if limit == 43:
                    continue
                s.need("t", vlast)
                for blk in range(2 * KT):
                    ht, which = blk // 2, blk % 2
                    half = blk % 2
                    s.need("t", tp_half_free[half])
                    src = GSL[:, which * D + ht * 128: which * D + (ht + 1) * 128]
                    ttok = s.op("t", lambda e, src=src, half=half: e.transpose(
                        pbh(half)[:, 0:128], src, identity=IDb), inc="bt")
                    dst_off = (o_xnT if which == 0 else o_xnmT) + ht * TQ + tt * 128
                    if which == 0:
                        s.need("s", ttok)
                        ev = s.op("s", lambda e, half=half, dst_off=dst_off: e.copy(
                            AR[:, dst_off:dst_off + 128], pbh(half)[:, 0:128]),
                            inc="bs")
                    else:
                        s.need("v", ttok)
                        ev = s.op("v", lambda e, half=half, dst_off=dst_off, ht=ht:
                                  e.tensor_scalar_mul(
                                      AR[:, dst_off:dst_off + 128],
                                      pbh(half)[:, 0:128],
                                      sb["c_nw"][:, ht:ht + 1]), inc="bv")
                    tp_half_free[half] = ev
                if "a" in HB:
                    s.barrier()
            s.barrier()

            wa_free = {}
            if not do_b:
                continue
            b_sub = limit if limit in (51, 52, 53) else (limit - 470 if limit >= 520 else 99)
            wbslot_free = {}
            pbg_free = None
            pbu_free = None
            for fg in range(NFG):
                for icl in range(FGRP):
                    ic = fg * FGRP + icl
                    sA = (ic % 2) * 2048
                    s.need("q", wa_free.get(sA))
                    gt = s.dma("q", lambda e, ic=ic, sA=sA: e.dma_start(
                        out=WA[:, sA:sA + KT * 128],
                        in_=sc_g.rearrange("(n p m) -> n p m", p=128, m=KT * 128)[ic]),
                        sem="da0" if sA == 0 else "da1")
                    s.need("q", wbslot_free.get(sA))
                    ut = s.dma("q", lambda e, ic=ic, sA=sA: e.dma_start(
                        out=WB[:, sA:sA + KT * 128],
                        in_=sc_u.rearrange("(n p m) -> n p m", p=128, m=KT * 128)[ic]),
                        sem="db0" if sA == 0 else "db1")
                    s.need("t", gt)
                    s.need("t", ut)
                    if "y" in K_HBX():
                        s.wait("t", "d")
                    s.need("t", pbg_free)
                    s.need("t", pbu_free)
                    for kt in range(KT):
                        lastg = s.op("t", lambda e, kt=kt, sA=sA: e.matmul(
                            PB[:, 0:TQ],
                            WA[:, sA + kt * 128:sA + (kt + 1) * 128],
                            AR[:, o_xnT + kt * TQ:o_xnT + (kt + 1) * TQ],
                            start=(kt == 0), stop=(kt == KT - 1),
                            skip_group_check=True),
                            inc=("bt" if kt == KT - 1 else None))
                    for kt in range(KT):
                        tok = s.op("t", lambda e, kt=kt, sA=sA: e.matmul(
                            PB[:, 512:512 + TQ],
                            WB[:, sA + kt * 128:sA + (kt + 1) * 128],
                            AR[:, o_xnT + kt * TQ:o_xnT + (kt + 1) * TQ],
                            start=(kt == 0), stop=(kt == KT - 1),
                            skip_group_check=True),
                            inc=("bt" if kt == KT - 1 else None))
                    wa_free[sA] = tok
                    wbslot_free[sA] = tok
                    if b_sub == 51:
                        pbg_free = tok
                        pbu_free = tok
                        continue
                    s.need("s", lastg)
                    if "x" in K_HBX():
                        s.need("s", tok)
                    s.need("s", pbu_free)
                    if b_sub == 61:
                        sg = s.op("s", lambda e: e.copy(
                            GSL[:, 0:TQ], PB[:, 0:TQ]), inc="bs")
                    elif b_sub == 62:
                        sg = s.op("s", lambda e: e.activation(
                            out=GSL[:, 0:TQ], in_=PB[:, 0:TQ], func=AF.Silu), inc="bs")
                    else:
                        sg = s.op("s", lambda e, ic=ic: e.activation(
                            out=GSL[:, 0:TQ], in_=PB[:, 0:TQ], func=AF.Silu,
                            scale=sb["c_gs"][:, ic:ic + 1]), inc="bs")
                    pbg_free = sg
                    s.need("v", sg)
                    s.need("v", tok)
                    if b_sub == 63:
                        hv = s.op("v", lambda e, icl=icl: e.tensor_mul(
                            HDN[:, icl * TQ:(icl + 1) * TQ], PB[:, 512:512 + TQ],
                            GSL[:, 0:TQ]), inc="bv")
                    elif b_sub == 64:
                        hv = s.op("v", lambda e, icl=icl: e.tensor_copy(
                            HDN[:, icl * TQ:(icl + 1) * TQ], PB[:, 512:512 + TQ]),
                            inc="bv")
                    else:
                        hv = s.op("v", lambda e, ic=ic, icl=icl: e.scalar_tensor_tensor(
                            out=HDN[:, icl * TQ:(icl + 1) * TQ], in0=PB[:, 512:512 + TQ],
                            scalar=sb["c_us"][:, ic:ic + 1], in1=GSL[:, 0:TQ],
                            op0=ALU.mult, op1=ALU.mult), inc="bv")
                    pbu_free = hv
                    if "b" in HB:
                        s.barrier()
                if b_sub in (51, 52, 61, 62, 63, 64):
                    continue
                hdn_ready = hv
                wbslot_free[("dn", 0)] = wbslot_free.get(0)
                wbslot_free[("dn", 2048)] = wbslot_free.get(2048)
                pa_free = None
                HW = D // 2
                pr = _pr4(PA, PB, PC, PD, NTT)
                for dh in range(2):
                    for icl in range(FGRP):
                        ic = fg * FGRP + icl
                        sB = (icl % 2) * 2048
                        s.need("q", wbslot_free.get(("dn", sB)))
                        dt_ = s.dma("q", lambda e, ic=ic, sB=sB, dh=dh: e.dma_start(
                            out=WB[:, sB:sB + HW],
                            in_=sc_d[ic * 128:(ic + 1) * 128, dh * HW:(dh + 1) * HW]),
                            sem="db0" if sB == 0 else "db1")
                        s.need("t", dt_)
                        if icl == 0:
                            s.need("t", hdn_ready)
                            s.need("t", pa_free)
                        ocs = [(i * 512, min(512, HW - i * 512))
                               for i in range((HW + 511) // 512)]
                        for tt4 in range(NTT):
                            for oc2, (oo, ol) in enumerate(ocs):
                                pt, po = pr(tt4, oo)
                                tok = s.op("t", lambda e, icl=icl, tt4=tt4, sB=sB,
                                           oo=oo, ol=ol, pt=pt, po=po: e.matmul(
                                               pt[:, po:po + ol],
                                               HDN[:, icl * TQ + tt4 * 128:
                                                   icl * TQ + (tt4 + 1) * 128],
                                               WB[:, sB + oo:sB + oo + ol],
                                               start=(icl == 0),
                                               stop=(icl == FGRP - 1),
                                               skip_group_check=True),
                                           inc=("bt" if (tt4 == NTT - 1 and
                                                         oc2 == len(ocs) - 1)
                                                else None))
                        wbslot_free[("dn", sB)] = tok
                    s.need("v", tok)
                    for tt4 in range(NTT):
                        for oo, ol in ocs:
                            pt, po = pr(tt4, oo)
                            pa_free = s.op(
                                "v", lambda e, tt4=tt4, dh=dh, pt=pt, po=po, oo=oo,
                                ol=ol: e.scalar_tensor_tensor(
                                    out=XP[:, tt4 * D + dh * HW + oo:
                                           tt4 * D + dh * HW + oo + ol],
                                    in0=pt[:, po:po + ol], scalar=1.0,
                                    in1=XP[:, tt4 * D + dh * HW + oo:
                                           tt4 * D + dh * HW + oo + ol],
                                    op0=ALU.mult, op1=ALU.add), inc="bv")
                            s.selfedge("v")
                    if "d" in HB:
                        s.barrier()
                pbg_free = pa_free
                pbu_free = pa_free
                wbslot_free[0] = wbslot_free.get(("dn", 0), wbslot_free.get(0))
                wbslot_free[2048] = wbslot_free.get(("dn", 2048),
                                                    wbslot_free.get(2048))
            if q_i == DBG_Q:
                tap("dbg_xnT", AR[:, o_xnT:o_xnT + KT * TQ])
                tap("dbg_xnmT", AR[:, o_xnmT:o_xnmT + KT * TQ])
                tap("dbg_hdn", HDN[:, :])
                tap("dbg_xpxn", XP[:, 0:D])
            if b_sub == 53:
                s.barrier()
                continue
            s.wait("q", "bv")
            for tt in range(NTT):
                s.dma("q", lambda e, tt=tt: e.dma_start(
                    out=sc_xpxn[tt * 128:(tt + 1) * 128, :],
                    in_=XP[:, tt * D:(tt + 1) * D]))
            s.barrier()

            if not do_c:
                continue
            blocks = ([("xbc", CT + i) for i in range(OCH)] +
                      [("dt", CT + OCH)])
            wa_free = {}
            pb_free = {0: None, 1: None}
            pc_free = None
            pa_gate_free = None
            gate_store_tok = {}
            pbreg = 0
            for bi_, (sect, oc) in enumerate(blocks):
                sA = (bi_ % 2) * 2048
                s.need("q", wa_free.get(sA))
                wt = s.dma("q", lambda e, oc=oc, sA=sA: e.dma_start(
                    out=WA[:, sA:sA + KT * 128],
                    in_=sc_in.rearrange("(n p m) -> n p m", p=128, m=KT * 128)[oc]),
                    sem="da0" if sA == 0 else "da1")
                s.need("t", wt)
                if q_i == DBG_Q and bi_ == 0:
                    s.need("q", wt)
                    tap("dbg_win", WA[:, sA:sA + KT * 128])
                if sect == "xbc":
                    m = oc - CT
                    reg = pbreg
                    pbreg ^= 1
                    s.need("t", pb_free[reg])
                    for kt in range(KT):
                        tok = s.op("t", lambda e, kt=kt, sA=sA, reg=reg: e.matmul(
                            PB[:, reg * 512:reg * 512 + TQ],
                            WA[:, sA + kt * 128:sA + (kt + 1) * 128],
                            AR[:, o_xnmT + kt * TQ:o_xnmT + (kt + 1) * TQ],
                            start=(kt == 0), stop=(kt == KT - 1),
                            skip_group_check=True),
                            inc=("bt" if kt == KT - 1 else None))
                    wa_free[sA] = tok
                    s.need("v", tok)
                    if m < CT:
                        ddst = xbca(m * TQ, (m + 1) * TQ)
                    else:
                        ddst = XBCB[:, (m - CT) * TQ:(m - CT + 1) * TQ]
                    pb_free[reg] = s.op("v", lambda e, reg=reg, ddst=ddst: e.tensor_copy(
                        ddst, PB[:, reg * 512:reg * 512 + TQ]), inc="bv")
                elif sect == "dt":
                    s.need("t", pc_free)
                    for kt in range(KT):
                        tok = s.op("t", lambda e, kt=kt, sA=sA: e.matmul(
                            PC[0:H, 0:TQ],
                            WA[:, sA + kt * 128:sA + kt * 128 + H],
                            AR[:, o_xnmT + kt * TQ:o_xnmT + (kt + 1) * TQ],
                            start=(kt == 0), stop=(kt == KT - 1),
                            skip_group_check=True),
                            inc=("bt" if kt == KT - 1 else None))
                    wa_free[sA] = tok
                    s.need("v", tok)
                    pc_free = s.op("v", lambda e: e.tensor_copy(
                        sb["b_dtraw"][:, :], PC[0:H, 0:TQ]), inc="bv")
                else:
                    raise AssertionError("unused branch")
            KTL = min(4, KT)
            ld_i = len(blocks)
            gblk = 128 * KT * 512
            for gg in range(G2 // 512):
                gtoks = []
                for ktq in range(KT // KTL):
                    sA = (ld_i % 2) * 2048
                    ld_i += 1
                    s.need("q", wa_free.get(sA))
                    wt = s.dma("q", lambda e, gg=gg, ktq=ktq, sA=sA: e.dma_start(
                        out=WA[:, sA:sA + KTL * 512],
                        in_=sc_in[gg * gblk:(gg + 1) * gblk]
                        .rearrange("(p m) -> p m", p=128)
                        [:, ktq * KTL * 512:(ktq + 1) * KTL * 512]),
                        sem="da0" if sA == 0 else "da1")
                    s.need("t", wt)
                    if ktq == 0:
                        s.need("t", pa_gate_free)
                    for kt4 in range(KTL):
                        kt = ktq * KTL + kt4
                        for tt in range(NTT):
                            tok = s.op("t", lambda e, kt=kt, kt4=kt4, tt=tt, sA=sA:
                                       e.matmul(
                                           PA[:, tt * 512:(tt + 1) * 512],
                                           AR[:, o_xnmT + kt * TQ + tt * 128:
                                              o_xnmT + kt * TQ + (tt + 1) * 128],
                                           WA[:, sA + kt4 * 512:sA + (kt4 + 1) * 512],
                                           start=(kt == 0), stop=(kt == KT - 1),
                                           skip_group_check=True),
                                       inc=("bt" if (kt4 == KTL - 1 and tt == NTT - 1)
                                            else None))
                    wa_free[sA] = tok
                s.need("s", tok)
                for tt in range(NTT):
                    s.need("s", gate_store_tok.get(tt))
                evs = []
                for tt in range(NTT):
                    evs.append(s.op("s", lambda e, tt=tt: e.activation(
                        out=GSL[:, tt * 512:(tt + 1) * 512],
                        in_=PA[:, tt * 512:(tt + 1) * 512],
                        func=AF.Silu), inc="bs"))
                pa_gate_free = evs[-1]
                s.need("q", evs[-1])
                g0 = gg * 512
                for tt in range(NTT):
                    gate_store_tok[tt] = s.dma(
                        "q", lambda e, tt=tt, g0=g0: e.dma_start(
                            out=sc_gate[tt * 128:(tt + 1) * 128, g0:g0 + 512],
                            in_=GSL[:, tt * 512:(tt + 1) * 512]),
                        sem="dg%d" % tt)
            if q_i == DBG_Q:
                tap("dbg_dtraw", sb["b_dtraw"][:, :])
            s.barrier()

            if not do_d:
                continue
            wb_free = {}
            pb_free = {0: None, 1: None}
            pbreg = 0
            for m in range(OCH):
                sA = (m % 2) * 1024
                s.need("q", wb_free.get(sA))
                wt = s.dma("q", lambda e, m=m, sA=sA: e.dma_start(
                    out=WB[:, sA:sA + KCV * 128],
                    in_=sc_cv[:, m * KCV * 128:(m + 1) * KCV * 128]),
                    sem="db0" if sA == 0 else "db1")
                if m < CT:
                    srcmain = lambda lo, hi, m=m: xbca(m * TQ + lo, m * TQ + hi)
                else:
                    srcmain = lambda lo, hi, mb=m - CT: XBCB[:, mb * TQ + lo:mb * TQ + hi]
                hist = XBCB[:, HIST0 + m * (KCV - 1):HIST0 + (m + 1) * (KCV - 1)]
                reg = pbreg
                pbreg ^= 1
                s.need("t", wt)
                s.need("t", pb_free[reg])
                first = True
                for k in range(KCV):
                    nh = (KCV - 1) - k
                    if nh > 0:
                        s.op("t", lambda e, k=k, nh=nh, sA=sA, reg=reg, hist=hist,
                             first=first: e.matmul(
                                 PB[:, reg * 512:reg * 512 + nh],
                                 WB[:, sA + k * 128:sA + (k + 1) * 128],
                                 hist[:, k:k + nh],
                                 start=first, stop=False, skip_group_check=True))
                        first = False
                    tok = s.op("t", lambda e, k=k, nh=nh, sA=sA, reg=reg,
                               srcmain=srcmain, first=first: e.matmul(
                                   PB[:, reg * 512 + nh:reg * 512 + TQ],
                                   WB[:, sA + k * 128:sA + (k + 1) * 128],
                                   srcmain(0, TQ - nh),
                                   start=first, stop=(k == KCV - 1),
                                   skip_group_check=True),
                               inc=("bt" if k == KCV - 1 else None))
                    first = False
                wb_free[sA] = tok
                s.need("v", tok)
                hv = s.op("v", lambda e, hist=hist, srcmain=srcmain: e.tensor_copy(
                    hist[:, :], srcmain(TQ - (KCV - 1), TQ)), inc="bv")
                s.need("s", hv)
                pb_free[reg] = s.op("s", lambda e, m=m, reg=reg, srcmain=srcmain:
                                    e.activation(
                                        out=srcmain(0, TQ),
                                        in_=PB[:, reg * 512:reg * 512 + TQ],
                                        func=AF.Silu,
                                        bias=sb["c_convb"][:, m:m + 1]), inc="bs")
            if q_i == DBG_Q:
                tap("dbg_xbc0", xbca(0, TQ))
                tap("dbg_Bc", XBCB[:, 0:TQ])
            s.barrier()

            if not do_d0:
                continue
            dtb = sb["c_dtb"][:, 0:1]
            B_ = {k: sb[k] for k in ["b_dtraw", "b_dt", "b_cum", "b_t1", "b_dtA",
                                     "b_dtAn", "b_lndt", "b_wdt", "b_chi", "b_clo"]}
            s.op("s", lambda e: e.activation(out=B_["b_t1"][:, :], in_=B_["b_dtraw"][:, :],
                                             func=AF.Abs, bias=dtb))
            s.op("s", lambda e: e.activation(out=B_["b_t1"][:, :], in_=B_["b_t1"][:, :],
                                             func=AF.Exp, scale=-1.0))
            s.op("s", lambda e: e.activation(out=B_["b_t1"][:, :], in_=B_["b_t1"][:, :],
                                             func=AF.Ln, bias=1.0))
            s.op("s", lambda e: e.activation(out=B_["b_dt"][:, :], in_=B_["b_dtraw"][:, :],
                                             func=AF.Relu, bias=dtb), inc="bs")
            s.wait("v", "bs")
            s.op("v", lambda e: e.tensor_add(B_["b_dt"][:, :], B_["b_dt"][:, :],
                                             B_["b_t1"][:, :]))
            s.selfedge("v")
            s.op("v", lambda e: e.tensor_scalar_mul(B_["b_dtA"][:, :], B_["b_dt"][:, :],
                                                    sb["c_negA"][:, 0:1]))
            s.selfedge("v")
            s.op("v", lambda e: e.tensor_scalar_mul(B_["b_dtAn"][:, :], B_["b_dtA"][:, :],
                                                    -1.0))
            for ch in range(NCHQ):
                s.op("v", lambda e, ch=ch: e.tensor_tensor_scan(
                    out=B_["b_cum"][:, ch * Q:(ch + 1) * Q],
                    data0=B_["b_dtA"][:, ch * Q:(ch + 1) * Q],
                    data1=sb["b_z"][:, :], initial=0.0,
                    op0=ALU.add, op1=ALU.add))
            s.selfedge("v")
            s.op("v", lambda e: e.tensor_copy(B_["b_chi"][:, :], B_["b_cum"][:, :]))
            for ch in range(NCHQ):
                s.op("v", lambda e, ch=ch: e.tensor_copy(
                    sb["b_cendh"][:, ch:ch + 1],
                    B_["b_cum"][:, (ch + 1) * Q - 1:(ch + 1) * Q]))
            s.selfedge("v")
            s.op("v", lambda e: e.tensor_sub(B_["b_clo"][:, :], B_["b_cum"][:, :],
                                             B_["b_chi"][:, :]))
            for ch in range(NCHQ):
                s.op("v", lambda e, ch=ch: e.tensor_sub(
                    sb["b_cscr"][:, ch:ch + 1],
                    B_["b_cum"][:, (ch + 1) * Q - 1:(ch + 1) * Q],
                    sb["b_cendh"][:, ch:ch + 1]))
            s.selfedge("v")
            s.op("v", lambda e: e.tensor_copy(sb["b_cendl"][:, :],
                                              sb["b_cscr"][:, 0:NCHQ]))
            for ch in range(NCHQ):
                s.op("v", lambda e, ch=ch: e.tensor_scalar_mul(
                    sb["b_cscr"][:, NCHQ + ch:NCHQ + ch + 1],
                    B_["b_cum"][:, (ch + 1) * Q - 1:(ch + 1) * Q], 1.0))
            vb = s.op("v", lambda e: e.tensor_copy(
                sb["b_cscr"][:, 0:NCHQ], sb["b_cscr"][:, 0:NCHQ]), inc="bv")
            s.wait("s", "bv")
            for ch in range(NCHQ):
                s.op("s", lambda e, ch=ch: e.activation(
                    out=B_["b_wdt"][:, ch * Q:(ch + 1) * Q],
                    in_=B_["b_cum"][:, ch * Q:(ch + 1) * Q], func=AF.Exp, scale=-1.0,
                    bias=sb["b_cscr"][:, NCHQ + ch:NCHQ + ch + 1]))
            s.op("s", lambda e: e.activation(out=B_["b_t1"][:, :], in_=B_["b_dt"][:, :],
                                             func=AF.Ln), inc="bs")
            s.wait("v", "bs")
            s.op("v", lambda e: e.tensor_scalar_max(B_["b_lndt"][:, :], B_["b_t1"][:, :],
                                                    LNDT_CLAMP))
            s.op("v", lambda e: e.tensor_mul(B_["b_wdt"][:, :], B_["b_wdt"][:, :],
                                             B_["b_dt"][:, :]), inc="bv")
            s.selfedge("v")
            s.wait("q", "bv")
            for hl, buf in ((0, B_["b_chi"]), (1, B_["b_clo"])):
                for ch in range(NCHQ):
                    s.dma("q", lambda e, hl=hl, buf=buf, ch=ch: e.dma_start(
                        out=sc_cumhl[hl:hl + 1, ch * H * Q:(ch + 1) * H * Q]
                        .rearrange("a (h i) -> (a h) i", h=H),
                        in_=buf[:, ch * Q:(ch + 1) * Q]))
            for hl, buf in ((0, sb["b_cendh"]), (1, sb["b_cendl"])):
                s.dma("q", lambda e, hl=hl, buf=buf: e.dma_start(
                    out=bass.AP(sc_cend.tensor, hl * NCHQ * H, [[1, H], [H, NCHQ]]),
                    in_=buf[:, :], allow_slow_non_contiguous=True))
            s.wait("q", "d")
            s.dma("q", lambda e: e.dma_start(out=sb["b_cendhl"][:, :], in_=sc_cend[:, :]))
            if q_i == DBG_Q:
                tap("dbg_dt", sb["b_dt"][:, :])
                tap("dbg_cum", sb["b_cum"][:, :])
            s.barrier()

            if not do_e:
                continue
            gsl_dma_done = None
            y_done = None
            prev_lam = None
            prev_pbh = {0: None, 1: None}
            gsl_slot_free = {}
            prev_epi_v = None
            cumhl_slot_free = {0: None, 1: None}
            agi_ctr = 0
            for ch in range(NCHQ):
                prev_ydone = y_done
                c0 = ch * Q
                s.need("q", prev_epi_v)
                if gsl_dma_done is not None:
                    for k in range(8):
                        if s.cnt["dg%d" % k]:
                            s.wait("q", "dg%d" % k)
                gtok = s.dma("q", lambda e, c0=c0: e.dma_start(
                    out=GSL[:, 0:G2], in_=sc_gate[c0:c0 + 128, :]), sem="dgt")
                s.need("t", prev_pbh[0])
                btokv = None
                for i_, nm in enumerate(("b_dtAn", "b_lndt", "b_wdt")):
                    s.need("t", btokv)
                    tk = s.op("t", lambda e, nm=nm, c0=c0: e.transpose(
                        pbh(0)[:, 0:H], sb[nm][:, c0:c0 + Q],
                        identity=sb["c_I"][0:H, 0:H]), inc="bt")
                    s.need("v", tk)
                    btokv = s.op("v", lambda e, i_=i_: e.tensor_copy(
                        sb["b_tok"][:, i_ * H:(i_ + 1) * H], pbh(0)[:, 0:H]), inc="bv")
                s.need("t", prev_lam)
                ctok = s.op("t", lambda e, c0=c0: e.transpose(
                    PD[:, 0:H], sb["b_cum"][:, c0:c0 + Q],
                    identity=sb["c_If"][0:H, 0:H]), inc="bt")
                s.need("s", ctok)
                s.need("s", prev_ydone)
                etok = s.op("s", lambda e: e.activation(
                    out=sb["b_Etok"][:, :], in_=PD[:, 0:H], func=AF.Exp), inc="bs")
                s.need("t", etok)
                ltok = s.op("t", lambda e, ch=ch: e.matmul(
                    PD[:, 128:128 + H], sb["c_ones2"][:, :],
                    sb["b_cendhl"][:, ch * H:(ch + 1) * H],
                    start=True, stop=True, skip_group_check=True), inc="bt")
                s.need("s", ltok)
                lamtok = s.op("s", lambda e: e.activation(
                    out=sb["b_Lam"][:, :], in_=PD[:, 128:128 + H], func=AF.Exp),
                    inc="bs")
                prev_lam = lamtok
                tp_half_free = {0: btokv, 1: prev_pbh[1]}
                for m in range(CT):
                    half = m % 2
                    s.need("t", tp_half_free[half])
                    ttok = s.op("t", lambda e, m=m, half=half, c0=c0: e.transpose(
                        pbh(half)[:, 0:128], xbca(m * TQ + c0, m * TQ + c0 + 128),
                        identity=IDb), inc="bt")
                    s.need("v", ttok)
                    tp_half_free[half] = s.op(
                        "v", lambda e, m=m, half=half: e.tensor_copy(
                            AR[:, o_xsT + m * 128:o_xsT + (m + 1) * 128],
                            pbh(half)[:, 0:128]), inc="bv")
                half = CT % 2
                s.need("t", tp_half_free[half])
                ttok = s.op("t", lambda e, c0=c0, half=half: e.transpose(
                    pbh(half)[:, 0:128], XBCB[:, c0:c0 + 128], identity=IDb), inc="bt")
                s.need("v", ttok)
                btv = s.op("v", lambda e, half=half: e.tensor_copy(
                    sb["b_BT"][:, :], pbh(half)[:, 0:128]), inc="bv")
                tp_half_free[half] = btv
                xsT_ready = ("bv", s.cnt["bv"])
                s.need("p", xsT_ready)
                wxtok = s.op("p", lambda e: e.tensor_tensor(
                    AR[:, o_wx:o_wx + G2].rearrange("p (h f) -> p h f", h=H),
                    AR[:, o_xsT:o_xsT + G2].rearrange("p (h f) -> p h f", h=H),
                    sb["b_tok"][:, 2 * H:3 * H, None].broadcast_to([128, H, PH]),
                    op=ALU.mult), inc="bp")
                if q_i == DBG_Q and ch == 0:
                    tap("dbg_tok", sb["b_tok"][:, :])
                    tap("dbg_Etok", sb["b_Etok"][:, :])
                    tap("dbg_Lam", sb["b_Lam"][:, :])
                    tap("dbg_xsT", AR[:, o_xsT:o_xsT + G2])
                    s.wait("q", "bp")
                    tap("dbg_WX", AR[:, o_wx:o_wx + G2])
                gmt = s.op("t", lambda e, c0=c0: e.matmul(
                    PC[:, 0:128], XBCB[:, c0:c0 + 128],
                    XBCB[:, TQ + c0:TQ + c0 + 128],
                    start=True, stop=True, skip_group_check=True), inc="bt")
                s.need("v", gmt)
                gmv = s.op("v", lambda e: e.tensor_mul(
                    sb["b_Gm"][:, :], PC[:, 0:128], sb["c_tri"][:, :]), inc="bv")

                if "h" in HB:
                    s.barrier()
                pa_half_free = {0: None, 1: None}
                expl_half_free = {0: None, 1: None}
                gml_toks = []
                for ag in range(NAG):
                    h0 = ag * AGH
                    par = ag % 2
                    pab = par * AGH * 128
                    eoff = o_expl + par * AGH * 128
                    cslot = agi_ctr % 2
                    agi_ctr += 1
                    s.need("q", cumhl_slot_free[cslot])
                    chl = s.dma("q", lambda e, ch=ch, h0=h0, cslot=cslot: e.dma_start(
                        out=sb["b_cumhl"][:, cslot * AGH * Q:(cslot + 1) * AGH * Q],
                        in_=sc_cumhl[:, (ch * H + h0) * Q:(ch * H + h0 + AGH) * Q]),
                        sem="dch%d" % cslot)
                    s.need("t", pa_half_free[par])
                    s.need("t", chl)
                    for hh in range(AGH):
                        s.op("t", lambda e, hh=hh, cslot=cslot, pab=pab: e.matmul(
                            PA[:, pab + hh * 128:pab + (hh + 1) * 128],
                            sb["c_ones2"][:, :],
                            sb["b_cumhl"][:, (cslot * AGH + hh) * Q:
                                          (cslot * AGH + hh + 1) * Q],
                            start=(hh % 4 == 0), stop=False, skip_group_check=True))
                    for hh in range(AGH):
                        h = h0 + hh
                        s.op("t", lambda e, h=h, hh=hh, pab=pab: e.matmul(
                            PA[:, pab + hh * 128:pab + (hh + 1) * 128],
                            sb["c_tri"][:, :],
                            sb["b_tok"][:, None, h:h + 1].broadcast_to([128, 128, 1]),
                            start=False, stop=False, skip_group_check=True))
                    for hh in range(AGH):
                        h = h0 + hh
                        s.op("t", lambda e, h=h, hh=hh, pab=pab: e.matmul(
                            PA[:, pab + hh * 128:pab + (hh + 1) * 128],
                            sb["c_I"][:, :],
                            sb["b_tok"][:, None, H + h:H + h + 1]
                            .broadcast_to([128, 128, 1]),
                            start=False, stop=False, skip_group_check=True))
                    nmi = AGH * 128 // 512
                    for mi in range(nmi):
                        tok = s.op("t", lambda e, mi=mi, pab=pab: e.matmul(
                            PA[:, pab + mi * 512:pab + (mi + 1) * 512],
                            sb["c_I"][:, :],
                            sb["c_up"][:, None, :].broadcast_to([128, AGH, 128])
                            [:, mi * 4:(mi + 1) * 4, :],
                            start=False, stop=True, skip_group_check=True),
                            inc=("bt" if mi == nmi - 1 else None))
                    cumhl_slot_free[cslot] = tok
                    s.need("s", tok)
                    s.need("s", expl_half_free[par])
                    exp_tok = s.op("s", lambda e, pab=pab, eoff=eoff: e.activation(
                        out=AR[:, eoff:eoff + AGH * 128],
                        in_=PA[:, pab:pab + AGH * 128],
                        func=AF.Exp), inc="bs")
                    pa_half_free[par] = exp_tok
                    s.need("v", exp_tok)
                    s.need("v", gmv)
                    gml_done = s.op("v", lambda e, h0=h0, eoff=eoff: e.tensor_tensor(
                        out=AR[:, o_gml + h0 * 128:o_gml + (h0 + AGH) * 128]
                        .rearrange("p (h f) -> p h f", h=AGH),
                        in0=AR[:, eoff:eoff + AGH * 128]
                        .rearrange("p (h f) -> p h f", h=AGH),
                        in1=sb["b_Gm"][:, None, :].broadcast_to([128, AGH, 128]),
                        op=ALU.mult), inc="bv")
                    expl_half_free[par] = gml_done
                    gml_toks.append(gml_done)
                if "g" in HB:
                    s.barrier()
                for yg in range(NYG):
                    h0 = yg * YGH
                    s.need("t", y_done)
                    agn = min(NAG - 1, (h0 + YGH - 1) // AGH)
                    s.need("t", gml_toks[agn])
                    itok = s.op("t", lambda e, yg=yg, c0=c0: e.matmul(
                        PB[:, 0:512], XBCB[:, TQ + c0:TQ + c0 + 128],
                        SBb[:, yg * YGH * PH:(yg + 1) * YGH * PH],
                        start=True, stop=True, skip_group_check=True), inc="bt")
                    for hh in range(YGH):
                        h = h0 + hh
                        s.op("t", lambda e, h=h, hh=hh: e.matmul(
                            PB[:, 512 + hh * PH:512 + (hh + 1) * PH],
                            AR[:, o_gml + h * 128:o_gml + (h + 1) * 128],
                            AR[:, o_xsT + h * PH:o_xsT + (h + 1) * PH],
                            start=(hh == 0), stop=False, skip_group_check=True))
                    nmt = YGH * PH // 128
                    for mm in range(nmt):
                        mt = yg * nmt + mm
                        tok = s.op("t", lambda e, mt=mt, mm=mm, c0=c0: e.matmul(
                            PB[:, 512 + mm * 128:512 + (mm + 1) * 128],
                            xbca(mt * TQ + c0, mt * TQ + c0 + 128),
                            sb["c_Dd"][:, mt * 128:(mt + 1) * 128],
                            start=False, stop=(mm == nmt - 1),
                            skip_group_check=True),
                            inc=("bt" if mm == nmt - 1 else None))
                    s.need("v", tok)
                    s.op("v", lambda e, yg=yg, h0=h0: e.tensor_tensor(
                        out=XY[:, yg * YGH * PH:(yg + 1) * YGH * PH]
                        .rearrange("p (h f) -> p h f", h=YGH),
                        in0=PB[:, 0:YGH * PH].rearrange("p (h f) -> p h f", h=YGH),
                        in1=sb["b_Etok"][:, h0:h0 + YGH, None]
                        .broadcast_to([128, YGH, PH]), op=ALU.mult))
                    s.selfedge("v")
                    yv = s.op("v", lambda e, yg=yg: e.scalar_tensor_tensor(
                        out=XY[:, yg * YGH * PH:(yg + 1) * YGH * PH],
                        in0=PB[:, 512:512 + YGH * PH], scalar=1.0,
                        in1=XY[:, yg * YGH * PH:(yg + 1) * YGH * PH],
                        op0=ALU.mult, op1=ALU.add), inc="bv")
                    y_done = yv
                    s.need("t", wxtok)
                    s.need("t", yv)
                    dstok = s.op("t", lambda e, yg=yg: e.matmul(
                        PC[:, 0:512], sb["b_BT"][:, :],
                        AR[:, o_wx + yg * YGH * PH:o_wx + (yg + 1) * YGH * PH],
                        start=True, stop=True, skip_group_check=True), inc="bt")
                    s.need("v", dstok)
                    s.op("v", lambda e, yg=yg, h0=h0: e.tensor_tensor(
                        out=sb["b_Sd"][:, :].rearrange("p (h f) -> p h f", h=YGH),
                        in0=ST[:, yg * YGH * PH:(yg + 1) * YGH * PH]
                        .rearrange("p (h f) -> p h f", h=YGH),
                        in1=sb["b_Lam"][:, h0:h0 + YGH, None]
                        .broadcast_to([128, YGH, PH]), op=ALU.mult))
                    s.selfedge("v")
                    s.op("v", lambda e, yg=yg: e.scalar_tensor_tensor(
                        out=ST[:, yg * YGH * PH:(yg + 1) * YGH * PH],
                        in0=PC[:, 0:512], scalar=1.0, in1=sb["b_Sd"][:, :],
                        op0=ALU.mult, op1=ALU.add))
                    s.selfedge("v")
                    y_done = s.op("v", lambda e, yg=yg: e.tensor_copy(
                        SBb[:, yg * YGH * PH:(yg + 1) * YGH * PH],
                        ST[:, yg * YGH * PH:(yg + 1) * YGH * PH]), inc="bv")
                if q_i == DBG_Q and ch == 0:
                    tap("dbg_Gm", sb["b_Gm"][:, :])
                    tap("dbg_gml", AR[:, o_gml:o_gml + H * 128])
                    tap("dbg_y", XY[:, :])
                    tap("dbg_S", ST[:, :])
                    tap("dbg_gate", GSL[:, 0:G2])
                s.need("v", gtok)
                ygv = s.op("v", lambda e: e.tensor_mul(
                    AR[:, o_yg:o_yg + G2], XY[:, 0:G2], GSL[:, 0:G2]), inc="bv")
                s.need("s", ygv)
                sqv = s.op("s", lambda e: e.activation(
                    out=AR[:, o_wx:o_wx + G2], in_=AR[:, o_yg:o_yg + G2],
                    func=AF.Square, accum_out=st[:, 14:15]), inc="bs")
                s.selfedge("s")
                s.op("s", lambda e: e.activation(out=st[:, 15:16], in_=st[:, 14:15],
                                                 func=AF.Ln, scale=1.0 / G2,
                                                 bias=float(EPS)))
                rstok = s.op("s", lambda e: e.activation(
                    out=st[:, 15:16], in_=st[:, 15:16], func=AF.Exp, scale=-0.5),
                    inc="bs")
                s.need("v", rstok)
                s.selfedge("v")
                ygn = s.op("v", lambda e: e.tensor_scalar_mul(
                    AR[:, o_yg:o_yg + G2], AR[:, o_yg:o_yg + G2],
                    st[:, 15:16]), inc="bv")
                s.selfedge("v")
                if q_i == DBG_Q and ch == 0:
                    tap("dbg_ygn", AR[:, o_yg:o_yg + G2])
                s.need("t", ygn)
                tp_half_free = {0: None, 1: None}
                for m in range(OKT):
                    half = m % 2
                    gslot = (m % 8) * 128
                    s.need("t", tp_half_free[half])
                    ttok = s.op("t", lambda e, m=m, half=half: e.transpose(
                        pbh(half)[:, 0:128], AR[:, o_yg + m * 128:o_yg + (m + 1) * 128],
                        identity=IDb), inc="bt")
                    s.need("v", ttok)
                    s.need("v", gsl_slot_free.get(gslot))
                    ev = s.op("v", lambda e, half=half, gslot=gslot: e.tensor_copy(
                        GSL[:, gslot:gslot + 128], pbh(half)[:, 0:128]), inc="bv")
                    tp_half_free[half] = ev
                    s.need("q", ev)
                    gsl_slot_free[gslot] = s.dma(
                        "q", lambda e, m=m, c0=c0, gslot=gslot: e.dma_start(
                            out=sc_ygnT[m, :, c0:c0 + 128],
                            in_=GSL[:, gslot:gslot + 128]),
                        sem="dg%d" % (gslot // 128))
                gsl_dma_done = ("d", s.cnt["d"])
                prev_epi_v = ("bv", s.cnt["bv"])
                prev_pbh = dict(tp_half_free)
                if "e" in HB:
                    s.barrier()
            s.barrier()

            if not do_f:
                continue
            wb_free = {}
            wy_free = {}
            HW = D // 2
            GSLf = GSL[:, :].bitcast(F32)
            WAf = WA[:, :].bitcast(F32)
            pr = _pr4(PA, PB, PC, PD, NTT)
            stg = [(GSLf, 0), (GSLf, HW), (WAf, 0), (WAf, HW)]
            ocs = [(i_ * 512, min(512, HW - i_ * 512))
                   for i_ in range((HW + 511) // 512)]
            out_done = [None] * NTT
            fv_prev_psum = []
            prev_fv = [None] * NTT
            for dh in range(2):
                for tt4 in range(NTT):
                    s.need("q", prev_fv[tt4])
                    xr = s.dma("q", lambda e, tt4=tt4, dh=dh: e.dma_start(
                        out=XY[:, tt4 * HW:(tt4 + 1) * HW],
                        in_=sc_xpxn[tt4 * 128:(tt4 + 1) * 128,
                                    dh * HW:(dh + 1) * HW]),
                        sem="dx%d" % tt4)
                    prev_fv[tt4] = xr
                xr_toks = list(prev_fv)
                for kt in range(OKT):
                    sA = (kt % 2) * 2048
                    sy = (kt % 2) * TQ
                    s.need("q", wb_free.get(sA))
                    wt = s.dma("q", lambda e, kt=kt, sA=sA, dh=dh: e.dma_start(
                        out=WB[:, sA:sA + HW],
                        in_=sc_o[kt * 128:(kt + 1) * 128, dh * HW:(dh + 1) * HW]),
                        sem="db0" if sA == 0 else "db1")
                    s.need("q", wy_free.get(sy))
                    yt = s.dma("q", lambda e, kt=kt, sy=sy: e.dma_start(
                        out=WY[:, sy:sy + TQ], in_=sc_ygnT[kt, :, :]),
                        sem="dy0" if sy == 0 else "dy1")
                    s.need("t", wt)
                    s.need("t", yt)
                    if kt == 0:
                        for f_ in fv_prev_psum:
                            s.need("t", f_)
                    for tt4 in range(NTT):
                        for oc2, (oo, ol) in enumerate(ocs):
                            pt, po = pr(tt4, oo)
                            tok = s.op("t", lambda e, kt=kt, sA=sA, sy=sy, tt4=tt4,
                                       oo=oo, ol=ol, pt=pt, po=po: e.matmul(
                                           pt[:, po:po + ol],
                                           WY[:, sy + tt4 * 128:sy + (tt4 + 1) * 128],
                                           WB[:, sA + oo:sA + oo + ol],
                                           start=(kt == 0), stop=(kt == OKT - 1),
                                           skip_group_check=True),
                                       inc=("bt" if (tt4 == NTT - 1 and
                                                     oc2 == len(ocs) - 1)
                                            else None))
                    wb_free[sA] = tok
                    wy_free[sy] = tok
                    if "f" in HB:
                        s.barrier()
                s.need("v", tok)
                for x_ in xr_toks:
                    s.need("v", x_)
                for o_ in out_done:
                    s.need("v", o_)
                fv_prev_psum = []
                for tt4 in range(NTT):
                    sg_, so_ = stg[tt4] if NTT == 4 else (
                        [(GSLf, 0), (WAf, 0)][tt4])
                    for oo, ol in ocs:
                        pt, po = pr(tt4, oo)
                        fv = s.op("v", lambda e, tt4=tt4, pt=pt, po=po, sg_=sg_,
                                  so_=so_, oo=oo, ol=ol: e.scalar_tensor_tensor(
                                      out=sg_[:, so_ + oo:so_ + oo + ol],
                                      in0=pt[:, po:po + ol], scalar=1.0,
                                      in1=XY[:, tt4 * HW + oo:tt4 * HW + oo + ol],
                                      op0=ALU.mult, op1=ALU.add), inc="bv")
                        s.selfedge("v")
                    prev_fv[tt4] = fv
                    fv_prev_psum.append(fv)
                    s.need("q", fv)
                    out_done[tt4] = s.dma(
                        "q", lambda e, tt4=tt4, dh=dh, T0=T0, sg_=sg_, so_=so_:
                        e.dma_start(
                            out=out_t[T0 + tt4 * 128:T0 + (tt4 + 1) * 128,
                                      dh * HW:(dh + 1) * HW],
                            in_=sg_[:, so_:so_ + HW]), sem="do%d" % tt4)
            s.barrier()

        if dump:
            s.barrier()
            dmp = {
                "dmp_gate": sc_gate, "dmp_xpxn": sc_xpxn,
                "dmp_cumhl": sc_cumhl, "dmp_cend": sc_cend,
            }
            dmp_sb = {
                "dmp_XBCB": XBCB, "dmp_XPb": None, "dmp_AR": AR,
                "dmp_cum": sb["b_cum"], "dmp_dt": sb["b_dt"],
                "dmp_dtraw": sb["b_dtraw"], "dmp_Etok": sb["b_Etok"],
                "dmp_Lam": sb["b_Lam"], "dmp_tok": sb["b_tok"],
                "dmp_ST": ST, "dmp_Gm": sb["b_Gm"], "dmp_XY": XY,
                "dmp_cumhl_sb": sb["b_cumhl"], "dmp_cendhl": sb["b_cendhl"],
                "dmp_cscr": sb["b_cscr"],
            }
            for nm, t_ in dmp.items():
                o_ = nc.dram_tensor(nm, list(t_.tensor.shape),
                                    t_.tensor.dtype, kind="ExternalOutput").ap()
                s.dma("q", lambda e, o_=o_, t_=t_: e.dma_start(out=o_, in_=t_))
            for nm, t_ in dmp_sb.items():
                if t_ is None:
                    t_ = XP[:, :].bitcast(BF16)
                else:
                    t_ = t_[:, :]
                o_ = nc.dram_tensor(nm, [t_.shape[0], t_.shape[1]],
                                    t_.dtype, kind="ExternalOutput").ap()
                s.dma("q", lambda e, o_=o_, t_=t_: e.dma_start(out=o_, in_=t_))
        for nm_ in s.q_dma_sems():
            s.wait("q", nm_)
        global _LAST_SCH
        _LAST_SCH = s
        s.simulate()
        s.emit(nc, sems)
    return nc


def kernel(L0_attn_hidden_states, L0_attn_norm_weight, L0_attn_in_proj_w,
           L0_attn_conv_w, L0_attn_conv_b, L0_attn_dt_bias, L0_attn_A_log,
           L0_attn_D, L0_attn_gate_norm_weight, L0_attn_out_proj_w,
           L0_ffn_gate_w, L0_ffn_up_w, L0_ffn_down_w, L0_ffn_gate_s,
           L0_ffn_up_s, L0_ffn_down_s):
    c = derive(full_cfg())
    w = dict(in_proj_w=np.asarray(L0_attn_in_proj_w, np.float32),
             conv_w=np.asarray(L0_attn_conv_w, np.float32),
             conv_b=np.asarray(L0_attn_conv_b, np.float32),
             dt_bias=np.asarray(L0_attn_dt_bias, np.float32),
             A_log=np.asarray(L0_attn_A_log, np.float32),
             D=np.asarray(L0_attn_D, np.float32),
             gate_norm_weight=np.asarray(L0_attn_gate_norm_weight, np.float32),
             out_proj_w=np.asarray(L0_attn_out_proj_w, np.float32),
             norm_weight=np.asarray(L0_attn_norm_weight, np.float32),
             gate_w=np.asarray(L0_ffn_gate_w, np.float32),
             up_w=np.asarray(L0_ffn_up_w, np.float32),
             down_w=np.asarray(L0_ffn_down_w, np.float32),
             gate_s=np.asarray(L0_ffn_gate_s, np.float32),
             up_s=np.asarray(L0_ffn_up_s, np.float32),
             down_s=np.asarray(L0_ffn_down_s, np.float32))
    consts = host_arrays(c, w)
    x = np.asarray(L0_attn_hidden_states, np.float32)
    nc = build(c)
    in_maps = []
    for b in range(8):
        m = dict(consts)
        m["x"] = np.ascontiguousarray(x[b])
        in_maps.append(m)
    res = run_bass_kernel_spmd(nc, in_maps, core_ids=list(range(8)))
    out = np.stack([np.asarray(res.results[b]["out"]) for b in range(8)], axis=0)
    return out.astype(np.float32)
